# version 01
# baseline (speedup 1.0000x reference)
from contextlib import ExitStack

import numpy as np
import ml_dtypes
import concourse.bass as bass
import concourse.mybir as mybir
from concourse.bass_utils import run_bass_kernel_spmd

F32 = mybir.dt.float32
BF16 = mybir.dt.bfloat16
I32 = mybir.dt.int32
AF = mybir.ActivationFunctionType
ALU = mybir.AluOpType
AX = mybir.AxisListType

B, T, C = 32, 2048, 1024
H, HD = 16, 64
NCORES = 8
SPC = B // NCORES
TT = T // 128
CT = C // 128
QCH = 256
NQC = T // QCH
FFN = 4096
NE = 8
NMAT = 6
LORA_R = 32

C_EPS = 0
C_INV1024 = 1
C_NEGHALF = 2
C_ATTSCL = 3
C_INV64 = 4
C_ONE = 5
C_NEGINV1024 = 6
C_MEAN64 = 7
C_NCONST = 8

DMA_ENGS = ("syd", "scd", "gpd")
ENGQ = {"pe": "tensor", "dve": "vector", "act": "scalar", "pool": "gpsimd",
        "syd": "sync", "scd": "scalar", "gpd": "gpsimd"}
QUEUES = ("sync", "tensor", "vector", "scalar", "gpsimd")


class Trk:
    def __init__(self):
        self.steps = []
        self.cnt = {e: 0 for e in ENGQ}
        self.clk = {e: {f: 0 for f in ENGQ} for e in ENGQ}
        self.hist = {e: [] for e in ENGQ}
        self.res = {}

    def step(self, eng, reads=(), writes=(), fn=None, tag=""):
        mycl = self.clk[eng]
        waits = {}

        def need(m):
            s2, k = m
            if s2 == eng or mycl[s2] >= k:
                return
            if waits.get(s2, 0) < k:
                waits[s2] = k
            h = self.hist[s2][k - 1]
            for s3, v3 in h.items():
                if v3 > mycl[s3]:
                    mycl[s3] = v3

        for r in reads:
            ent = self.res.get(r)
            if ent and ent[0]:
                need(ent[0])
        for r in writes:
            ent = self.res.get(r)
            if ent:
                if ent[0]:
                    need(ent[0])
                for m in ent[1]:
                    need(m)
        self.cnt[eng] += 1
        k = self.cnt[eng]
        mycl[eng] = k
        self.hist[eng].append(dict(mycl))
        wset = set(writes)
        for r in wset:
            self.res[r] = [(eng, k), []]
        for r in reads:
            if r in wset:
                continue
            ent = self.res.setdefault(r, [None, []])
            ent[1].append((eng, k))
        self.steps.append((eng, fn, sorted(waits.items()), tag))

    def emit(self, nc, ctx):
        sems = {e: ctx.enter_context(nc.semaphore(f"sem_{e}"))
                for e in ENGQ}
        by_q = {q: [] for q in QUEUES}
        for st in self.steps:
            by_q[ENGQ[st[0]]].append(st)
        with nc.Block() as blk:
            for q in QUEUES:
                steps = by_q[q]

                def mkbody(steps):
                    def body(e):
                        for (eng, fn, waits, tag) in steps:
                            for (s2, k) in waits:
                                e.wait_ge(sems[s2],
                                          k * (16 if s2 in DMA_ENGS else 1))
                            ins = fn(e)
                            ins.then_inc(sems[eng],
                                         16 if eng in DMA_ENGS else 1)
                    return body
                getattr(blk, q)(mkbody(steps))
        final = {e: self.cnt[e] for e in ENGQ}
        with nc.Block() as blk:
            def fin(e):
                for s2, k in final.items():
                    if k:
                        e.wait_ge(sems[s2], k * (16 if s2 in DMA_ENGS else 1))
            blk.sync(fin)


class P:
    def __init__(self, nc, ctx, trk, dram, dbg=()):
        self.dbg = set(dbg)
        self.nc = nc
        self.ctx = ctx
        self.t = trk
        self.dram = dram
        self.pb = [ctx.enter_context(
            nc.psum_tensor(f"pb{i}", [128, 512], F32)) for i in range(8)]
        self.sbufs = {}

    def sb(self, name, shape, dtype):
        h = self.ctx.enter_context(
            self.nc.sbuf_tensor("sb_" + name, shape, dtype))
        self.sbufs[name] = h[:]
        return self.sbufs[name]

    def view(self, name, arena_ap, col_off_bytes, parts, cols, dtype):
        base = arena_ap.bitcast(dtype)
        esz = 2 if dtype == BF16 else 4
        c0 = col_off_bytes // esz
        self.sbufs[name] = base[0:parts, c0:c0 + cols]
        return self.sbufs[name]

    def tap(self, name, src_ap, reads):
        if name not in self.dbg:
            return
        d = self.dram[name]
        self.t.step("scd", tuple(reads), (),
                    lambda e: e.dma_start(d, src_ap))

    def fence(self, names):
        self.t.step("pool", (), tuple(names), lambda e: e.engine_nop(),
                    "fence")


def nxc(ct, tcol):
    return ct * T + tcol


WT_NAMES = ("wqk0", "wqk1", "wv0", "wv1", "cos2", "sin2", "wo_sb",
            "fc1w0", "fc1w1", "fc2w0", "fc2w1", "gate_sb", "gate_bf",
            "qt_sb", "sc_sb", "wdq0", "wdq1", "sT")
ATT_NAMES = ("attT", "moe_hT")
SCR_NAMES = ("w_xin0", "w_xin1", "w_nx0", "w_nx1", "uacc0", "uacc1",
             "qkall", "qknorm", "l1sub",
             "w_z0", "w_z1", "w_out0", "w_out1", "w_tmp", "w_tmp2",
             "w_sqscr",
             "st_sb") + tuple(f"stq{i}" for i in range(16)) + (
             "qb", "kb", "qkc", "rot", "ptb0", "ptb1",
             "att2_0", "att2_1")


def emit_init(p):
    t, d = p.t, p.dram
    t.step("syd", (), ("cpool",), lambda e: e.dma_start(
        p.sbufs["cpool"], d["consts"].partition_broadcast(128).squeeze(1)))
    t.step("syd", (), ("identf",),
           lambda e: e.dma_start(p.sbufs["identf"], d["identf"]))
    t.step("syd", (), ("mask_sb",), lambda e: e.dma_start(
        p.sbufs["mask_sb"].rearrange("p (i q) -> p i q", q=QCH),
        d["masks"].transpose([1, 0, 2])))
    t.step("pool", (), ("onesr",),
           lambda e: e.memset(p.sbufs["onesr"], 1.0))
    t.step("dve", ("identf",), ("identb",),
           lambda e: e.tensor_copy(p.sbufs["identb"], p.sbufs["identf"]))


def emit_phase0_cast(p):
    t, d = p.t, p.dram
    for ei in range(NE):
        for fc in range(FFN // 512):
            t.step("gpd", (), (f"fc1bf_{ei}",),
                   (lambda ei, fc: lambda e: e.dma_start(
                       d["f0_fc1_bf"][ei, fc], d["f0_fc1"][ei, fc]))(ei, fc))
        for f8 in range(4):
            t.step("gpd", (), (f"fc2bf_{ei}",),
                   (lambda ei, f8: lambda e: e.dma_start(
                       d["f0_fc2_bf"][ei, f8 * 8:(f8 + 1) * 8],
                       d["f0_fc2"][ei, f8 * 8:(f8 + 1) * 8]))(ei, f8))


def ln_steps(p, src_ap, src_reads, sc, out_fn):
    t = p.t
    stats = p.sbufs["w_st"]
    cpool = p.sbufs["cpool"]
    W = src_ap.shape[-1]
    sq = p.sbufs["w_sq"]

    def f_mu(e):
        return e.reduce_sum(stats[:, sc:sc + 1], src_ap, axis=AX.X)
    t.step("dve", tuple(src_reads), (f"w_st{sc}",), f_mu)

    def f_neg(e):
        return e.tensor_scalar(stats[:, sc + 1:sc + 2], stats[:, sc:sc + 1],
                               -1.0 / W, None, ALU.mult)
    t.step("pool", (f"w_st{sc}",), (f"w_st{sc}n",), f_neg)

    def f_sq(e):
        e.activation(sq[:, 0:W], src_ap, AF.Square,
                     bias=stats[:, sc + 1:sc + 2], scale=1.0,
                     accum_out=stats[:, sc + 2:sc + 3])
        e.activation(stats[:, sc + 3:sc + 4], stats[:, sc + 2:sc + 3],
                     AF.Ln, bias=cpool[:, C_EPS:C_EPS + 1],
                     scale=cpool[:, C_INV1024:C_INV1024 + 1])
        return e.activation(stats[:, sc + 3:sc + 4], stats[:, sc + 3:sc + 4],
                            AF.Exp, bias=0.0,
                            scale=cpool[:, C_NEGHALF:C_NEGHALF + 1])
    t.step("act", tuple(src_reads) + (f"w_st{sc}", f"w_st{sc}n", "cpool"),
           (f"w_st{sc}b", "w_sqscr"), f_sq)
    out_fn(stats[:, sc + 1:sc + 2], stats[:, sc + 3:sc + 4],
           (f"w_st{sc}", f"w_st{sc}b"))


def emit_u_phase(p, src, ud, src_res):
    t = p.t
    xin = p.sbufs["w_xin"]
    out = p.sbufs["w_out"]
    for tt in range(TT):
        sl = tt % 2
        t.step("scd", (f"{src_res}{tt}",), (f"w_xin{sl}",),
               (lambda tt, sl: lambda e: e.dma_start(
                   xin[:, sl * 1024:(sl + 1) * 1024],
                   src[tt * 128:(tt + 1) * 128, :]))(tt, sl))
        src_ap = xin[:, sl * 1024:(sl + 1) * 1024]

        def mko(tt=tt, sl=sl, src_ap=src_ap):
            def out_fn(negmu, rstd, streads):
                def f(e):
                    return e.tensor_scalar(
                        out[:, sl * 1024:(sl + 1) * 1024], src_ap,
                        negmu, rstd, ALU.add, ALU.mult)
                t.step("dve", (f"w_xin{sl}",) + streads, (f"w_out{sl}",), f)
                t.step("scd", (f"w_out{sl}",), (f"ud{tt}",),
                       (lambda tt, sl: lambda e: e.dma_start(
                           ud[tt * 128:(tt + 1) * 128, :],
                           out[:, sl * 1024:(sl + 1) * 1024]))(tt, sl))
            return out_fn
        ln_steps(p, src_ap, (f"w_xin{sl}",), (tt % 2) * 4, mko())


def emit_transposes_tile(p, tt, src_ap, src_res, dstT_name):
    t = p.t
    identb = p.sbufs["identb"]
    dstT = p.sbufs[dstT_name]
    pbk = 6 + (tt % 2)
    pt = p.pb[pbk].ap().bitcast(BF16)

    def f_tr(e):
        ins = None
        for ct in range(CT):
            ins = e.transpose(pt[:, ct * 128:(ct + 1) * 128],
                              src_ap[:, ct * 128:(ct + 1) * 128], identb[:])
        return ins
    t.step("pe", (src_res, "identb"), (f"pb{pbk}",), f_tr)

    def f_ev(e):
        ins = None
        for half in range(2):
            outap = dstT.rearrange("p (ct tx) -> p ct tx", tx=T)[
                :, half * 4:(half + 1) * 4, tt * 128:(tt + 1) * 128]
            ins = e.tensor_copy(
                outap, pt[:, half * 512:(half + 1) * 512].rearrange(
                    "p (ct tx) -> p ct tx", tx=128))
        return ins
    t.step("dve", (f"pb{pbk}",), (dstT_name,), f_ev)


def emit_nx_phase(p, nxT_name, src_dram, src_res_base, gamma=None, beta=None):
    t = p.t
    nxbf = p.sbufs["w_nx"]
    xin = p.sbufs["w_xin"]
    for tt in range(TT):
        sl = tt % 2
        t.step("scd", (f"{src_res_base}{tt}",), (f"w_xin{sl}",),
               (lambda tt, sl: lambda e: e.dma_start(
                   xin[:, sl * 1024:(sl + 1) * 1024],
                   src_dram[tt * 128:(tt + 1) * 128, :]))(tt, sl))
        src_ap = xin[:, sl * 1024:(sl + 1) * 1024]

        def mko(tt=tt, sl=sl, src_ap=src_ap):
            def out_fn(negmu, rstd, streads):
                if gamma is None:
                    def f(e):
                        return e.tensor_scalar(
                            nxbf[:, sl * 1024:(sl + 1) * 1024], src_ap,
                            negmu, rstd, ALU.add, ALU.mult)
                    t.step("dve", (f"w_xin{sl}",) + streads,
                           (f"w_nx{sl}",), f)
                else:
                    g_tab = p.sbufs[gamma]
                    b_tab = p.sbufs[beta]
                    tmp = p.sbufs["w_tmp"]

                    def f(e):
                        e.tensor_scalar(tmp[:, 0:1024], src_ap, negmu, rstd,
                                        ALU.add, ALU.mult)
                        e.tensor_mul(tmp[:, 0:1024], tmp[:, 0:1024], g_tab)
                        return e.tensor_add(
                            nxbf[:, sl * 1024:(sl + 1) * 1024],
                            tmp[:, 0:1024], b_tab)
                    t.step("dve", (f"w_xin{sl}",) + streads + (gamma, beta),
                           (f"w_nx{sl}", "w_tmp"), f)
            return out_fn
        ln_steps(p, src_ap, (f"w_xin{sl}",), (tt % 2) * 4, mko())
        emit_transposes_tile(p, tt, p.sbufs["w_nx"][:, (tt % 2) * 1024:
                                                    (tt % 2 + 1) * 1024],
                             f"w_nx{tt % 2}", nxT_name)


def emit_attn_heads(p, wp, mode, nxT_name):
    t, d = p.t, p.dram
    nxT = p.sbufs[nxT_name]
    attT = p.sbufs["attT"]
    identb = p.sbufs["identb"]
    identf = p.sbufs["identf"]
    onesr = p.sbufs["onesr"]
    cpool = p.sbufs["cpool"]
    mask_sb = p.sbufs["mask_sb"]
    qb, kb = p.sbufs["qb"], p.sbufs["kb"]
    vgrp = p.sbufs["vgrp"]
    st_sb = p.sbufs["st_sb"]
    bmx = p.sbufs["bmx"]
    chain = p.sbufs["chain"]
    chain_bf = p.sbufs["chain_bf"]
    ptb = p.sbufs["ptb"]
    att2 = p.sbufs["att2"]
    wqk_sb = p.sbufs["wqk_sb"]
    wv_sb = p.sbufs["wv_sb"]
    bias = (mode == "plain")

    if mode == "rope":
        t.step("syd", (), ("cos2",), lambda e: e.dma_start(
            p.sbufs["cos2"], d["ropec"]))
        t.step("syd", (), ("sin2",), lambda e: e.dma_start(
            p.sbufs["sin2"], d["ropes"]))

    def f_ones(e):
        ins = None
        for tt in range(TT):
            for j in range(4):
                ins = e.memset(vgrp[:, tt * 260 + j * 65 + 64:
                                    tt * 260 + j * 65 + 65], 1.0)
        return ins
    t.step("pool", (), ("vgrp", "vgrp_ones"), f_ones)

    for h in range(H):
        wslot = h % 2
        if mode in ("rope", "plain"):
            t.step("gpd", (), (f"wqk{wslot}",),
                   (lambda h, wslot: lambda e: e.dma_start(
                       wqk_sb[:, wslot * 1024:(wslot + 1) * 1024],
                       d[wp + "_wqk"][h]))(h, wslot))
            if h % 4 == 0:
                g = h // 4
                gs = g % 2
                t.step("gpd", (), (f"wv{gs}",),
                       (lambda g, gs: lambda e: e.dma_start(
                           wv_sb[:, gs * 2048:(gs + 1) * 2048],
                           d[wp + "_wv"][g]))(g, gs))
            for chunk in range(4):
                pbk = chunk % 2

                def f_qk(e, h=h, chunk=chunk, pbk=pbk, wslot=wslot):
                    ins = None
                    for ct in range(CT):
                        ins = e.matmul(
                            p.pb[pbk].ap()[:, 0:512],
                            wqk_sb[:, wslot * 1024 + ct * 128:
                                   wslot * 1024 + (ct + 1) * 128],
                            nxT[:, nxc(ct, chunk * 512):
                                nxc(ct, (chunk + 1) * 512)],
                            start=(ct == 0), stop=(ct == CT - 1 and not bias))
                    if bias:
                        ins = e.matmul(
                            p.pb[pbk].ap()[:, 0:512],
                            p.sbufs["bqk_sb"][0:1, h * 128:(h + 1) * 128],
                            onesr[0:1, 0:512], start=False, stop=True)
                    return ins
                rd = (f"wqk{wslot}", nxT_name, "onesr")
                if bias:
                    rd = rd + ("bqk_sb",)
                t.step("pe", rd, (f"pb{pbk}",), f_qk)
                if mode == "rope":
                    qkc = p.sbufs["qkc"]
                    rot = p.sbufs["rot"]

                    def f_rope(e, chunk=chunk, pbk=pbk):
                        cs = slice(chunk * 512, (chunk + 1) * 512)
                        e.tensor_copy(qkc[:, 0:512], p.pb[pbk].ap()[:, 0:512])
                        e.tensor_scalar(rot[0:64, 0:512], qkc[64:128, 0:512],
                                        -1.0, None, ALU.mult)
                        e.tensor_copy(rot[64:128, 0:512], qkc[0:64, 0:512])
                        e.tensor_mul(qkc[:, 0:512], qkc[:, 0:512],
                                     p.sbufs["cos2"][:, cs])
                        e.tensor_mul(rot[:, 0:512], rot[:, 0:512],
                                     p.sbufs["sin2"][:, cs])
                        e.tensor_add(qkc[:, 0:512], qkc[:, 0:512],
                                     rot[:, 0:512])
                        e.tensor_copy(qb[0:32, cs], qkc[0:32, 0:512])
                        e.tensor_copy(kb[0:32, cs], qkc[32:64, 0:512])
                        e.tensor_copy(qb[32:64, cs], qkc[64:96, 0:512])
                        return e.tensor_copy(kb[32:64, cs],
                                             qkc[96:128, 0:512])
                    t.step("dve", (f"pb{pbk}", "cos2", "sin2"),
                           ("qb", "kb", "qkc", "rot"), f_rope)
                else:
                    def f_ev(e, chunk=chunk, pbk=pbk):
                        cs = slice(chunk * 512, (chunk + 1) * 512)
                        e.tensor_copy(qb[:, cs], p.pb[pbk].ap()[0:64, 0:512])
                        return e.tensor_copy(kb[:, cs],
                                             p.pb[pbk].ap()[64:128, 0:512])
                    t.step("dve", (f"pb{pbk}",), ("qb", "kb"), f_ev)
            if h == 0 and mode == "rope":
                p.tap("dbg_qb", p.sbufs["qb"], ("qb",))
                p.tap("dbg_kb", p.sbufs["kb"], ("kb",))
            if h % 4 == 0:
                g = h // 4
                gs = g % 2
                for tt in range(TT):
                    def f_v(e, gs=gs, tt=tt, g=g):
                        ins = None
                        for ct in range(CT):
                            ins = e.matmul(
                                p.pb[2].ap()[:, 0:256],
                                nxT[:, nxc(ct, tt * 128):
                                    nxc(ct, (tt + 1) * 128)],
                                wv_sb[:, gs * 2048 + ct * 256:
                                      gs * 2048 + (ct + 1) * 256],
                                start=(ct == 0),
                                stop=(ct == CT - 1 and not bias))
                        if bias:
                            ins = e.matmul(
                                p.pb[2].ap()[:, 0:256], onesr[0:1, 0:128],
                                p.sbufs["bv_sb"][0:1, g * 256:(g + 1) * 256],
                                start=False, stop=True)
                        return ins
                    rd = (f"wv{gs}", nxT_name, "onesr")
                    if bias:
                        rd = rd + ("bv_sb",)
                    t.step("pe", rd, ("pb2",), f_v)

                    def f_vev(e, tt=tt):
                        return e.tensor_copy(
                            vgrp[:, tt * 260:tt * 260 + 260].rearrange(
                                "p (j dd) -> p j dd", dd=65)[:, :, 0:64],
                            p.pb[2].ap()[:, 0:256].rearrange(
                                "p (j dd) -> p j dd", dd=64))
                    t.step("dve", ("pb2", "vgrp_ones"), ("vgrp",), f_vev)
        else:
            qkall = p.sbufs["qkall"]
            qkn = p.sbufs["qknorm"]
            lntab = p.sbufs["l1lntab"]
            t.step("gpd", (), (f"wqk{wslot}",),
                   (lambda h, wslot: lambda e: e.dma_start(
                       wqk_sb[:, wslot * 1536:(wslot + 1) * 1536],
                       d["a1_wqkv"][h]))(h, wslot))
            for tt in range(TT):
                pbk = 3 + (tt % 2)

                def f_qkv(e, tt=tt, pbk=pbk, wslot=wslot):
                    ins = None
                    for ct in range(CT):
                        ins = e.matmul(
                            p.pb[pbk].ap()[:, 0:192],
                            nxT[:, nxc(ct, tt * 128):nxc(ct, (tt + 1) * 128)],
                            wqk_sb[:, wslot * 1536 + ct * 192:
                                   wslot * 1536 + (ct + 1) * 192],
                            start=(ct == 0), stop=(ct == CT - 1))
                    return ins
                t.step("pe", (f"wqk{wslot}", nxT_name), (f"pb{pbk}",), f_qkv)

                def f_ev(e, tt=tt, pbk=pbk, h=h):
                    e.tensor_copy(qkall[:, tt * 128:(tt + 1) * 128],
                                  p.pb[pbk].ap()[:, 0:128])
                    return e.tensor_copy(
                        vgrp[:, tt * 260 + (h % 4) * 65:
                             tt * 260 + (h % 4) * 65 + 64],
                        p.pb[pbk].ap()[:, 128:192])
                t.step("dve", (f"pb{pbk}", "vgrp_ones"), ("qkall", "vgrp"),
                       f_ev)

            def f_mu(e):
                return e.reduce_sum(
                    p.sbufs["l1mu"][:, 0:32],
                    qkall.rearrange("p (g dd) -> p g dd", dd=64), axis=AX.X)
            t.step("dve", ("qkall",), ("l1mu",), f_mu)

            def f_mul(e):
                return e.activation(p.sbufs["l1mu"][:, 32:64],
                                    p.sbufs["l1mu"][:, 0:32], AF.Copy,
                                    bias=0.0,
                                    scale=cpool[:, C_MEAN64:C_MEAN64 + 1])
            t.step("act", ("l1mu", "cpool"), ("l1mu2",), f_mul)

            def f_sub(e):
                mub = p.sbufs["l1mu"][:, 32:64].unsqueeze(2).broadcast_to(
                    (128, 32, 64))
                e.tensor_tensor(
                    p.sbufs["l1sub"].rearrange("p (g dd) -> p g dd", dd=64),
                    qkall.rearrange("p (g dd) -> p g dd", dd=64),
                    mub, ALU.subtract)
                e.tensor_mul(p.sbufs["w_sq"][:, 0:1024],
                             p.sbufs["l1sub"][:, 0:1024],
                             p.sbufs["l1sub"][:, 0:1024])
                e.tensor_mul(p.sbufs["w_tmp"][:, 0:1024],
                             p.sbufs["l1sub"][:, 1024:2048],
                             p.sbufs["l1sub"][:, 1024:2048])
                e.reduce_sum(
                    p.sbufs["l1var"][:, 0:16],
                    p.sbufs["w_sq"][:, 0:1024].rearrange(
                        "p (g dd) -> p g dd", dd=64), axis=AX.X)
                return e.reduce_sum(
                    p.sbufs["l1var"][:, 16:32],
                    p.sbufs["w_tmp"][:, 0:1024].rearrange(
                        "p (g dd) -> p g dd", dd=64), axis=AX.X)
            t.step("dve", ("qkall", "l1mu2"),
                   ("l1sub", "l1var", "w_sqscr", "w_tmp"), f_sub)

            def f_rstd(e):
                e.activation(p.sbufs["l1var"][:, 0:32],
                             p.sbufs["l1var"][:, 0:32], AF.Ln,
                             bias=cpool[:, C_EPS:C_EPS + 1],
                             scale=cpool[:, C_INV64:C_INV64 + 1])
                return e.activation(p.sbufs["l1var"][:, 0:32],
                                    p.sbufs["l1var"][:, 0:32], AF.Exp,
                                    bias=0.0,
                                    scale=cpool[:, C_NEGHALF:C_NEGHALF + 1])
            t.step("act", ("l1var", "cpool"), ("l1var",), f_rstd)

            def f_norm(e):
                rb = p.sbufs["l1var"][:, 0:32].unsqueeze(2).broadcast_to(
                    (128, 32, 64))
                e.tensor_tensor(
                    p.sbufs["l1sub"].rearrange("p (g dd) -> p g dd", dd=64),
                    p.sbufs["l1sub"].rearrange("p (g dd) -> p g dd", dd=64),
                    rb, ALU.mult)
                gtab = lntab[:, 0:128].unsqueeze(1).broadcast_to(
                    (128, 16, 128))
                e.tensor_tensor(
                    p.sbufs["l1sub"].rearrange("p (g dd) -> p g dd", dd=128),
                    p.sbufs["l1sub"].rearrange("p (g dd) -> p g dd", dd=128),
                    gtab, ALU.mult)
                btab = lntab[:, 128:256].unsqueeze(1).broadcast_to(
                    (128, 16, 128))
                return e.tensor_tensor(
                    qkn.rearrange("p (g dd) -> p g dd", dd=128),
                    p.sbufs["l1sub"].rearrange("p (g dd) -> p g dd", dd=128),
                    btab, ALU.add)
            t.step("dve", ("l1sub", "l1var", "l1lntab"), ("qknorm", "l1sub"),
                   f_norm)
            for tt in range(TT):
                pbk = 6 + (tt % 2)
                ptq = p.pb[pbk].ap().bitcast(BF16)

                def f_tr(e, tt=tt, ptq=ptq):
                    return e.transpose(ptq[:, 0:128],
                                       qkn[:, tt * 128:(tt + 1) * 128],
                                       identb[:])
                t.step("pe", ("qknorm", "identb"), (f"pb{pbk}",), f_tr)

                def f_cp(e, tt=tt, ptq=ptq):
                    e.tensor_copy(qb[:, tt * 128:(tt + 1) * 128],
                                  ptq[0:64, 0:128])
                    return e.tensor_copy(kb[:, tt * 128:(tt + 1) * 128],
                                         ptq[64:128, 0:128])
                t.step("dve", (f"pb{pbk}",), ("qb", "kb"), f_cp)

        for qc in range(NQC):
            nbl = 2 * qc + 2
            for i in range(nbl):
                pbk = 3 + (i % 2)

                def f_s(e, i=i, qc=qc, pbk=pbk):
                    return e.matmul(p.pb[pbk].ap()[:, 0:QCH],
                                    kb[:, i * 128:(i + 1) * 128],
                                    qb[:, qc * QCH:(qc + 1) * QCH],
                                    start=True, stop=True)
                t.step("pe", ("qb", "kb"), (f"pb{pbk}",), f_s)

                def f_sev(e, i=i, pbk=pbk):
                    e.tensor_copy(st_sb[:, i * QCH:(i + 1) * QCH],
                                  p.pb[pbk].ap()[:, 0:QCH])
                    return e.reduce_max(bmx[:, i:i + 1],
                                        p.pb[pbk].ap()[:, 0:QCH], axis=AX.X)
                t.step("dve", (f"pb{pbk}",), (f"stq{i}", "bmx"), f_sev)

            t.step("act", ("bmx",), ("bmx2",),
                   (lambda nbl: lambda e: e.activation(
                       p.sbufs["bmx2"][:, 0:nbl], bmx[:, 0:nbl], AF.Copy,
                       bias=0.0, scale=1.0))(nbl))
            t.step("dve", ("bmx2",), ("chain",),
                   (lambda nbl: lambda e: e.reduce_max(
                       chain[:, 0:1], p.sbufs["bmx2"][:, 0:nbl],
                       axis=AX.X))(nbl))
            t.step("pe", ("chain", "identf"), ("pb5",),
                   lambda e: e.transpose(p.pb[5].ap()[0:1, 256:384],
                                         chain[:, 0:1], identf[:]))

            def f_cred(e):
                return e.reduce_max(chain[0:1, 1:2],
                                    p.pb[5].ap()[0:1, 256:384],
                                    axis=AX.X, negate=True)
            t.step("dve", ("pb5",), ("chain",), f_cred)

            def f_csc(e):
                return e.activation(chain_bf[0:1, 0:1], chain[0:1, 1:2],
                                    AF.Copy, bias=0.0,
                                    scale=cpool[0:1, C_ATTSCL:C_ATTSCL + 1])
            t.step("act", ("chain", "cpool"), ("chain_bf",), f_csc)

            t.step("pe", ("chain_bf", "onesr"), ("pb5",),
                   lambda e: e.matmul(p.pb[5].ap()[:, 440:441],
                                      onesr[0:1, 0:128], chain_bf[0:1, 0:1],
                                      start=True, stop=True))
            t.step("dve", ("pb5",), ("chainM",),
                   lambda e: e.tensor_copy(chain[:, 3:4],
                                           p.pb[5].ap()[:, 440:441]))

            first_av = [True]
            for i in range(nbl):
                psl = i % 2

                def f_exp(e, i=i, psl=psl):
                    return e.activation(ptb[:, psl * QCH:(psl + 1) * QCH],
                                        st_sb[:, i * QCH:(i + 1) * QCH],
                                        AF.Exp, bias=chain[:, 3:4],
                                        scale=cpool[:, C_ATTSCL:C_ATTSCL + 1])
                t.step("act", (f"stq{i}", "chainM", "cpool"), (f"ptb{psl}",),
                       f_exp)
                if i >= nbl - 2:
                    mi = 0 if i == nbl - 2 else 1

                    def f_msk(e, psl=psl, mi=mi):
                        return e.tensor_mul(
                            ptb[:, psl * QCH:(psl + 1) * QCH],
                            ptb[:, psl * QCH:(psl + 1) * QCH],
                            mask_sb[:, mi * QCH:(mi + 1) * QCH])
                    t.step("dve", ("mask_sb", f"ptb{psl}"), (f"ptb{psl}",),
                           f_msk)

                def f_av(e, i=i, psl=psl, nbl=nbl, h=h):
                    ins = None
                    st0 = first_av[0]
                    for qs in range(2):
                        if i == nbl - 1 and qs == 0:
                            continue
                        ins = e.matmul(
                            p.pb[5].ap()[:, qs * 65:qs * 65 + 65],
                            ptb[:, psl * QCH + qs * 128:
                                psl * QCH + (qs + 1) * 128],
                            vgrp[:, i * 260 + (h % 4) * 65:
                                 i * 260 + (h % 4) * 65 + 65],
                            start=st0 and (qs == (0 if i < nbl - 1 else 1)),
                            stop=(i == nbl - 1 and qs == 1))
                    first_av[0] = False
                    return ins
                t.step("pe", (f"ptb{psl}", "vgrp"), ("pb5",), f_av)

            for qs in range(2):
                def f_rec(e, qs=qs):
                    e.activation(chain[:, 4 + qs:5 + qs],
                                 p.pb[5].ap()[:, qs * 65 + 64:qs * 65 + 65],
                                 AF.Ln, bias=0.0, scale=1.0)
                    return e.activation(chain[:, 4 + qs:5 + qs],
                                        chain[:, 4 + qs:5 + qs], AF.Exp,
                                        bias=0.0, scale=-1.0)
                t.step("act", ("pb5",), (f"rec{qs}",), f_rec)

                def f_on(e, qs=qs):
                    return e.tensor_scalar(
                        att2[:, qs * 64:(qs + 1) * 64],
                        p.pb[5].ap()[:, qs * 65:qs * 65 + 64],
                        chain[:, 4 + qs:5 + qs], None, ALU.mult)
                t.step("dve", ("pb5", f"rec{qs}"), (f"att2_{qs}",), f_on)

                pbk = 6 + qs
                pta = p.pb[pbk].ap().bitcast(BF16)

                def f_atr(e, qs=qs, pta=pta):
                    return e.transpose(pta[0:64, 0:128],
                                       att2[:, qs * 64:(qs + 1) * 64],
                                       identb[:])
                t.step("pe", (f"att2_{qs}", "identb"), (f"pb{pbk}",), f_atr)

                def f_acp(e, qs=qs, qc=qc, h=h, pta=pta):
                    tt = qc * 2 + qs
                    pbase = (h % 2) * 64
                    return e.tensor_copy(
                        attT[pbase:pbase + 64,
                             (h // 2) * T + tt * 128:
                             (h // 2) * T + (tt + 1) * 128], pta[0:64, 0:128])
                t.step("dve", (f"pb{pbk}",), ("attT",), f_acp)


def emit_proj(p, wo_name, bo_name, epilogue, src_dram=None, dst_dram=None,
              ud=None, src_res=None, dst_res=None):
    t = p.t
    attT = p.sbufs["attT"]
    wo_sb = p.sbufs[wo_name]
    bo_sb = p.sbufs[bo_name] if bo_name else None
    onesr = p.sbufs["onesr"]
    uacc = p.sbufs["uacc"]
    zsb = p.sbufs["w_z"]

    for tt in range(TT):
        usl = tt % 2
        if epilogue == "accum_u":
            t.step("scd", (f"ud{tt}",), (f"uacc{usl}",),
                   (lambda tt, usl: lambda e: e.dma_start(
                       uacc[:, usl * 1024:(usl + 1) * 1024],
                       ud[tt * 128:(tt + 1) * 128, :]))(tt, usl))
        for oc in range(2):
            pbk = oc

            def f_p(e, tt=tt, oc=oc, pbk=pbk):
                ins = None
                for ct in range(CT):
                    ins = e.matmul(
                        p.pb[pbk].ap()[:, 0:512],
                        attT[:, ct * T + tt * 128:ct * T + (tt + 1) * 128],
                        wo_sb[:, ct * 1024 + oc * 512:
                              ct * 1024 + (oc + 1) * 512],
                        start=(ct == 0),
                        stop=(ct == CT - 1 and bo_sb is None))
                if bo_sb is not None:
                    ins = e.matmul(p.pb[pbk].ap()[:, 0:512], onesr[0:1, 0:128],
                                   bo_sb[0:1, oc * 512:(oc + 1) * 512],
                                   start=False, stop=True)
                return ins
            rd = ("attT", wo_name, "onesr")
            if bo_name:
                rd = rd + (bo_name,)
            t.step("pe", rd, (f"pb{pbk}",), f_p)
            if epilogue == "accum_u":
                def f_a(e, oc=oc, pbk=pbk, usl=usl):
                    sl = slice(usl * 1024 + oc * 512,
                               usl * 1024 + (oc + 1) * 512)
                    return e.tensor_add(uacc[:, sl], uacc[:, sl],
                                        p.pb[pbk].ap()[:, 0:512])
                t.step("dve", (f"pb{pbk}", f"uacc{usl}"), (f"uacc{usl}",),
                       f_a)
            else:
                def f_z(e, tt=tt, oc=oc, pbk=pbk):
                    return e.tensor_copy(
                        zsb[:, (tt % 2) * 1024 + oc * 512:
                            (tt % 2) * 1024 + (oc + 1) * 512],
                        p.pb[pbk].ap()[:, 0:512])
                t.step("dve", (f"pb{pbk}",), (f"w_z{tt % 2}o{oc}",), f_z)
        if epilogue == "accum_u":
            t.step("scd", (f"uacc{usl}",), (f"ud{tt}",),
                   (lambda tt, usl: lambda e: e.dma_start(
                       ud[tt * 128:(tt + 1) * 128, :],
                       uacc[:, usl * 1024:(usl + 1) * 1024]))(tt, usl))
        else:
            emit_sandwich_tile(
                p, tt, zsb[:, (tt % 2) * 1024:(tt % 2 + 1) * 1024],
                (f"w_z{tt % 2}o0", f"w_z{tt % 2}o1"), src_dram, dst_dram,
                src_res, dst_res)


def emit_sandwich_tile(p, tt, z_ap, z_reads, src_dram, dst_dram,
                       src_res, dst_res):
    t = p.t
    xin = p.sbufs["w_xin"]
    out_sb = p.sbufs["w_out"]
    sl = tt % 2

    t.step("scd", (f"{src_res}{tt}",), (f"w_xin{sl}",),
           (lambda tt, sl: lambda e: e.dma_start(
               xin[:, sl * 1024:(sl + 1) * 1024],
               src_dram[tt * 128:(tt + 1) * 128, :]))(tt, sl))

    def out_fn(negmu, rstd, streads):
        def f(e):
            e.tensor_scalar(out_sb[:, sl * 1024:(sl + 1) * 1024], z_ap,
                            negmu, rstd, ALU.add, ALU.mult)
            return e.tensor_add(out_sb[:, sl * 1024:(sl + 1) * 1024],
                                out_sb[:, sl * 1024:(sl + 1) * 1024],
                                xin[:, sl * 1024:(sl + 1) * 1024])
        t.step("dve", tuple(z_reads) + streads + (f"w_xin{sl}",),
               (f"w_out{sl}",), f)
        t.step("scd", (f"w_out{sl}",), (f"{dst_res}{tt}",),
               (lambda tt, sl: lambda e: e.dma_start(
                   dst_dram[tt * 128:(tt + 1) * 128, :],
                   out_sb[:, sl * 1024:(sl + 1) * 1024]))(tt, sl))
    ln_steps(p, z_ap, tuple(z_reads), (tt % 2) * 4, out_fn)


def emit_u_sandwich(p, src, dst, ud, src_res, dst_res):
    zsb = p.sbufs["w_z"]
    t = p.t
    for tt in range(TT):
        zl = tt % 2
        t.step("scd", (f"ud{tt}",), (f"w_z{zl}o0", f"w_z{zl}o1"),
               (lambda tt, zl: lambda e: e.dma_start(
                   zsb[:, zl * 1024:(zl + 1) * 1024],
                   ud[tt * 128:(tt + 1) * 128, :]))(tt, zl))
        emit_sandwich_tile(p, tt, zsb[:, zl * 1024:(zl + 1) * 1024],
                           (f"w_z{zl}o0", f"w_z{zl}o1"), src, dst,
                           src_res, dst_res)


def emit_moe(p, nxT_name, ud):
    t, d = p.t, p.dram
    nxT = p.sbufs[nxT_name]
    onesr = p.sbufs["onesr"]
    cpool = p.sbufs["cpool"]
    gl = p.sbufs["moe_gl"]
    gw = p.sbufs["moe_gw"]
    gate_sb = p.sbufs["gate_sb"]
    gate_bf = p.sbufs["gate_bf"]
    fc1sb = p.sbufs["fc1sb"]
    fc2sb = p.sbufs["fc2sb"]
    fc1b_sb = p.sbufs["fc1b_sb"]
    fc2b_sb = p.sbufs["fc2b_sb"]
    fc2b_bf = p.sbufs["fc2b_bf"]
    hT = p.sbufs["moe_hT"]
    uacc = p.sbufs["uacc"]

    t.step("syd", (), ("gate_sb",),
           lambda e: e.dma_start(gate_sb, d["f0_gate"]))
    t.step("dve", ("gate_sb",), ("gate_bf",),
           lambda e: e.tensor_copy(gate_bf, gate_sb))

    for tt in range(TT):
        def f_g(e, tt=tt):
            ins = None
            for ct in range(CT):
                ins = e.matmul(
                    p.pb[2].ap()[:, tt * 8:(tt + 1) * 8],
                    nxT[:, nxc(ct, tt * 128):nxc(ct, (tt + 1) * 128)],
                    gate_bf[:, ct * 8:(ct + 1) * 8],
                    start=(ct == 0 and tt == 0),
                    stop=(ct == CT - 1 and tt == TT - 1))
            return ins
        t.step("pe", (nxT_name, "gate_bf"), ("pb2",), f_g)
    t.step("dve", ("pb2",), ("moe_gl",),
           lambda e: e.tensor_copy(gl[:, 0:128], p.pb[2].ap()[:, 0:128]))

    def f_m1(e):
        return e.reduce_max(gw[:, 0:16],
                            gl[:, 0:128].rearrange("p (g k) -> p g k", k=8),
                            axis=AX.X)
    t.step("dve", ("moe_gl",), ("moe_gw",), f_m1)
    t.step("act", ("moe_gw",), ("moe_m1",),
           lambda e: e.activation(gw[:, 16:32], gw[:, 0:16], AF.Copy,
                                  bias=0.0, scale=1.0))

    def f_eq1(e):
        m1b = gw[:, 16:32].unsqueeze(2).broadcast_to((128, 16, 8))
        e.tensor_tensor(gw[:, 128:256].rearrange("p (g k) -> p g k", k=8),
                        gl[:, 0:128].rearrange("p (g k) -> p g k", k=8),
                        m1b, ALU.is_equal)
        e.scalar_tensor_tensor(gw[:, 256:384], gw[:, 128:256], -1e30,
                               gl[:, 0:128], ALU.mult, ALU.add)
        return e.reduce_max(gw[:, 32:48],
                            gw[:, 256:384].rearrange("p (g k) -> p g k", k=8),
                            axis=AX.X)
    t.step("dve", ("moe_gl", "moe_m1"), ("moe_gw",), f_eq1)
    t.step("act", ("moe_gw",), ("moe_m2",),
           lambda e: e.activation(gw[:, 48:64], gw[:, 32:48], AF.Copy,
                                  bias=0.0, scale=1.0))
    t.step("dve", ("moe_m1", "moe_m2"), ("moe_d",),
           lambda e: e.tensor_sub(gw[:, 64:80], gw[:, 48:64], gw[:, 16:32]))

    def f_pa(e):
        e.activation(gw[:, 80:96], gw[:, 64:80], AF.Exp, bias=0.0, scale=1.0)
        e.activation(gw[:, 96:112], gw[:, 80:96], AF.Ln,
                     bias=cpool[:, C_ONE:C_ONE + 1], scale=1.0)
        return e.activation(gw[:, 96:112], gw[:, 96:112], AF.Exp,
                            bias=0.0, scale=-1.0)
    t.step("act", ("moe_d", "cpool"), ("moe_gw2",), f_pa)

    def f_dg(e):
        e.tensor_mul(gw[:, 112:128], gw[:, 80:96], gw[:, 96:112])
        m2b = gw[:, 48:64].unsqueeze(2).broadcast_to((128, 16, 8))
        e.tensor_tensor(gw[:, 384:512].rearrange("p (g k) -> p g k", k=8),
                        gl[:, 0:128].rearrange("p (g k) -> p g k", k=8),
                        m2b, ALU.is_equal)
        p1b = gw[:, 96:112].unsqueeze(2).broadcast_to((128, 16, 8))
        e.tensor_tensor(gw[:, 128:256].rearrange("p (g k) -> p g k", k=8),
                        gw[:, 128:256].rearrange("p (g k) -> p g k", k=8),
                        p1b, ALU.mult)
        p2b = gw[:, 112:128].unsqueeze(2).broadcast_to((128, 16, 8))
        e.tensor_tensor(gw[:, 384:512].rearrange("p (g k) -> p g k", k=8),
                        gw[:, 384:512].rearrange("p (g k) -> p g k", k=8),
                        p2b, ALU.mult)
        return e.tensor_add(gl[:, 128:256], gw[:, 128:256], gw[:, 384:512])
    t.step("dve", ("moe_gl", "moe_gw", "moe_gw2", "moe_m1", "moe_m2"),
           ("moe_gw", "moe_g"), f_dg)

    for ei in range(NE):
        t.step("syd", (), ("fc1b_sb",),
               (lambda ei: lambda e: e.dma_start(
                   fc1b_sb, d["f0_fc1b"][ei]))(ei))
        t.step("syd", (), ("fc2b_sb",),
               (lambda ei: lambda e: e.dma_start(
                   fc2b_sb, d["f0_fc2b"][ei]))(ei))
        t.step("dve", ("fc2b_sb",), ("fc2b_bf",),
               lambda e: e.tensor_copy(fc2b_bf, fc2b_sb))
        for tc in range(4):
            for fch in range(FFN // 512):
                wsl = fch % 2
                t.step("syd", (f"fc1bf_{ei}",), (f"fc1w{wsl}",),
                       (lambda ei, fch, wsl: lambda e: e.dma_start(
                           fc1sb[:, wsl * 4096:(wsl + 1) * 4096],
                           d["f0_fc1_bf"][ei, fch]))(ei, fch, wsl))
                for ft in range(4):
                    fglob = fch * 4 + ft
                    pbk = 3 + (fglob % 2)

                    def f_f1(e, tc=tc, wsl=wsl, ft=ft, pbk=pbk):
                        ins = None
                        for ct in range(CT):
                            ins = e.matmul(
                                p.pb[pbk].ap()[:, 0:512],
                                fc1sb[:, wsl * 4096 + ct * 512 + ft * 128:
                                      wsl * 4096 + ct * 512 + (ft + 1) * 128],
                                nxT[:, nxc(ct, tc * 512):
                                    nxc(ct, (tc + 1) * 512)],
                                start=(ct == 0), stop=(ct == CT - 1))
                        return ins
                    t.step("pe", (f"fc1w{wsl}", nxT_name), (f"pb{pbk}",),
                           f_f1)

                    def f_g1(e, fglob=fglob, pbk=pbk):
                        return e.activation(
                            hT[:, fglob * 512:(fglob + 1) * 512],
                            p.pb[pbk].ap()[:, 0:512], AF.Gelu,
                            bias=fc1b_sb[:, fglob:fglob + 1], scale=1.0)
                    t.step("act", (f"pb{pbk}", "fc1b_sb"), ("moe_hT",), f_g1)
            for ft in range(FFN // 128):
                wsl = ft % 2
                t.step("syd", (f"fc2bf_{ei}",), (f"fc2w{wsl}",),
                       (lambda ei, ft, wsl: lambda e: e.dma_start(
                           fc2sb[:, wsl * 1024:(wsl + 1) * 1024],
                           d["f0_fc2_bf"][ei, ft]))(ei, ft, wsl))

                def f_f2(e, ft=ft, wsl=wsl):
                    ins = None
                    for ts in range(4):
                        for oc in range(2):
                            ins = e.matmul(
                                p.pb[ts * 2 + oc].ap()[:, 0:512],
                                hT[:, ft * 512 + ts * 128:
                                   ft * 512 + (ts + 1) * 128],
                                fc2sb[:, wsl * 1024 + oc * 512:
                                      wsl * 1024 + (oc + 1) * 512],
                                start=(ft == 0), stop=False)
                    return ins
                t.step("pe", (f"fc2w{wsl}", "moe_hT"),
                       tuple(f"pb{bk}" for bk in range(8)), f_f2)

            def f_b2(e, ei=ei):
                ins = None
                for ts in range(4):
                    for oc in range(2):
                        ins = e.matmul(
                            p.pb[ts * 2 + oc].ap()[:, 0:512],
                            onesr[0:1, 0:128],
                            fc2b_bf[0:1, oc * 512:(oc + 1) * 512],
                            start=False, stop=True)
                return ins
            t.step("pe", ("fc2b_bf", "onesr"),
                   tuple(f"pb{bk}" for bk in range(8)), f_b2)
            for ts in range(4):
                tt = tc * 4 + ts
                usl = ts % 2
                t.step("scd", (f"ud{tt}",), (f"uacc{usl}",),
                       (lambda tt, usl: lambda e: e.dma_start(
                           uacc[:, usl * 1024:(usl + 1) * 1024],
                           ud[tt * 128:(tt + 1) * 128, :]))(tt, usl))
                for oc in range(2):
                    bk = ts * 2 + oc

                    def f_u(e, bk=bk, tt=tt, oc=oc, ei=ei, usl=usl):
                        sl = slice(usl * 1024 + oc * 512,
                                   usl * 1024 + (oc + 1) * 512)
                        return e.scalar_tensor_tensor(
                            uacc[:, sl], p.pb[bk].ap()[:, 0:512],
                            gl[:, 128 + tt * 8 + ei:129 + tt * 8 + ei],
                            uacc[:, sl], ALU.mult, ALU.add)
                    t.step("dve", (f"pb{bk}", "moe_g", f"uacc{usl}"),
                           (f"uacc{usl}",), f_u)
                t.step("scd", (f"uacc{usl}",), (f"ud{tt}",),
                       (lambda tt, usl: lambda e: e.dma_start(
                           ud[tt * 128:(tt + 1) * 128, :],
                           uacc[:, usl * 1024:(usl + 1) * 1024]))(tt, usl))


def emit_qlora_ffn(p, ud):
    t, d = p.t, p.dram
    onesr = p.sbufs["onesr"]
    identb = p.sbufs["identb"]
    hT = p.sbufs["nxT"]
    ygT = p.sbufs["attT"]
    qt_sb = p.sbufs["qt_sb"]
    sc_sb = p.sbufs["sc_sb"]
    wdq = p.sbufs["wdq"]
    la_sb = p.sbufs["la_sb"]
    lb_sb = p.sbufs["lb_sb"]
    qbias = p.sbufs["qbias"]
    qbias_f = p.sbufs["qbias_f"]
    sT = p.sbufs["sT"]
    g_tab = p.sbufs["gb_g"]
    b_tab = p.sbufs["gb_b"]
    zsb = p.sbufs["w_z"]
    sq = p.sbufs["w_sq"]
    ybf = p.sbufs["w_nx"]
    uacc = p.sbufs["uacc"]
    nxbf = p.sbufs["w_nx"]

    t.step("gpd", (), ("la_sb",), lambda e: e.dma_start(
        la_sb.rearrange("p (i r) -> p i r", r=CT * LORA_R),
        d["q1_la"].transpose([1, 0, 2])))

    def cast_and_T():
        for tt in range(TT):
            sl = tt % 2
            t.step("gpd", (f"ud{tt}",), (f"w_nx{sl}",),
                   (lambda tt, sl: lambda e: e.dma_start(
                       nxbf[:, sl * 1024:(sl + 1) * 1024],
                       ud[tt * 128:(tt + 1) * 128, :]))(tt, sl))
            emit_transposes_tile(
                p, tt, nxbf[:, sl * 1024:(sl + 1) * 1024], f"w_nx{sl}",
                "nxT")

    cast_and_T()

    for blk in range(3):
        t.step("syd", (), ("gb_g",),
               (lambda blk: lambda e: e.dma_start(
                   g_tab,
                   d["q1_lng"][blk].partition_broadcast(128).squeeze(1)))(blk))
        t.step("syd", (), ("gb_b",),
               (lambda blk: lambda e: e.dma_start(
                   b_tab,
                   d["q1_lnb"][blk].partition_broadcast(128).squeeze(1)))(blk))
        for half in range(2):
            mi = 2 * blk + half
            inT, inT_name = (hT, "nxT") if half == 0 else (ygT, "attT")
            t.step("gpd", (), ("lb_sb",),
                   (lambda mi: lambda e: e.dma_start(
                       lb_sb, d["q1_lb"][mi]))(mi))
            t.step("syd", (), ("qbias_f",),
                   (lambda mi: lambda e: e.dma_start(
                       qbias_f, d["q1_b"][mi]))(mi))
            t.step("dve", ("qbias_f",), ("qbias",),
                   lambda e: e.tensor_copy(qbias, qbias_f))
            for chunk in range(4):
                def f_sl(e, mi=mi, chunk=chunk, inT=inT):
                    ins = None
                    for ct in range(CT):
                        ins = e.matmul(
                            p.pb[2].ap()[0:32, 0:512],
                            la_sb[:, mi * (CT * LORA_R) + ct * LORA_R:
                                  mi * (CT * LORA_R) + (ct + 1) * LORA_R],
                            inT[:, nxc(ct, chunk * 512):
                                nxc(ct, (chunk + 1) * 512)],
                            start=(ct == 0), stop=(ct == CT - 1))
                    return ins
                t.step("pe", ("la_sb", inT_name), ("pb2",), f_sl)

                def f_se(e, chunk=chunk):
                    return e.tensor_copy(
                        sT[0:32, chunk * 512:(chunk + 1) * 512],
                        p.pb[2].ap()[0:32, 0:512])
                t.step("dve", ("pb2",), ("sT",), f_se)
            for oc in range(2):
                for hf in range(2):
                    t.step("syd", (), ("qt_sb",),
                           (lambda mi, oc, hf: lambda e: e.dma_start(
                               qt_sb, d["q1_qt"][mi, oc][
                                   :, hf * 2048:(hf + 1) * 2048]))(mi, oc, hf))
                    t.step("syd", (), ("sc_sb",),
                           (lambda mi, oc, hf: lambda e: e.dma_start(
                               sc_sb, d["q1_sc"][mi, oc][
                                   :, hf * 2048:(hf + 1) * 2048]))(mi, oc, hf))

                    def f_dq(e, oc=oc, hf=hf):
                        ins = None
                        for dt4 in range(4):
                            slc = slice(dt4 * 512, (dt4 + 1) * 512)
                            e.tensor_copy(sq[:, 0:512], qt_sb[:, slc])
                            e.tensor_scalar(sq[:, 512:1024], sq[:, 0:512],
                                            1.0 / 7.5, -1.0, ALU.mult,
                                            ALU.add)
                            ins = e.tensor_mul(
                                wdq[:, oc * 4096 + hf * 2048 + dt4 * 512:
                                    oc * 4096 + hf * 2048 + (dt4 + 1) * 512],
                                sq[:, 512:1024], sc_sb[:, slc])
                        return ins
                    t.step("dve", ("qt_sb", "sc_sb"),
                           (f"wdq{oc}", "w_sqscr"), f_dq)
            for tt in range(TT):
                usl = tt % 2
                if half == 1:
                    t.step("scd", (f"ud{tt}",), (f"uacc{usl}",),
                           (lambda tt, usl: lambda e: e.dma_start(
                               uacc[:, usl * 1024:(usl + 1) * 1024],
                               ud[tt * 128:(tt + 1) * 128, :]))(tt, usl))
                for oc in range(2):
                    pbk = oc

                    def f_gm(e, mi=mi, oc=oc, tt=tt, pbk=pbk, inT=inT):
                        for ct in range(CT):
                            e.matmul(
                                p.pb[pbk].ap()[:, 0:512],
                                inT[:, nxc(ct, tt * 128):
                                    nxc(ct, (tt + 1) * 128)],
                                wdq[:, oc * 4096 + ct * 512:
                                    oc * 4096 + (ct + 1) * 512],
                                start=(ct == 0), stop=False)
                        e.matmul(
                            p.pb[pbk].ap()[:, 0:512],
                            sT[0:32, tt * 128:(tt + 1) * 128],
                            lb_sb[0:32, oc * 512:(oc + 1) * 512],
                            start=False, stop=False)
                        return e.matmul(
                            p.pb[pbk].ap()[:, 0:512], onesr[0:1, 0:128],
                            qbias[0:1, oc * 512:(oc + 1) * 512],
                            start=False, stop=True)
                    t.step("pe", (f"wdq{oc}", inT_name, "sT", "lb_sb",
                                  "qbias", "onesr"), (f"pb{pbk}",), f_gm)
                    if half == 1:
                        def f_h(e, oc=oc, pbk=pbk, usl=usl):
                            sl = slice(usl * 1024 + oc * 512,
                                       usl * 1024 + (oc + 1) * 512)
                            return e.tensor_add(uacc[:, sl], uacc[:, sl],
                                                p.pb[pbk].ap()[:, 0:512])
                        t.step("dve", (f"pb{pbk}", f"uacc{usl}"),
                               (f"uacc{usl}",), f_h)
                    else:
                        def f_z(e, tt=tt, oc=oc, pbk=pbk):
                            return e.tensor_copy(
                                zsb[:, (tt % 2) * 1024 + oc * 512:
                                    (tt % 2) * 1024 + (oc + 1) * 512],
                                p.pb[pbk].ap()[:, 0:512])
                        t.step("dve", (f"pb{pbk}",),
                               (f"w_z{tt % 2}o{oc}",), f_z)
                if half == 1:
                    t.step("scd", (f"uacc{usl}",), (f"ud{tt}",),
                           (lambda tt, usl: lambda e: e.dma_start(
                               ud[tt * 128:(tt + 1) * 128, :],
                               uacc[:, usl * 1024:(usl + 1) * 1024]))(tt,
                                                                      usl))
                else:
                    z_ap = zsb[:, (tt % 2) * 1024:(tt % 2 + 1) * 1024]
                    zr = (f"w_z{tt % 2}o0", f"w_z{tt % 2}o1")

                    def mko(tt=tt, z_ap=z_ap, zr=zr):
                        def out_fn(negmu, rstd, streads):
                            tmp = p.sbufs["w_tmp"]
                            nsl = tt % 2

                            def f(e):
                                e.tensor_scalar(tmp[:, 0:1024], z_ap, negmu,
                                                rstd, ALU.add, ALU.mult)
                                e.tensor_mul(tmp[:, 0:1024], tmp[:, 0:1024],
                                             g_tab)
                                return e.tensor_add(tmp[:, 0:1024],
                                                    tmp[:, 0:1024], b_tab)
                            t.step("dve", zr + streads + ("gb_g", "gb_b"),
                                   ("w_tmp",), f)

                            def fg(e):
                                return e.activation(
                                    ybf[:, nsl * 1024:(nsl + 1) * 1024],
                                    tmp[:, 0:1024], AF.Gelu, bias=0.0,
                                    scale=1.0)
                            t.step("act", ("w_tmp",), (f"w_nx{nsl}",), fg)
                        return out_fn
                    ln_steps(p, z_ap, zr, (tt % 2) * 4, mko())
                    emit_transposes_tile(
                        p, tt, ybf[:, (tt % 2) * 1024:(tt % 2 + 1) * 1024],
                        f"w_nx{tt % 2}", "attT")
        if blk < 2:
            p.fence(("nxT",))
            cast_and_T()


def emit_slot(p, sl, src, dst, src_res, dst_res):
    d = p.dram
    t = p.t
    ud = d["ud"]
    if sl == "l0a":
        p.fence(SCR_NAMES)
        emit_nx_phase(p, "nxT", src, src_res)
        p.tap("dbg_nxT", p.sbufs["nxT"], ("nxT",))
        p.fence(WT_NAMES + SCR_NAMES)
        emit_attn_heads(p, "a0", "rope", "nxT")
        p.tap("dbg_attT", p.sbufs["attT"], ("attT",))
        p.fence(WT_NAMES)
        t.step("gpd", (), ("wo_sb",),
               lambda e: e.dma_start(p.sbufs["wo_sb"], d["a0_wo"]))
        t.step("gpd", (), ("bo_sb",),
               lambda e: e.dma_start(p.sbufs["bo_sb"], d["a0_bo"]))
        p.fence(SCR_NAMES)
        emit_proj(p, "wo_sb", "bo_sb", "sandwich", src_dram=src,
                  dst_dram=dst, src_res=src_res, dst_res=dst_res)
    elif sl == "l0f":
        p.fence(SCR_NAMES)
        emit_u_phase(p, src, ud, src_res)
        t.step("syd", (), ("gb_g",), lambda e: e.dma_start(
            p.sbufs["gb_g"],
            d["f0_ln1w"].partition_broadcast(128).squeeze(1)))
        t.step("syd", (), ("gb_b",), lambda e: e.dma_start(
            p.sbufs["gb_b"],
            d["f0_ln1b"].partition_broadcast(128).squeeze(1)))
        emit_nx_phase(p, "nxT", ud, "ud", gamma="gb_g", beta="gb_b")
        t.step("gpd", (), ("bqk_sb",),
               lambda e: e.dma_start(p.sbufs["bqk_sb"], d["f0_bqk"]))
        t.step("gpd", (), ("bv_sb",),
               lambda e: e.dma_start(p.sbufs["bv_sb"], d["f0_bv"]))
        p.fence(WT_NAMES + SCR_NAMES)
        emit_attn_heads(p, "f0", "plain", "nxT")
        p.fence(WT_NAMES)
        t.step("gpd", (), ("wo_sb",),
               lambda e: e.dma_start(p.sbufs["wo_sb"], d["f0_wo"]))
        t.step("gpd", (), ("bo_sb",),
               lambda e: e.dma_start(p.sbufs["bo_sb"], d["f0_bo"]))
        p.fence(SCR_NAMES)
        emit_proj(p, "wo_sb", "bo_sb", "accum_u", ud=ud)
        t.step("syd", (), ("gb_g",), lambda e: e.dma_start(
            p.sbufs["gb_g"],
            d["f0_ln2w"].partition_broadcast(128).squeeze(1)))
        t.step("syd", (), ("gb_b",), lambda e: e.dma_start(
            p.sbufs["gb_b"],
            d["f0_ln2b"].partition_broadcast(128).squeeze(1)))
        emit_nx_phase(p, "nxT", ud, "ud", gamma="gb_g", beta="gb_b")
        p.fence(WT_NAMES + ATT_NAMES)
        emit_moe(p, "nxT", ud)
        p.fence(SCR_NAMES + ATT_NAMES)
        emit_u_sandwich(p, src, dst, ud, src_res, dst_res)
    elif sl == "l1a":
        p.fence(SCR_NAMES)
        emit_nx_phase(p, "nxT", src, src_res)
        t.step("syd", (), ("l1lntab",), lambda e: e.dma_start(
            p.sbufs["l1lntab"],
            d["a1_lntab"].partition_broadcast(128).squeeze(1)))
        p.fence(WT_NAMES + SCR_NAMES)
        emit_attn_heads(p, "a1", "l1", "nxT")
        p.fence(WT_NAMES)
        t.step("gpd", (), ("wo_sb",),
               lambda e: e.dma_start(p.sbufs["wo_sb"], d["a1_wu"]))
        p.fence(SCR_NAMES)
        emit_proj(p, "wo_sb", None, "sandwich", src_dram=src, dst_dram=dst,
                  src_res=src_res, dst_res=dst_res)
    elif sl == "l1f":
        p.fence(SCR_NAMES)
        emit_u_phase(p, src, ud, src_res)
        p.fence(WT_NAMES + ATT_NAMES)
        emit_qlora_ffn(p, ud)
        p.fence(SCR_NAMES)
        emit_u_sandwich(p, src, dst, ud, src_res, dst_res)


def build_program(seqs=SPC, slots=("l0a", "l0f", "l1a", "l1f"), dbg=()):
    nc = bass.Bass("TRN2", target_bir_lowering=False)
    dram = {}

    def din(name, shape, dtype=F32):
        dram[name] = nc.dram_tensor(name, list(shape), dtype,
                                    kind="ExternalInput").ap()

    def dout(name, shape, dtype=F32):
        dram[name] = nc.dram_tensor(name, list(shape), dtype,
                                    kind="ExternalOutput").ap()

    def dint(name, shape, dtype=F32):
        dram[name] = nc.dram_tensor(name, list(shape), dtype,
                                    kind="Internal").ap()

    din("x", (SPC, T, C))
    dout("y", (SPC, T, C))
    dint("xa", (SPC, T, C))
    dint("xb", (SPC, T, C))
    dint("ud", (T, C))

    din("consts", (1, C_NCONST))
    din("identf", (128, 128))
    din("ropec", (128, T))
    din("ropes", (128, T))
    din("masks", (2, 128, QCH))

    din("a0_wqk", (H, 128, C))
    din("a0_wv", (4, 128, 2048))
    din("a0_wo", (128, CT * 1024))
    din("a0_bo", (1, C))

    din("f0_wqk", (H, 128, C))
    din("f0_bqk", (1, H * 128))
    din("f0_wv", (4, 128, 2048))
    din("f0_bv", (1, 4 * 256))
    din("f0_wo", (128, CT * 1024))
    din("f0_bo", (1, C))
    din("f0_ln1w", (1, C))
    din("f0_ln1b", (1, C))
    din("f0_ln2w", (1, C))
    din("f0_ln2b", (1, C))
    din("f0_gate", (128, CT * NE))
    din("f0_fc1", (NE, FFN // 512, 128, CT * 512))
    din("f0_fc1b", (NE, 128, FFN // 128))
    din("f0_fc2", (NE, FFN // 128, 128, C))
    din("f0_fc2b", (NE, 1, C))
    dint("f0_fc1_bf", (NE, FFN // 512, 128, CT * 512), BF16)
    dint("f0_fc2_bf", (NE, FFN // 128, 128, C), BF16)

    din("a1_wqkv", (H, 128, CT * 192))
    din("a1_wu", (128, CT * 1024))
    din("a1_lntab", (1, 256))

    din("q1_qt", (NMAT, 2, 128, CT * 512), I32)
    din("q1_sc", (NMAT, 2, 128, CT * 512))
    din("q1_b", (NMAT, 1, C))
    din("q1_la", (NMAT, 128, CT * LORA_R))
    din("q1_lb", (NMAT, LORA_R, C))
    din("q1_lng", (3, 1, C))
    din("q1_lnb", (3, 1, C))

    for nm, shp, dt in (("dbg_nxT", (128, CT * T), BF16),
                        ("dbg_qb", (64, T), BF16),
                        ("dbg_kb", (64, T), BF16),
                        ("dbg_attT", (128, CT * T), BF16),
                        ("dbg_st", (128, 16 * QCH), F32),
                        ("dbg_vg", (128, TT * 260), BF16)):
        if nm in dbg:
            dram[nm] = nc.dram_tensor(nm, list(shp), dt,
                                      kind="ExternalOutput").ap()
    trk = Trk()
    with ExitStack() as ctx:
        p = P(nc, ctx, trk, dram, dbg)
        p.sb("cpool", [128, C_NCONST], F32)
        p.sb("identf", [128, 128], F32)
        p.sb("identb", [128, 128], BF16)
        p.sb("onesr", [1, 512], BF16)
        p.sb("mask_sb", [128, 2 * QCH], F32)
        p.sb("nxT", [128, CT * T], BF16)
        att_ar = p.sb("att_ar", [128, CT * T], BF16)
        p.sbufs["attT"] = att_ar
        p.sbufs["moe_hT"] = att_ar
        p.sb("vgrp", [128, TT * 260], BF16)
        p.sb("st_sb", [128, 16 * QCH], F32)

        p.sb("qkc", [128, 512], F32)
        p.sb("rot", [128, 512], F32)
        p.sb("ptbT", [128, 2 * QCH], BF16)
        p.sbufs["ptb"] = p.sbufs["ptbT"]
        p.sb("att2", [128, 128], BF16)
        p.sb("bmx", [128, 16], F32)
        p.sb("bmx2", [128, 16], F32)
        p.sb("chain", [128, 8], F32)
        p.sb("chain_bf", [1, 2], BF16)
        p.sb("l1mu", [128, 64], F32)
        p.sb("l1var", [128, 32], F32)
        p.sb("l1lntab", [128, 256], F32)
        p.sb("gb_g", [128, C], F32)
        p.sb("gb_b", [128, C], F32)
        p.sb("moe_gl", [128, 256], F32)
        p.sb("moe_gw", [128, 512], F32)
        p.sb("fc1b_sb", [128, FFN // 128], F32)
        p.sb("fc2b_sb", [1, C], F32)
        p.sb("fc2b_bf", [1, C], BF16)
        p.sb("bo_sb", [1, C], BF16)
        p.sb("bqk_sb", [1, H * 128], BF16)
        p.sb("bv_sb", [1, 4 * 256], BF16)
        p.sb("qbias_f", [1, C], F32)
        p.sb("qbias", [1, C], BF16)
        p.sb("la_sb", [128, NMAT * CT * LORA_R], BF16)
        p.sb("lb_sb", [LORA_R, C], BF16)
        zqk = p.sb("zqk", [128, 2 * 1024], F32)
        p.sbufs["w_z"] = zqk
        qbkb = zqk.bitcast(BF16)
        p.sbufs["qb"] = qbkb[0:64, 0:2048]
        p.sbufs["kb"] = qbkb[0:64, 2048:4096]
        p.sb("w_xin", [128, 2 * 1024], F32)
        p.sb("w_st", [128, 8], F32)
        p.sb("w_sq", [128, 1024], F32)
        p.sb("w_nx", [128, 2 * 1024], BF16)
        p.sb("w_tmp", [128, 1024], F32)
        uo = p.sb("uacc", [128, 2 * 1024], F32)
        p.sbufs["w_out"] = uo
        p.sbufs["l1sub"] = p.sbufs["uacc"][:, 0:2048]
        p.sbufs["qkall"] = p.sbufs["w_xin"][:, 0:2048]
        p.sbufs["qknorm"] = p.sbufs["w_nx"][:, 0:2048]
        wt = p.sb("wt_ar", [128, 9216], F32)
        p.view("wqk_sb", wt, 0, 128, 3072, BF16)
        p.view("wv_sb", wt, 6144, 128, 4096, BF16)
        p.view("cos2", wt, 14336, 128, 2048, F32)
        p.view("sin2", wt, 22528, 128, 2048, F32)
        p.view("wo_sb", wt, 0, 128, 8192, BF16)
        p.view("fc1sb", wt, 0, 128, 8192, BF16)
        p.view("fc2sb", wt, 16384, 128, 2048, BF16)
        p.view("gate_sb", wt, 20480, 128, 64, F32)
        p.view("gate_bf", wt, 20736, 128, 64, BF16)
        p.view("wdq", wt, 0, 128, 8192, BF16)
        p.view("qt_sb", wt, 16384, 128, 2048, I32)
        p.view("sc_sb", wt, 24576, 128, 2048, F32)
        p.view("sT", wt, 32768, 32, 2048, BF16)

        emit_init(p)
        phase0_done = [False]
        for s in range(seqs):
            cur = dram["x"][s]
            cur_res = f"xinp{s}_"
            nslots = len(slots)
            for idx, sl in enumerate(slots):
                if sl == "l0f" and not phase0_done[0]:
                    emit_phase0_cast(p)
                    phase0_done[0] = True
                if idx == nslots - 1:
                    nxt = dram["y"][s]
                else:
                    nxt = dram["xa"][s] if idx % 2 == 0 else dram["xb"][s]
                nxt_res = f"xch{s}_{idx}_"
                emit_slot(p, sl, cur, nxt, cur_res, nxt_res)
                cur = nxt
                cur_res = nxt_res
        trk.emit(nc, ctx)
    return nc



def _t128(a):
    Cdim, X = a.shape
    return np.ascontiguousarray(
        a.reshape(Cdim // 128, 128, X).transpose(1, 0, 2).reshape(128, -1))


def pack_inputs(inp):
    f32 = np.float32
    g = {}
    g["consts"] = np.array([[1e-5, 1.0 / 1024, -0.5, 0.125, 1.0 / 64,
                             1.0, -1.0 / 1024, 1.0 / 64]], f32)
    g["identf"] = np.eye(128, dtype=f32)
    invf = (1.0 / 10000.0 ** (np.arange(32, dtype=np.float64) / 32))
    tpos = np.arange(T, dtype=np.float64)
    fr = np.outer(invf, tpos)
    g["ropec"] = np.ascontiguousarray(np.tile(np.cos(fr).astype(f32),
                                              (4, 1)))
    g["ropes"] = np.ascontiguousarray(np.tile(np.sin(fr).astype(f32),
                                              (4, 1)))
    kk = np.arange(128)[:, None]
    qq = np.arange(QCH)[None, :]
    g["masks"] = np.stack([(kk <= qq).astype(f32),
                           (kk + 128 <= qq).astype(f32)])

    W = np.asarray(inp["L0_attn_Wqkv"], f32)
    wqk = np.empty((H, 128, C), f32)
    for h in range(H):
        q = W[:, h * 64:(h + 1) * 64]
        k = W[:, 1024 + h * 64:1024 + (h + 1) * 64]
        cols = np.concatenate([q[:, 0::2], k[:, 0::2], q[:, 1::2],
                               k[:, 1::2]], axis=1)
        wqk[h] = _t128(cols)
    g["a0_wqk"] = wqk
    wv = np.empty((4, 128, 2048), f32)
    for gi in range(4):
        wv[gi] = _t128(W[:, 2048 + gi * 256:2048 + (gi + 1) * 256])
    g["a0_wv"] = wv
    g["a0_wo"] = _t128(np.asarray(inp["L0_attn_Wo"], f32))
    g["a0_bo"] = np.asarray(inp["L0_attn_bo"], f32).reshape(1, C)

    W = np.asarray(inp["L0_ffn_Wqkv"], f32)
    bq = np.asarray(inp["L0_ffn_bqkv"], f32)
    wqk = np.empty((H, 128, C), f32)
    bqk = np.empty((1, H * 128), f32)
    for h in range(H):
        cols = np.concatenate([W[:, h * 64:(h + 1) * 64],
                               W[:, 1024 + h * 64:1024 + (h + 1) * 64]],
                              axis=1)
        wqk[h] = _t128(cols)
        bqk[0, h * 128:h * 128 + 64] = bq[h * 64:(h + 1) * 64]
        bqk[0, h * 128 + 64:(h + 1) * 128] = bq[1024 + h * 64:
                                                1024 + (h + 1) * 64]
    g["f0_wqk"] = wqk
    g["f0_bqk"] = bqk
    wv = np.empty((4, 128, 2048), f32)
    for gi in range(4):
        wv[gi] = _t128(W[:, 2048 + gi * 256:2048 + (gi + 1) * 256])
    g["f0_wv"] = wv
    g["f0_bv"] = np.ascontiguousarray(bq[2048:].reshape(1, 1024))
    g["f0_wo"] = _t128(np.asarray(inp["L0_ffn_Wo"], f32))
    g["f0_bo"] = np.asarray(inp["L0_ffn_bo"], f32).reshape(1, C)
    g["f0_ln1w"] = np.asarray(inp["L0_ffn_ln1_w"], f32).reshape(1, C)
    g["f0_ln1b"] = np.asarray(inp["L0_ffn_ln1_b"], f32).reshape(1, C)
    g["f0_ln2w"] = np.asarray(inp["L0_ffn_ln2_w"], f32).reshape(1, C)
    g["f0_ln2b"] = np.asarray(inp["L0_ffn_ln2_b"], f32).reshape(1, C)
    g["f0_gate"] = _t128(np.asarray(inp["L0_ffn_gate_W"], f32))
    fc1 = np.asarray(inp["L0_ffn_fc1_w"], f32)
    a = np.empty((NE, 8, 128, 4096), f32)
    for ei in range(NE):
        At = fc1[ei].T
        a[ei] = At.reshape(8, 128, 8, 512).transpose(2, 1, 0, 3).reshape(
            8, 128, 4096)
    g["f0_fc1"] = a
    g["f0_fc1b"] = np.ascontiguousarray(
        np.asarray(inp["L0_ffn_fc1_b"], f32).reshape(NE, 32, 128)
        .transpose(0, 2, 1))
    fc2 = np.asarray(inp["L0_ffn_fc2_w"], f32)
    a = np.empty((NE, 32, 128, C), f32)
    for ei in range(NE):
        a[ei] = fc2[ei].T.reshape(32, 128, C)
    g["f0_fc2"] = a
    g["f0_fc2b"] = np.asarray(inp["L0_ffn_fc2_b"], f32).reshape(NE, 1, C)

    Wq = np.asarray(inp["L1_attn_Wq"], f32)
    Wk = np.asarray(inp["L1_attn_Wk"], f32)
    Wv = np.asarray(inp["L1_attn_Wv"], f32)
    wqkv = np.empty((H, 128, CT * 192), f32)
    for h in range(H):
        cols = np.concatenate([Wq[h * 64:(h + 1) * 64, :].T,
                               Wk[h * 64:(h + 1) * 64, :].T,
                               Wv[h * 64:(h + 1) * 64, :].T], axis=1)
        wqkv[h] = _t128(cols)
    g["a1_wqkv"] = wqkv
    g["a1_wu"] = _t128(np.asarray(inp["L1_attn_Wu"], f32).T)
    lntab = np.zeros((1, 256), f32)
    lntab[0, 0:64] = np.asarray(inp["L1_attn_qln_w"], f32)
    lntab[0, 64:128] = np.asarray(inp["L1_attn_kln_w"], f32)
    lntab[0, 128:192] = np.asarray(inp["L1_attn_qln_b"], f32)
    lntab[0, 192:256] = np.asarray(inp["L1_attn_kln_b"], f32)
    g["a1_lntab"] = lntab

    qw = np.asarray(inp["L1_ffn_qweights"], np.int32)
    sc = np.asarray(inp["L1_ffn_scales"], f32)
    aq = np.empty((NMAT, 2, 128, 4096), np.int32)
    asc = np.empty((NMAT, 2, 128, 4096), f32)
    for i in range(NMAT):
        A = qw[i]
        aq[i] = A.reshape(2, 512, 8, 128).transpose(0, 3, 2, 1).reshape(
            2, 128, 4096)
        S = np.repeat(sc[i], 16, axis=1)
        asc[i] = S.reshape(2, 512, 8, 128).transpose(0, 3, 2, 1).reshape(
            2, 128, 4096)
    g["q1_qt"] = aq
    g["q1_sc"] = asc
    g["q1_b"] = np.asarray(inp["L1_ffn_biases"], f32).reshape(NMAT, 1, C)
    la = np.asarray(inp["L1_ffn_lora_a"], f32)
    a = np.empty((NMAT, 128, CT * LORA_R), f32)
    for i in range(NMAT):
        a[i] = _t128(la[i].T)
    g["q1_la"] = a
    g["q1_lb"] = np.ascontiguousarray(
        np.asarray(inp["L1_ffn_lora_b"], f32).transpose(0, 2, 1))
    g["q1_lng"] = np.asarray(inp["L1_ffn_ln_gamma"], f32).reshape(3, 1, C)
    g["q1_lnb"] = np.asarray(inp["L1_ffn_ln_beta"], f32).reshape(3, 1, C)
    return g


def kernel(**inp):
    x = np.asarray(inp["L0_attn_x"], np.float32)
    g = pack_inputs(inp)
    nc = build_program()
    in_maps = []
    for core in range(NCORES):
        m = dict(g)
        m["x"] = np.ascontiguousarray(x[core * SPC:(core + 1) * SPC])
        in_maps.append(m)
    res = run_bass_kernel_spmd(nc, in_maps, core_ids=list(range(NCORES)))
    out = np.concatenate([r["y"] for r in res.results], axis=0)
    return out.astype(np.float32)


# version 03
# speedup vs baseline: 1.0196x; 1.0196x over previous
from contextlib import ExitStack

import numpy as np
import ml_dtypes
import concourse.bass as bass
import concourse.mybir as mybir
from concourse.bass_utils import run_bass_kernel_spmd

F32 = mybir.dt.float32
BF16 = mybir.dt.bfloat16
I32 = mybir.dt.int32
AF = mybir.ActivationFunctionType
ALU = mybir.AluOpType
AX = mybir.AxisListType

B, T, C = 32, 2048, 1024
H, HD = 16, 64
NCORES = 8
SPC = B // NCORES
TT = T // 128
CT = C // 128
QCH = 256
NQC = T // QCH
FFN = 4096
NE = 8
NMAT = 6
LORA_R = 32

C_EPS = 0
C_INV1024 = 1
C_NEGHALF = 2
C_ATTSCL = 3
C_INV64 = 4
C_ONE = 5
C_NEGINV1024 = 6
C_MEAN64 = 7
C_NCONST = 8

DMA_ENGS = ("syd", "scd", "gpd")
ENGQ = {"pe": "tensor", "dve": "vector", "act": "scalar", "pool": "gpsimd",
        "syd": "sync", "scd": "scalar", "gpd": "gpsimd"}
QUEUES = ("sync", "tensor", "vector", "scalar", "gpsimd")


class Trk:
    def __init__(self):
        self.steps = []
        self.cnt = {e: 0 for e in ENGQ}
        self.clk = {e: {f: 0 for f in ENGQ} for e in ENGQ}
        self.hist = {e: [] for e in ENGQ}
        self.res = {}

    def step(self, eng, reads=(), writes=(), fn=None, tag=""):
        mycl = self.clk[eng]
        waits = {}

        def need(m):
            s2, k = m
            if s2 == eng or mycl[s2] >= k:
                return
            if waits.get(s2, 0) < k:
                waits[s2] = k
            h = self.hist[s2][k - 1]
            for s3, v3 in h.items():
                if v3 > mycl[s3]:
                    mycl[s3] = v3

        for r in reads:
            ent = self.res.get(r)
            if ent and ent[0]:
                need(ent[0])
        for r in writes:
            ent = self.res.get(r)
            if ent:
                if ent[0]:
                    need(ent[0])
                for m in ent[1]:
                    need(m)
        self.cnt[eng] += 1
        k = self.cnt[eng]
        mycl[eng] = k
        self.hist[eng].append(dict(mycl))
        wset = set(writes)
        for r in wset:
            self.res[r] = [(eng, k), []]
        for r in reads:
            if r in wset:
                continue
            ent = self.res.setdefault(r, [None, []])
            ent[1].append((eng, k))
        self.steps.append((eng, fn, sorted(waits.items()), tag))

    def emit(self, nc, ctx):
        sems = {e: ctx.enter_context(nc.semaphore(f"sem_{e}"))
                for e in ENGQ}
        by_q = {q: [] for q in QUEUES}
        for st in self.steps:
            by_q[ENGQ[st[0]]].append(st)
        with nc.Block() as blk:
            for q in QUEUES:
                steps = by_q[q]

                def mkbody(steps):
                    def body(e):
                        for (eng, fn, waits, tag) in steps:
                            for (s2, k) in waits:
                                e.wait_ge(sems[s2],
                                          k * (16 if s2 in DMA_ENGS else 1))
                            ins = fn(e)
                            ins.then_inc(sems[eng],
                                         16 if eng in DMA_ENGS else 1)
                    return body
                getattr(blk, q)(mkbody(steps))
        final = {e: self.cnt[e] for e in ENGQ}
        with nc.Block() as blk:
            def fin(e):
                for s2, k in final.items():
                    if k:
                        e.wait_ge(sems[s2], k * (16 if s2 in DMA_ENGS else 1))
            blk.sync(fin)


class P:
    def __init__(self, nc, ctx, trk, dram, dbg=()):
        self.dbg = set(dbg)
        self.nc = nc
        self.ctx = ctx
        self.t = trk
        self.dram = dram
        self.pb = [ctx.enter_context(
            nc.psum_tensor(f"pb{i}", [128, 512], F32)) for i in range(8)]
        self.sbufs = {}

    def sb(self, name, shape, dtype):
        h = self.ctx.enter_context(
            self.nc.sbuf_tensor("sb_" + name, shape, dtype))
        self.sbufs[name] = h[:]
        return self.sbufs[name]

    def view(self, name, arena_ap, col_off_bytes, parts, cols, dtype):
        base = arena_ap.bitcast(dtype)
        esz = 2 if dtype == BF16 else 4
        c0 = col_off_bytes // esz
        self.sbufs[name] = base[0:parts, c0:c0 + cols]
        return self.sbufs[name]

    def tap(self, name, src_ap, reads):
        if name not in self.dbg:
            return
        d = self.dram[name]
        self.t.step("scd", tuple(reads), (),
                    lambda e: e.dma_start(d, src_ap))

    def fence(self, names):
        self.t.step("pool", (), tuple(names), lambda e: e.engine_nop(),
                    "fence")


def nxc(ct, tcol):
    return ct * T + tcol


WT_NAMES = ("wqk0", "wqk1", "wv0", "wv1", "cos2", "sin2", "wo_sb",
            "fc1w0", "fc1w1", "fc2w0", "fc2w1", "gate_sb", "gate_bf",
            "qt_sb", "sc_sb", "wdq0", "wdq1", "sT")
ATT_NAMES = ("attT", "attT0", "attT1", "attT2", "attT3") + tuple(
    f"moe_hT{i}" for i in range(32))
SCR_NAMES = ("w_xin0", "w_xin1", "w_nx0", "w_nx1", "uacc0", "uacc1",
             "qkall", "qknorm", "l1sub",
             "w_z0o0", "w_z0o1", "w_z1o0", "w_z1o1", "w_out0", "w_out1",
             "w_tmp", "w_tmp2", "w_sqscr",
             "st_sb") + tuple(f"stq{i}" for i in range(16)) + (
             "qb", "kb", "qkc", "rot", "ptb0", "ptb1",
             "att2_0", "att2_1", "bmx0", "bmx1")


def emit_init(p):
    t, d = p.t, p.dram
    t.step("syd", (), ("cpool",), lambda e: e.dma_start(
        p.sbufs["cpool"], d["consts"].partition_broadcast(128).squeeze(1)))
    t.step("syd", (), ("identf",),
           lambda e: e.dma_start(p.sbufs["identf"], d["identf"]))
    t.step("syd", (), ("mask_sb",), lambda e: e.dma_start(
        p.sbufs["mask_sb"].rearrange("p (i q) -> p i q", q=QCH),
        d["masks"].transpose([1, 0, 2])))
    t.step("pool", (), ("onesr",),
           lambda e: e.memset(p.sbufs["onesr"], 1.0))
    t.step("dve", ("identf",), ("identb",),
           lambda e: e.tensor_copy(p.sbufs["identb"], p.sbufs["identf"]))


def emit_phase0_cast(p):
    t, d = p.t, p.dram
    for ei in range(NE):
        for fc in range(FFN // 512):
            t.step("gpd", (), (f"fc1bf_{ei}",),
                   (lambda ei, fc: lambda e: e.dma_start(
                       d["f0_fc1_bf"][ei, fc], d["f0_fc1"][ei, fc]))(ei, fc))
        for f8 in range(4):
            t.step("gpd", (), (f"fc2bf_{ei}",),
                   (lambda ei, f8: lambda e: e.dma_start(
                       d["f0_fc2_bf"][ei, f8 * 8:(f8 + 1) * 8],
                       d["f0_fc2"][ei, f8 * 8:(f8 + 1) * 8]))(ei, f8))


def ln_steps(p, src_ap, src_reads, sc, out_fn):
    t = p.t
    stats = p.sbufs["w_st"]
    cpool = p.sbufs["cpool"]
    W = src_ap.shape[-1]
    sq = p.sbufs["w_sq"]

    def f_mu(e):
        return e.reduce_sum(stats[:, sc:sc + 1], src_ap, axis=AX.X)
    t.step("dve", tuple(src_reads), (f"w_st{sc}",), f_mu)

    def f_neg(e):
        return e.tensor_scalar(stats[:, sc + 1:sc + 2], stats[:, sc:sc + 1],
                               -1.0 / W, None, ALU.mult)
    t.step("pool", (f"w_st{sc}",), (f"w_st{sc}n",), f_neg)

    def f_sq(e):
        e.activation(sq[:, 0:W], src_ap, AF.Square,
                     bias=stats[:, sc + 1:sc + 2], scale=1.0,
                     accum_out=stats[:, sc + 2:sc + 3])
        e.activation(stats[:, sc + 3:sc + 4], stats[:, sc + 2:sc + 3],
                     AF.Ln, bias=cpool[:, C_EPS:C_EPS + 1],
                     scale=cpool[:, C_INV1024:C_INV1024 + 1])
        return e.activation(stats[:, sc + 3:sc + 4], stats[:, sc + 3:sc + 4],
                            AF.Exp, bias=0.0,
                            scale=cpool[:, C_NEGHALF:C_NEGHALF + 1])
    t.step("act", tuple(src_reads) + (f"w_st{sc}", f"w_st{sc}n", "cpool"),
           (f"w_st{sc}b", "w_sqscr"), f_sq)
    out_fn(stats[:, sc + 1:sc + 2], stats[:, sc + 3:sc + 4],
           (f"w_st{sc}", f"w_st{sc}b"))


def emit_u_phase(p, src, ud, src_res):
    t = p.t
    xin = p.sbufs["w_xin"]
    out = p.sbufs["w_out"]
    for tt in range(TT):
        sl = tt % 2
        t.step("scd", (f"{src_res}{tt}",), (f"w_xin{sl}",),
               (lambda tt, sl: lambda e: e.dma_start(
                   xin[:, sl * 1024:(sl + 1) * 1024],
                   src[tt * 128:(tt + 1) * 128, :]))(tt, sl))
        src_ap = xin[:, sl * 1024:(sl + 1) * 1024]

        def mko(tt=tt, sl=sl, src_ap=src_ap):
            def out_fn(negmu, rstd, streads):
                def f(e):
                    return e.tensor_scalar(
                        out[:, sl * 1024:(sl + 1) * 1024], src_ap,
                        negmu, rstd, ALU.add, ALU.mult)
                t.step("dve", (f"w_xin{sl}",) + streads, (f"w_out{sl}",), f)
                t.step("scd", (f"w_out{sl}",), (f"ud{tt}",),
                       (lambda tt, sl: lambda e: e.dma_start(
                           ud[tt * 128:(tt + 1) * 128, :],
                           out[:, sl * 1024:(sl + 1) * 1024]))(tt, sl))
            return out_fn
        ln_steps(p, src_ap, (f"w_xin{sl}",), (tt % 4) * 4, mko())


def emit_transposes_tile(p, tt, src_ap, src_res, dstT_name, dst_res=None):
    t = p.t
    identb = p.sbufs["identb"]
    dstT = p.sbufs[dstT_name]
    if dst_res is None:
        dst_res = f"{dstT_name}{tt // 4}"
    pbk = 6 + (tt % 2)
    pt = p.pb[pbk].ap().bitcast(BF16)

    def f_tr(e):
        ins = None
        for ct in range(CT):
            ins = e.transpose(pt[:, ct * 128:(ct + 1) * 128],
                              src_ap[:, ct * 128:(ct + 1) * 128], identb[:])
        return ins
    t.step("pe", (src_res, "identb"), (f"pb{pbk}",), f_tr)

    def f_ev(e):
        ins = None
        for half in range(2):
            outap = dstT.rearrange("p (ct tx) -> p ct tx", tx=T)[
                :, half * 4:(half + 1) * 4, tt * 128:(tt + 1) * 128]
            ins = e.tensor_copy(
                outap, pt[:, half * 512:(half + 1) * 512].rearrange(
                    "p (ct tx) -> p ct tx", tx=128))
        return ins
    t.step("dve", (f"pb{pbk}",), (dst_res,), f_ev)


def emit_nx_phase(p, nxT_name, src_dram, src_res_base, gamma=None, beta=None):
    t = p.t
    nxbf = p.sbufs["w_nx"]
    xin = p.sbufs["w_xin"]
    for tt in range(TT):
        sl = tt % 2
        t.step("scd", (f"{src_res_base}{tt}",), (f"w_xin{sl}",),
               (lambda tt, sl: lambda e: e.dma_start(
                   xin[:, sl * 1024:(sl + 1) * 1024],
                   src_dram[tt * 128:(tt + 1) * 128, :]))(tt, sl))
        src_ap = xin[:, sl * 1024:(sl + 1) * 1024]

        def mko(tt=tt, sl=sl, src_ap=src_ap):
            def out_fn(negmu, rstd, streads):
                if gamma is None:
                    def f(e):
                        return e.tensor_scalar(
                            nxbf[:, sl * 1024:(sl + 1) * 1024], src_ap,
                            negmu, rstd, ALU.add, ALU.mult)
                    t.step("dve", (f"w_xin{sl}",) + streads,
                           (f"w_nx{sl}",), f)
                else:
                    g_tab = p.sbufs[gamma]
                    b_tab = p.sbufs[beta]
                    tmp = p.sbufs["w_tmp"]

                    def f(e):
                        e.tensor_scalar(tmp[:, 0:1024], src_ap, negmu, rstd,
                                        ALU.add, ALU.mult)
                        e.tensor_mul(tmp[:, 0:1024], tmp[:, 0:1024], g_tab)
                        return e.tensor_add(
                            nxbf[:, sl * 1024:(sl + 1) * 1024],
                            tmp[:, 0:1024], b_tab)
                    t.step("dve", (f"w_xin{sl}",) + streads + (gamma, beta),
                           (f"w_nx{sl}", "w_tmp"), f)
            return out_fn
        ln_steps(p, src_ap, (f"w_xin{sl}",), (tt % 4) * 4, mko())
        emit_transposes_tile(p, tt, p.sbufs["w_nx"][:, (tt % 2) * 1024:
                                                    (tt % 2 + 1) * 1024],
                             f"w_nx{tt % 2}", nxT_name)


def emit_attn_heads(p, wp, mode, nxT_name):
    t, d = p.t, p.dram
    nxT = p.sbufs[nxT_name]
    attT = p.sbufs["attT"]
    identb = p.sbufs["identb"]
    identf = p.sbufs["identf"]
    onesr = p.sbufs["onesr"]
    cpool = p.sbufs["cpool"]
    mask_sb = p.sbufs["mask_sb"]
    qb, kb = p.sbufs["qb"], p.sbufs["kb"]
    vgrp = p.sbufs["vgrp"]
    st_sb = p.sbufs["st_sb"]
    bmx = p.sbufs["bmx"]
    chain = p.sbufs["chain"]
    chain_bf = p.sbufs["chain_bf"]
    ptb = p.sbufs["ptb"]
    att2 = p.sbufs["att2"]
    wqk_sb = p.sbufs["wqk_sb"]
    wv_sb = p.sbufs["wv_sb"]
    bias = (mode == "plain")

    if mode == "rope":
        t.step("syd", (), ("cos2",), lambda e: e.dma_start(
            p.sbufs["cos2"], d["ropec"]))
        t.step("syd", (), ("sin2",), lambda e: e.dma_start(
            p.sbufs["sin2"], d["ropes"]))

    def f_ones(e):
        ins = None
        for tt in range(TT):
            for j in range(4):
                ins = e.memset(vgrp[:, tt * 260 + j * 65 + 64:
                                    tt * 260 + j * 65 + 65], 1.0)
        return ins
    t.step("pool", (), ("vgrp", "vgrp_ones"), f_ones)

    for h in range(H):
        wslot = h % 2
        if mode in ("rope", "plain"):
            t.step("gpd", (), (f"wqk{wslot}",),
                   (lambda h, wslot: lambda e: e.dma_start(
                       wqk_sb[:, wslot * 1024:(wslot + 1) * 1024],
                       d[wp + "_wqk"][h]))(h, wslot))
            if h % 4 == 0:
                g = h // 4
                gs = g % 2
                t.step("gpd", (), (f"wv{gs}",),
                       (lambda g, gs: lambda e: e.dma_start(
                           wv_sb[:, gs * 2048:(gs + 1) * 2048],
                           d[wp + "_wv"][g]))(g, gs))
            for chunk in range(4):
                pbk = chunk % 2

                def f_qk(e, h=h, chunk=chunk, pbk=pbk, wslot=wslot):
                    ins = None
                    for ct in range(CT):
                        ins = e.matmul(
                            p.pb[pbk].ap()[:, 0:512],
                            wqk_sb[:, wslot * 1024 + ct * 128:
                                   wslot * 1024 + (ct + 1) * 128],
                            nxT[:, nxc(ct, chunk * 512):
                                nxc(ct, (chunk + 1) * 512)],
                            start=(ct == 0), stop=(ct == CT - 1 and not bias))
                    if bias:
                        ins = e.matmul(
                            p.pb[pbk].ap()[:, 0:512],
                            p.sbufs["bqk_sb"][0:1, h * 128:(h + 1) * 128],
                            onesr[0:1, 0:512], start=False, stop=True)
                    return ins
                rd = (f"wqk{wslot}", f"{nxT_name}{chunk}", "onesr")
                if bias:
                    rd = rd + ("bqk_sb",)
                t.step("pe", rd, (f"pb{pbk}",), f_qk)
                if mode == "rope":
                    qkc = p.sbufs["qkc"]
                    rot = p.sbufs["rot"]

                    def f_rope(e, chunk=chunk, pbk=pbk):
                        cs = slice(chunk * 512, (chunk + 1) * 512)
                        e.tensor_copy(qkc[:, 0:512], p.pb[pbk].ap()[:, 0:512])
                        e.tensor_scalar(rot[0:64, 0:512], qkc[64:128, 0:512],
                                        -1.0, None, ALU.mult)
                        e.tensor_copy(rot[64:128, 0:512], qkc[0:64, 0:512])
                        e.tensor_mul(qkc[:, 0:512], qkc[:, 0:512],
                                     p.sbufs["cos2"][:, cs])
                        e.tensor_mul(rot[:, 0:512], rot[:, 0:512],
                                     p.sbufs["sin2"][:, cs])
                        e.tensor_add(qkc[:, 0:512], qkc[:, 0:512],
                                     rot[:, 0:512])
                        e.tensor_copy(qb[0:32, cs], qkc[0:32, 0:512])
                        e.tensor_copy(kb[0:32, cs], qkc[32:64, 0:512])
                        e.tensor_copy(qb[32:64, cs], qkc[64:96, 0:512])
                        return e.tensor_copy(kb[32:64, cs],
                                             qkc[96:128, 0:512])
                    t.step("dve", (f"pb{pbk}", "cos2", "sin2"),
                           ("qb", "kb", "qkc", "rot"), f_rope)
                else:
                    def f_ev(e, chunk=chunk, pbk=pbk):
                        cs = slice(chunk * 512, (chunk + 1) * 512)
                        e.tensor_copy(qb[:, cs], p.pb[pbk].ap()[0:64, 0:512])
                        return e.tensor_copy(kb[:, cs],
                                             p.pb[pbk].ap()[64:128, 0:512])
                    t.step("dve", (f"pb{pbk}",), ("qb", "kb"), f_ev)
            if h == 0 and mode == "rope":
                p.tap("dbg_qb", p.sbufs["qb"], ("qb",))
                p.tap("dbg_kb", p.sbufs["kb"], ("kb",))
            if h % 4 == 0:
                g = h // 4
                gs = g % 2
                for tt in range(TT):
                    def f_v(e, gs=gs, tt=tt, g=g):
                        ins = None
                        for ct in range(CT):
                            ins = e.matmul(
                                p.pb[2].ap()[:, 0:256],
                                nxT[:, nxc(ct, tt * 128):
                                    nxc(ct, (tt + 1) * 128)],
                                wv_sb[:, gs * 2048 + ct * 256:
                                      gs * 2048 + (ct + 1) * 256],
                                start=(ct == 0),
                                stop=(ct == CT - 1 and not bias))
                        if bias:
                            ins = e.matmul(
                                p.pb[2].ap()[:, 0:256], onesr[0:1, 0:128],
                                p.sbufs["bv_sb"][0:1, g * 256:(g + 1) * 256],
                                start=False, stop=True)
                        return ins
                    rd = (f"wv{gs}", f"{nxT_name}{tt // 4}", "onesr")
                    if bias:
                        rd = rd + ("bv_sb",)
                    t.step("pe", rd, ("pb2",), f_v)

                    def f_vev(e, tt=tt):
                        return e.tensor_copy(
                            vgrp[:, tt * 260:tt * 260 + 260].rearrange(
                                "p (j dd) -> p j dd", dd=65)[:, :, 0:64],
                            p.pb[2].ap()[:, 0:256].rearrange(
                                "p (j dd) -> p j dd", dd=64))
                    t.step("dve", ("pb2", "vgrp_ones"), ("vgrp",), f_vev)
        else:
            qkall = p.sbufs["qkall"]
            qkn = p.sbufs["qknorm"]
            lntab = p.sbufs["l1lntab"]
            t.step("gpd", (), (f"wqk{wslot}",),
                   (lambda h, wslot: lambda e: e.dma_start(
                       wqk_sb[:, wslot * 1536:(wslot + 1) * 1536],
                       d["a1_wqkv"][h]))(h, wslot))
            for tt in range(TT):
                pbk = 3 + (tt % 2)

                def f_qkv(e, tt=tt, pbk=pbk, wslot=wslot):
                    ins = None
                    for ct in range(CT):
                        ins = e.matmul(
                            p.pb[pbk].ap()[:, 0:192],
                            nxT[:, nxc(ct, tt * 128):nxc(ct, (tt + 1) * 128)],
                            wqk_sb[:, wslot * 1536 + ct * 192:
                                   wslot * 1536 + (ct + 1) * 192],
                            start=(ct == 0), stop=(ct == CT - 1))
                    return ins
                t.step("pe", (f"wqk{wslot}", f"{nxT_name}{tt // 4}"),
                       (f"pb{pbk}",), f_qkv)

                def f_ev(e, tt=tt, pbk=pbk, h=h):
                    e.tensor_copy(qkall[:, tt * 128:(tt + 1) * 128],
                                  p.pb[pbk].ap()[:, 0:128])
                    return e.tensor_copy(
                        vgrp[:, tt * 260 + (h % 4) * 65:
                             tt * 260 + (h % 4) * 65 + 64],
                        p.pb[pbk].ap()[:, 128:192])
                t.step("dve", (f"pb{pbk}", "vgrp_ones"), ("qkall", "vgrp"),
                       f_ev)

            def f_mu(e):
                return e.reduce_sum(
                    p.sbufs["l1mu"][:, 0:32],
                    qkall.rearrange("p (g dd) -> p g dd", dd=64), axis=AX.X)
            t.step("dve", ("qkall",), ("l1mu",), f_mu)

            def f_mul(e):
                return e.activation(p.sbufs["l1mu"][:, 32:64],
                                    p.sbufs["l1mu"][:, 0:32], AF.Copy,
                                    bias=0.0,
                                    scale=cpool[:, C_MEAN64:C_MEAN64 + 1])
            t.step("act", ("l1mu", "cpool"), ("l1mu2",), f_mul)

            def f_sub(e):
                mub = p.sbufs["l1mu"][:, 32:64].unsqueeze(2).broadcast_to(
                    (128, 32, 64))
                e.tensor_tensor(
                    p.sbufs["l1sub"].rearrange("p (g dd) -> p g dd", dd=64),
                    qkall.rearrange("p (g dd) -> p g dd", dd=64),
                    mub, ALU.subtract)
                e.tensor_mul(p.sbufs["w_sq"][:, 0:1024],
                             p.sbufs["l1sub"][:, 0:1024],
                             p.sbufs["l1sub"][:, 0:1024])
                e.tensor_mul(p.sbufs["w_tmp"][:, 0:1024],
                             p.sbufs["l1sub"][:, 1024:2048],
                             p.sbufs["l1sub"][:, 1024:2048])
                e.reduce_sum(
                    p.sbufs["l1var"][:, 0:16],
                    p.sbufs["w_sq"][:, 0:1024].rearrange(
                        "p (g dd) -> p g dd", dd=64), axis=AX.X)
                return e.reduce_sum(
                    p.sbufs["l1var"][:, 16:32],
                    p.sbufs["w_tmp"][:, 0:1024].rearrange(
                        "p (g dd) -> p g dd", dd=64), axis=AX.X)
            t.step("dve", ("qkall", "l1mu2"),
                   ("l1sub", "l1var", "w_sqscr", "w_tmp"), f_sub)

            def f_rstd(e):
                e.activation(p.sbufs["l1var"][:, 0:32],
                             p.sbufs["l1var"][:, 0:32], AF.Ln,
                             bias=cpool[:, C_EPS:C_EPS + 1],
                             scale=cpool[:, C_INV64:C_INV64 + 1])
                return e.activation(p.sbufs["l1var"][:, 0:32],
                                    p.sbufs["l1var"][:, 0:32], AF.Exp,
                                    bias=0.0,
                                    scale=cpool[:, C_NEGHALF:C_NEGHALF + 1])
            t.step("act", ("l1var", "cpool"), ("l1var",), f_rstd)

            def f_norm(e):
                rb = p.sbufs["l1var"][:, 0:32].unsqueeze(2).broadcast_to(
                    (128, 32, 64))
                e.tensor_tensor(
                    p.sbufs["l1sub"].rearrange("p (g dd) -> p g dd", dd=64),
                    p.sbufs["l1sub"].rearrange("p (g dd) -> p g dd", dd=64),
                    rb, ALU.mult)
                gtab = lntab[:, 0:128].unsqueeze(1).broadcast_to(
                    (128, 16, 128))
                e.tensor_tensor(
                    p.sbufs["l1sub"].rearrange("p (g dd) -> p g dd", dd=128),
                    p.sbufs["l1sub"].rearrange("p (g dd) -> p g dd", dd=128),
                    gtab, ALU.mult)
                btab = lntab[:, 128:256].unsqueeze(1).broadcast_to(
                    (128, 16, 128))
                return e.tensor_tensor(
                    qkn.rearrange("p (g dd) -> p g dd", dd=128),
                    p.sbufs["l1sub"].rearrange("p (g dd) -> p g dd", dd=128),
                    btab, ALU.add)
            t.step("dve", ("l1sub", "l1var", "l1lntab"), ("qknorm", "l1sub"),
                   f_norm)
            for tt in range(TT):
                pbk = 6 + (tt % 2)
                ptq = p.pb[pbk].ap().bitcast(BF16)

                def f_tr(e, tt=tt, ptq=ptq):
                    return e.transpose(ptq[:, 0:128],
                                       qkn[:, tt * 128:(tt + 1) * 128],
                                       identb[:])
                t.step("pe", ("qknorm", "identb"), (f"pb{pbk}",), f_tr)

                def f_cp(e, tt=tt, ptq=ptq):
                    e.tensor_copy(qb[:, tt * 128:(tt + 1) * 128],
                                  ptq[0:64, 0:128])
                    return e.tensor_copy(kb[:, tt * 128:(tt + 1) * 128],
                                         ptq[64:128, 0:128])
                t.step("dve", (f"pb{pbk}",), ("qb", "kb"), f_cp)

        for qc in range(NQC):
            nbl = 2 * qc + 2
            cp = qc % 2
            ob = 5 if cp == 0 else 2
            for i in range(nbl):
                pbk = 3 + (i % 2)

                def f_s(e, i=i, qc=qc, pbk=pbk):
                    return e.matmul(p.pb[pbk].ap()[:, 0:QCH],
                                    kb[:, i * 128:(i + 1) * 128],
                                    qb[:, qc * QCH:(qc + 1) * QCH],
                                    start=True, stop=True)
                t.step("pe", ("qb", "kb"), (f"pb{pbk}",), f_s)

                def f_sev(e, i=i, pbk=pbk):
                    return e.tensor_copy(st_sb[:, i * QCH:(i + 1) * QCH],
                                         p.pb[pbk].ap()[:, 0:QCH])
                t.step("dve", (f"pb{pbk}",), (f"stq{i}",), f_sev)

                def f_bm(e, i=i, cp=cp):
                    return e.reduce_max(bmx[:, cp * 16 + i:cp * 16 + i + 1],
                                        st_sb[:, i * QCH:(i + 1) * QCH],
                                        axis=AX.X)
                t.step("dve", (f"stq{i}",), (f"bmx{cp}",), f_bm)

            t.step("act", (f"bmx{cp}",), (f"bmx2_{cp}",),
                   (lambda nbl, cp: lambda e: e.activation(
                       p.sbufs["bmx2"][:, cp * 16:cp * 16 + nbl],
                       bmx[:, cp * 16:cp * 16 + nbl], AF.Copy,
                       bias=0.0, scale=1.0))(nbl, cp))
            t.step("dve", (f"bmx2_{cp}",), (f"chcm{cp}",),
                   (lambda nbl, cp: lambda e: e.reduce_max(
                       chain[:, cp:cp + 1],
                       p.sbufs["bmx2"][:, cp * 16:cp * 16 + nbl],
                       axis=AX.X))(nbl, cp))
            t.step("pe", (f"chcm{cp}", "identf"), (f"pb{ob}",),
                   (lambda cp, ob: lambda e: e.transpose(
                       p.pb[ob].ap()[0:1, 256:384], chain[:, cp:cp + 1],
                       identf[:]))(cp, ob))

            t.step("dve", (f"pb{ob}",), (f"chred{cp}",),
                   (lambda cp, ob: lambda e: e.reduce_max(
                       chain[0:1, 2 + cp:3 + cp],
                       p.pb[ob].ap()[0:1, 256:384],
                       axis=AX.X, negate=True))(cp, ob))

            t.step("act", (f"chred{cp}", "cpool"), (f"chbf{cp}",),
                   (lambda cp: lambda e: e.activation(
                       chain_bf[0:1, cp:cp + 1], chain[0:1, 2 + cp:3 + cp],
                       AF.Copy, bias=0.0,
                       scale=cpool[0:1, C_ATTSCL:C_ATTSCL + 1]))(cp))

            t.step("pe", (f"chbf{cp}", "onesr"), (f"pb{ob}",),
                   (lambda cp, ob: lambda e: e.matmul(
                       p.pb[ob].ap()[:, 440:441], onesr[0:1, 0:128],
                       chain_bf[0:1, cp:cp + 1],
                       start=True, stop=True))(cp, ob))
            t.step("dve", (f"pb{ob}",), (f"chainM{cp}",),
                   (lambda cp, ob: lambda e: e.tensor_copy(
                       chain[:, 4 + cp:5 + cp],
                       p.pb[ob].ap()[:, 440:441]))(cp, ob))

            first_av = [True]
            for i in range(nbl):
                psl = i % 2

                def f_exp(e, i=i, psl=psl, cp=cp):
                    return e.activation(ptb[:, psl * QCH:(psl + 1) * QCH],
                                        st_sb[:, i * QCH:(i + 1) * QCH],
                                        AF.Exp, bias=chain[:, 4 + cp:5 + cp],
                                        scale=cpool[:, C_ATTSCL:C_ATTSCL + 1])
                t.step("act", (f"stq{i}", f"chainM{cp}", "cpool"),
                       (f"ptb{psl}",), f_exp)
                if i >= nbl - 2:
                    mi = 0 if i == nbl - 2 else 1

                    def f_msk(e, psl=psl, mi=mi):
                        return e.tensor_mul(
                            ptb[:, psl * QCH:(psl + 1) * QCH],
                            ptb[:, psl * QCH:(psl + 1) * QCH],
                            mask_sb[:, mi * QCH:(mi + 1) * QCH])
                    t.step("dve", ("mask_sb", f"ptb{psl}"), (f"ptb{psl}",),
                           f_msk)

                def f_av(e, i=i, psl=psl, nbl=nbl, h=h, ob=ob):
                    ins = None
                    st0 = first_av[0]
                    for qs in range(2):
                        if i == nbl - 1 and qs == 0:
                            continue
                        ins = e.matmul(
                            p.pb[ob].ap()[:, qs * 65:qs * 65 + 65],
                            ptb[:, psl * QCH + qs * 128:
                                psl * QCH + (qs + 1) * 128],
                            vgrp[:, i * 260 + (h % 4) * 65:
                                 i * 260 + (h % 4) * 65 + 65],
                            start=st0 and (qs == (0 if i < nbl - 1 else 1)),
                            stop=(i == nbl - 1 and qs == 1))
                    first_av[0] = False
                    return ins
                t.step("pe", (f"ptb{psl}", "vgrp"), (f"pb{ob}",), f_av)

            for qs in range(2):
                def f_rec(e, qs=qs, cp=cp, ob=ob):
                    e.activation(chain[:, 6 + 2 * cp + qs:7 + 2 * cp + qs],
                                 p.pb[ob].ap()[:, qs * 65 + 64:qs * 65 + 65],
                                 AF.Ln, bias=0.0, scale=1.0)
                    return e.activation(
                        chain[:, 6 + 2 * cp + qs:7 + 2 * cp + qs],
                        chain[:, 6 + 2 * cp + qs:7 + 2 * cp + qs], AF.Exp,
                        bias=0.0, scale=-1.0)
                t.step("act", (f"pb{ob}",), (f"rec{cp}{qs}",), f_rec)

                def f_on(e, qs=qs, cp=cp, ob=ob):
                    return e.tensor_scalar(
                        att2[:, qs * 64:(qs + 1) * 64],
                        p.pb[ob].ap()[:, qs * 65:qs * 65 + 64],
                        chain[:, 6 + 2 * cp + qs:7 + 2 * cp + qs],
                        None, ALU.mult)
                t.step("dve", (f"pb{ob}", f"rec{cp}{qs}"), (f"att2_{qs}",),
                       f_on)

                pbk = 6 + qs
                pta = p.pb[pbk].ap().bitcast(BF16)

                def f_atr(e, qs=qs, pta=pta):
                    return e.transpose(pta[0:64, 0:128],
                                       att2[:, qs * 64:(qs + 1) * 64],
                                       identb[:])
                t.step("pe", (f"att2_{qs}", "identb"), (f"pb{pbk}",), f_atr)

                def f_acp(e, qs=qs, qc=qc, h=h, pta=pta):
                    tt = qc * 2 + qs
                    pbase = (h % 2) * 64
                    return e.tensor_copy(
                        attT[pbase:pbase + 64,
                             (h // 2) * T + tt * 128:
                             (h // 2) * T + (tt + 1) * 128], pta[0:64, 0:128])
                t.step("dve", (f"pb{pbk}",), ("attT",), f_acp)


def emit_proj(p, wo_name, bo_name, epilogue, src_dram=None, dst_dram=None,
              ud=None, src_res=None, dst_res=None):
    t = p.t
    attT = p.sbufs["attT"]
    wo_sb = p.sbufs[wo_name]
    bo_sb = p.sbufs[bo_name] if bo_name else None
    onesr = p.sbufs["onesr"]
    uacc = p.sbufs["uacc"]
    zsb = p.sbufs["w_z"]

    for tt in range(TT):
        usl = tt % 2
        if epilogue == "accum_u":
            t.step("scd", (f"ud{tt}",), (f"uacc{usl}",),
                   (lambda tt, usl: lambda e: e.dma_start(
                       uacc[:, usl * 1024:(usl + 1) * 1024],
                       ud[tt * 128:(tt + 1) * 128, :]))(tt, usl))
        for oc in range(2):
            pbk = oc

            def f_p(e, tt=tt, oc=oc, pbk=pbk):
                ins = None
                for ct in range(CT):
                    ins = e.matmul(
                        p.pb[pbk].ap()[:, 0:512],
                        attT[:, ct * T + tt * 128:ct * T + (tt + 1) * 128],
                        wo_sb[:, ct * 1024 + oc * 512:
                              ct * 1024 + (oc + 1) * 512],
                        start=(ct == 0),
                        stop=(ct == CT - 1 and bo_sb is None))
                if bo_sb is not None:
                    ins = e.matmul(p.pb[pbk].ap()[:, 0:512], onesr[0:1, 0:128],
                                   bo_sb[0:1, oc * 512:(oc + 1) * 512],
                                   start=False, stop=True)
                return ins
            rd = ("attT", wo_name, "onesr")
            if bo_name:
                rd = rd + (bo_name,)
            t.step("pe", rd, (f"pb{pbk}",), f_p)
            if epilogue == "accum_u":
                def f_a(e, oc=oc, pbk=pbk, usl=usl):
                    sl = slice(usl * 1024 + oc * 512,
                               usl * 1024 + (oc + 1) * 512)
                    return e.tensor_add(uacc[:, sl], uacc[:, sl],
                                        p.pb[pbk].ap()[:, 0:512])
                t.step("dve", (f"pb{pbk}", f"uacc{usl}"), (f"uacc{usl}",),
                       f_a)
            else:
                def f_z(e, tt=tt, oc=oc, pbk=pbk):
                    return e.tensor_copy(
                        zsb[:, (tt % 2) * 1024 + oc * 512:
                            (tt % 2) * 1024 + (oc + 1) * 512],
                        p.pb[pbk].ap()[:, 0:512])
                t.step("dve", (f"pb{pbk}",), (f"w_z{tt % 2}o{oc}",), f_z)
        if epilogue == "accum_u":
            t.step("scd", (f"uacc{usl}",), (f"ud{tt}",),
                   (lambda tt, usl: lambda e: e.dma_start(
                       ud[tt * 128:(tt + 1) * 128, :],
                       uacc[:, usl * 1024:(usl + 1) * 1024]))(tt, usl))
        else:
            emit_sandwich_tile(
                p, tt, zsb[:, (tt % 2) * 1024:(tt % 2 + 1) * 1024],
                (f"w_z{tt % 2}o0", f"w_z{tt % 2}o1"), src_dram, dst_dram,
                src_res, dst_res)


def emit_sandwich_tile(p, tt, z_ap, z_reads, src_dram, dst_dram,
                       src_res, dst_res):
    t = p.t
    xin = p.sbufs["w_xin"]
    out_sb = p.sbufs["w_out"]
    sl = tt % 2

    t.step("scd", (f"{src_res}{tt}",), (f"w_xin{sl}",),
           (lambda tt, sl: lambda e: e.dma_start(
               xin[:, sl * 1024:(sl + 1) * 1024],
               src_dram[tt * 128:(tt + 1) * 128, :]))(tt, sl))

    def out_fn(negmu, rstd, streads):
        def f(e):
            e.tensor_scalar(out_sb[:, sl * 1024:(sl + 1) * 1024], z_ap,
                            negmu, rstd, ALU.add, ALU.mult)
            return e.tensor_add(out_sb[:, sl * 1024:(sl + 1) * 1024],
                                out_sb[:, sl * 1024:(sl + 1) * 1024],
                                xin[:, sl * 1024:(sl + 1) * 1024])
        t.step("dve", tuple(z_reads) + streads + (f"w_xin{sl}",),
               (f"w_out{sl}",), f)
        t.step("scd", (f"w_out{sl}",), (f"{dst_res}{tt}",),
               (lambda tt, sl: lambda e: e.dma_start(
                   dst_dram[tt * 128:(tt + 1) * 128, :],
                   out_sb[:, sl * 1024:(sl + 1) * 1024]))(tt, sl))
    ln_steps(p, z_ap, tuple(z_reads), (tt % 4) * 4, out_fn)


def emit_u_sandwich(p, src, dst, ud, src_res, dst_res):
    zsb = p.sbufs["w_z"]
    t = p.t
    for tt in range(TT):
        zl = tt % 2
        t.step("scd", (f"ud{tt}",), (f"w_z{zl}o0", f"w_z{zl}o1"),
               (lambda tt, zl: lambda e: e.dma_start(
                   zsb[:, zl * 1024:(zl + 1) * 1024],
                   ud[tt * 128:(tt + 1) * 128, :]))(tt, zl))
        emit_sandwich_tile(p, tt, zsb[:, zl * 1024:(zl + 1) * 1024],
                           (f"w_z{zl}o0", f"w_z{zl}o1"), src, dst,
                           src_res, dst_res)


def emit_moe(p, nxT_name, ud):
    t, d = p.t, p.dram
    nxT = p.sbufs[nxT_name]
    onesr = p.sbufs["onesr"]
    cpool = p.sbufs["cpool"]
    gl = p.sbufs["moe_gl"]
    gw = p.sbufs["moe_gw"]
    gate_sb = p.sbufs["gate_sb"]
    gate_bf = p.sbufs["gate_bf"]
    fc1sb = p.sbufs["fc1sb"]
    fc2sb = p.sbufs["fc2sb"]
    fc1b_sb = p.sbufs["fc1b_sb"]
    fc2b_sb = p.sbufs["fc2b_sb"]
    fc2b_bf = p.sbufs["fc2b_bf"]
    hT = p.sbufs["moe_hT"]
    uacc = p.sbufs["uacc"]

    t.step("syd", (), ("gate_sb",),
           lambda e: e.dma_start(gate_sb, d["f0_gate"]))
    t.step("dve", ("gate_sb",), ("gate_bf",),
           lambda e: e.tensor_copy(gate_bf, gate_sb))

    for tt in range(TT):
        def f_g(e, tt=tt):
            ins = None
            for ct in range(CT):
                ins = e.matmul(
                    p.pb[2].ap()[:, tt * 8:(tt + 1) * 8],
                    nxT[:, nxc(ct, tt * 128):nxc(ct, (tt + 1) * 128)],
                    gate_bf[:, ct * 8:(ct + 1) * 8],
                    start=(ct == 0 and tt == 0),
                    stop=(ct == CT - 1 and tt == TT - 1))
            return ins
        t.step("pe", (f"{nxT_name}{tt // 4}", "gate_bf"), ("pb2",), f_g)
    t.step("dve", ("pb2",), ("moe_gl",),
           lambda e: e.tensor_copy(gl[:, 0:128], p.pb[2].ap()[:, 0:128]))

    def f_m1(e):
        return e.reduce_max(gw[:, 0:16],
                            gl[:, 0:128].rearrange("p (g k) -> p g k", k=8),
                            axis=AX.X)
    t.step("dve", ("moe_gl",), ("moe_gw",), f_m1)
    t.step("act", ("moe_gw",), ("moe_m1",),
           lambda e: e.activation(gw[:, 16:32], gw[:, 0:16], AF.Copy,
                                  bias=0.0, scale=1.0))

    def f_eq1(e):
        m1b = gw[:, 16:32].unsqueeze(2).broadcast_to((128, 16, 8))
        e.tensor_tensor(gw[:, 128:256].rearrange("p (g k) -> p g k", k=8),
                        gl[:, 0:128].rearrange("p (g k) -> p g k", k=8),
                        m1b, ALU.is_equal)
        e.scalar_tensor_tensor(gw[:, 256:384], gw[:, 128:256], -1e30,
                               gl[:, 0:128], ALU.mult, ALU.add)
        return e.reduce_max(gw[:, 32:48],
                            gw[:, 256:384].rearrange("p (g k) -> p g k", k=8),
                            axis=AX.X)
    t.step("dve", ("moe_gl", "moe_m1"), ("moe_gw",), f_eq1)
    t.step("act", ("moe_gw",), ("moe_m2",),
           lambda e: e.activation(gw[:, 48:64], gw[:, 32:48], AF.Copy,
                                  bias=0.0, scale=1.0))
    t.step("dve", ("moe_m1", "moe_m2"), ("moe_d",),
           lambda e: e.tensor_sub(gw[:, 64:80], gw[:, 48:64], gw[:, 16:32]))

    def f_pa(e):
        e.activation(gw[:, 80:96], gw[:, 64:80], AF.Exp, bias=0.0, scale=1.0)
        e.activation(gw[:, 96:112], gw[:, 80:96], AF.Ln,
                     bias=cpool[:, C_ONE:C_ONE + 1], scale=1.0)
        return e.activation(gw[:, 96:112], gw[:, 96:112], AF.Exp,
                            bias=0.0, scale=-1.0)
    t.step("act", ("moe_d", "cpool"), ("moe_gw2",), f_pa)

    def f_dg(e):
        e.tensor_mul(gw[:, 112:128], gw[:, 80:96], gw[:, 96:112])
        m2b = gw[:, 48:64].unsqueeze(2).broadcast_to((128, 16, 8))
        e.tensor_tensor(gw[:, 384:512].rearrange("p (g k) -> p g k", k=8),
                        gl[:, 0:128].rearrange("p (g k) -> p g k", k=8),
                        m2b, ALU.is_equal)
        p1b = gw[:, 96:112].unsqueeze(2).broadcast_to((128, 16, 8))
        e.tensor_tensor(gw[:, 128:256].rearrange("p (g k) -> p g k", k=8),
                        gw[:, 128:256].rearrange("p (g k) -> p g k", k=8),
                        p1b, ALU.mult)
        p2b = gw[:, 112:128].unsqueeze(2).broadcast_to((128, 16, 8))
        e.tensor_tensor(gw[:, 384:512].rearrange("p (g k) -> p g k", k=8),
                        gw[:, 384:512].rearrange("p (g k) -> p g k", k=8),
                        p2b, ALU.mult)
        return e.tensor_add(gl[:, 128:256], gw[:, 128:256], gw[:, 384:512])
    t.step("dve", ("moe_gl", "moe_gw", "moe_gw2", "moe_m1", "moe_m2"),
           ("moe_gw", "moe_g"), f_dg)

    for ei in range(NE):
        t.step("syd", (), ("fc1b_sb",),
               (lambda ei: lambda e: e.dma_start(
                   fc1b_sb, d["f0_fc1b"][ei]))(ei))
        t.step("syd", (), ("fc2b_sb",),
               (lambda ei: lambda e: e.dma_start(
                   fc2b_sb, d["f0_fc2b"][ei]))(ei))
        t.step("dve", ("fc2b_sb",), ("fc2b_bf",),
               lambda e: e.tensor_copy(fc2b_bf, fc2b_sb))
        for tc in range(4):
            for fch in range(FFN // 512):
                wsl = fch % 2
                t.step("syd", (f"fc1bf_{ei}",), (f"fc1w{wsl}",),
                       (lambda ei, fch, wsl: lambda e: e.dma_start(
                           fc1sb[:, wsl * 4096:(wsl + 1) * 4096],
                           d["f0_fc1_bf"][ei, fch]))(ei, fch, wsl))
                for ft in range(4):
                    fglob = fch * 4 + ft
                    pbk = 3 + (fglob % 2)

                    def f_f1(e, tc=tc, wsl=wsl, ft=ft, pbk=pbk):
                        ins = None
                        for ct in range(CT):
                            ins = e.matmul(
                                p.pb[pbk].ap()[:, 0:512],
                                fc1sb[:, wsl * 4096 + ct * 512 + ft * 128:
                                      wsl * 4096 + ct * 512 + (ft + 1) * 128],
                                nxT[:, nxc(ct, tc * 512):
                                    nxc(ct, (tc + 1) * 512)],
                                start=(ct == 0), stop=(ct == CT - 1))
                        return ins
                    t.step("pe", (f"fc1w{wsl}", f"{nxT_name}{tc}"),
                           (f"pb{pbk}",), f_f1)

                    def f_g1(e, fglob=fglob, pbk=pbk):
                        return e.activation(
                            hT[:, fglob * 512:(fglob + 1) * 512],
                            p.pb[pbk].ap()[:, 0:512], AF.Gelu,
                            bias=fc1b_sb[:, fglob:fglob + 1], scale=1.0)
                    t.step("act", (f"pb{pbk}", "fc1b_sb"),
                           (f"moe_hT{fglob}",), f_g1)
            for ft in range(FFN // 128):
                wsl = (ft // 2) % 2
                if ft % 2 == 0:
                    t.step("scd", (f"fc2bf_{ei}",), (f"fc2w{wsl}",),
                           (lambda ei, ft, wsl: lambda e: e.dma_start(
                               fc2sb[:, wsl * 2048:(wsl + 1) * 2048]
                               .rearrange("p (j c) -> p j c", c=1024),
                               d["f0_fc2_bf"][ei, ft:ft + 2]
                               .transpose([1, 0, 2])))(ei, ft, wsl))

                def f_f2(e, ft=ft, wsl=wsl):
                    ins = None
                    off = wsl * 2048 + (ft % 2) * 1024
                    for ts in range(4):
                        for oc in range(2):
                            ins = e.matmul(
                                p.pb[ts * 2 + oc].ap()[:, 0:512],
                                hT[:, ft * 512 + ts * 128:
                                   ft * 512 + (ts + 1) * 128],
                                fc2sb[:, off + oc * 512:off + (oc + 1) * 512],
                                start=(ft == 0), stop=False)
                    return ins
                t.step("pe", (f"fc2w{wsl}", f"moe_hT{ft}"),
                       tuple(f"pb{bk}" for bk in range(8)), f_f2)

            def f_b2(e, ei=ei):
                ins = None
                for ts in range(4):
                    for oc in range(2):
                        ins = e.matmul(
                            p.pb[ts * 2 + oc].ap()[:, 0:512],
                            onesr[0:1, 0:128],
                            fc2b_bf[0:1, oc * 512:(oc + 1) * 512],
                            start=False, stop=True)
                return ins
            t.step("pe", ("fc2b_bf", "onesr"),
                   tuple(f"pb{bk}" for bk in range(8)), f_b2)
            for ts in range(4):
                tt = tc * 4 + ts
                usl = ts % 2
                t.step("scd", (f"ud{tt}",), (f"uacc{usl}",),
                       (lambda tt, usl: lambda e: e.dma_start(
                           uacc[:, usl * 1024:(usl + 1) * 1024],
                           ud[tt * 128:(tt + 1) * 128, :]))(tt, usl))
                for oc in range(2):
                    bk = ts * 2 + oc

                    def f_u(e, bk=bk, tt=tt, oc=oc, ei=ei, usl=usl):
                        sl = slice(usl * 1024 + oc * 512,
                                   usl * 1024 + (oc + 1) * 512)
                        return e.scalar_tensor_tensor(
                            uacc[:, sl], p.pb[bk].ap()[:, 0:512],
                            gl[:, 128 + tt * 8 + ei:129 + tt * 8 + ei],
                            uacc[:, sl], ALU.mult, ALU.add)
                    t.step("dve", (f"pb{bk}", "moe_g", f"uacc{usl}"),
                           (f"uacc{usl}",), f_u)
                t.step("scd", (f"uacc{usl}",), (f"ud{tt}",),
                       (lambda tt, usl: lambda e: e.dma_start(
                           ud[tt * 128:(tt + 1) * 128, :],
                           uacc[:, usl * 1024:(usl + 1) * 1024]))(tt, usl))


def emit_qlora_ffn(p, ud):
    t, d = p.t, p.dram
    onesr = p.sbufs["onesr"]
    identb = p.sbufs["identb"]
    hT = p.sbufs["nxT"]
    ygT = p.sbufs["attT"]
    qt_sb = p.sbufs["qt_sb"]
    sc_sb = p.sbufs["sc_sb"]
    wdq = p.sbufs["wdq"]
    la_sb = p.sbufs["la_sb"]
    lb_sb = p.sbufs["lb_sb"]
    qbias = p.sbufs["qbias"]
    qbias_f = p.sbufs["qbias_f"]
    sT = p.sbufs["sT"]
    g_tab = p.sbufs["gb_g"]
    b_tab = p.sbufs["gb_b"]
    zsb = p.sbufs["w_z"]
    sq = p.sbufs["w_sq"]
    ybf = p.sbufs["w_nx"]
    uacc = p.sbufs["uacc"]
    nxbf = p.sbufs["w_nx"]

    t.step("gpd", (), ("la_sb",), lambda e: e.dma_start(
        la_sb.rearrange("p (i r) -> p i r", r=CT * LORA_R),
        d["q1_la"].transpose([1, 0, 2])))

    def cast_and_T():
        for tt in range(TT):
            sl = tt % 2
            t.step("gpd", (f"ud{tt}",), (f"w_nx{sl}",),
                   (lambda tt, sl: lambda e: e.dma_start(
                       nxbf[:, sl * 1024:(sl + 1) * 1024],
                       ud[tt * 128:(tt + 1) * 128, :]))(tt, sl))
            emit_transposes_tile(
                p, tt, nxbf[:, sl * 1024:(sl + 1) * 1024], f"w_nx{sl}",
                "nxT")

    cast_and_T()

    for blk in range(3):
        t.step("syd", (), ("gb_g",),
               (lambda blk: lambda e: e.dma_start(
                   g_tab,
                   d["q1_lng"][blk].partition_broadcast(128).squeeze(1)))(blk))
        t.step("syd", (), ("gb_b",),
               (lambda blk: lambda e: e.dma_start(
                   b_tab,
                   d["q1_lnb"][blk].partition_broadcast(128).squeeze(1)))(blk))
        for half in range(2):
            mi = 2 * blk + half
            inT, inT_name = (hT, "nxT") if half == 0 else (ygT, "attT")
            t.step("gpd", (), ("lb_sb",),
                   (lambda mi: lambda e: e.dma_start(
                       lb_sb, d["q1_lb"][mi]))(mi))
            t.step("syd", (), ("qbias_f",),
                   (lambda mi: lambda e: e.dma_start(
                       qbias_f, d["q1_b"][mi]))(mi))
            t.step("dve", ("qbias_f",), ("qbias",),
                   lambda e: e.tensor_copy(qbias, qbias_f))
            for chunk in range(4):
                def f_sl(e, mi=mi, chunk=chunk, inT=inT):
                    ins = None
                    for ct in range(CT):
                        ins = e.matmul(
                            p.pb[2].ap()[0:32, 0:512],
                            la_sb[:, mi * (CT * LORA_R) + ct * LORA_R:
                                  mi * (CT * LORA_R) + (ct + 1) * LORA_R],
                            inT[:, nxc(ct, chunk * 512):
                                nxc(ct, (chunk + 1) * 512)],
                            start=(ct == 0), stop=(ct == CT - 1))
                    return ins
                t.step("pe", ("la_sb", f"{inT_name}{chunk}"), ("pb2",),
                       f_sl)

                def f_se(e, chunk=chunk):
                    return e.tensor_copy(
                        sT[0:32, chunk * 512:(chunk + 1) * 512],
                        p.pb[2].ap()[0:32, 0:512])
                t.step("dve", ("pb2",), ("sT",), f_se)
            for oc in range(2):
                for hf in range(2):
                    t.step("syd", (), ("qt_sb",),
                           (lambda mi, oc, hf: lambda e: e.dma_start(
                               qt_sb, d["q1_qt"][mi, oc][
                                   :, hf * 2048:(hf + 1) * 2048]))(mi, oc, hf))
                    t.step("syd", (), ("sc_sb",),
                           (lambda mi, oc, hf: lambda e: e.dma_start(
                               sc_sb, d["q1_sc"][mi, oc][
                                   :, hf * 2048:(hf + 1) * 2048]))(mi, oc, hf))

                    def f_dq(e, oc=oc, hf=hf):
                        ins = None
                        for dt4 in range(4):
                            slc = slice(dt4 * 512, (dt4 + 1) * 512)
                            e.tensor_copy(sq[:, 0:512], qt_sb[:, slc])
                            e.tensor_scalar(sq[:, 512:1024], sq[:, 0:512],
                                            1.0 / 7.5, -1.0, ALU.mult,
                                            ALU.add)
                            ins = e.tensor_mul(
                                wdq[:, oc * 4096 + hf * 2048 + dt4 * 512:
                                    oc * 4096 + hf * 2048 + (dt4 + 1) * 512],
                                sq[:, 512:1024], sc_sb[:, slc])
                        return ins
                    t.step("dve", ("qt_sb", "sc_sb"),
                           (f"wdq{oc}", "w_sqscr"), f_dq)
            for tt in range(TT):
                usl = tt % 2
                if half == 1:
                    t.step("scd", (f"ud{tt}",), (f"uacc{usl}",),
                           (lambda tt, usl: lambda e: e.dma_start(
                               uacc[:, usl * 1024:(usl + 1) * 1024],
                               ud[tt * 128:(tt + 1) * 128, :]))(tt, usl))
                for oc in range(2):
                    pbk = oc

                    def f_gm(e, mi=mi, oc=oc, tt=tt, pbk=pbk, inT=inT):
                        for ct in range(CT):
                            e.matmul(
                                p.pb[pbk].ap()[:, 0:512],
                                inT[:, nxc(ct, tt * 128):
                                    nxc(ct, (tt + 1) * 128)],
                                wdq[:, oc * 4096 + ct * 512:
                                    oc * 4096 + (ct + 1) * 512],
                                start=(ct == 0), stop=False)
                        e.matmul(
                            p.pb[pbk].ap()[:, 0:512],
                            sT[0:32, tt * 128:(tt + 1) * 128],
                            lb_sb[0:32, oc * 512:(oc + 1) * 512],
                            start=False, stop=False)
                        return e.matmul(
                            p.pb[pbk].ap()[:, 0:512], onesr[0:1, 0:128],
                            qbias[0:1, oc * 512:(oc + 1) * 512],
                            start=False, stop=True)
                    t.step("pe", (f"wdq{oc}", f"{inT_name}{tt // 4}", "sT",
                                  "lb_sb", "qbias", "onesr"),
                           (f"pb{pbk}",), f_gm)
                    if half == 1:
                        def f_h(e, oc=oc, pbk=pbk, usl=usl):
                            sl = slice(usl * 1024 + oc * 512,
                                       usl * 1024 + (oc + 1) * 512)
                            return e.tensor_add(uacc[:, sl], uacc[:, sl],
                                                p.pb[pbk].ap()[:, 0:512])
                        t.step("dve", (f"pb{pbk}", f"uacc{usl}"),
                               (f"uacc{usl}",), f_h)
                    else:
                        def f_z(e, tt=tt, oc=oc, pbk=pbk):
                            return e.tensor_copy(
                                zsb[:, (tt % 2) * 1024 + oc * 512:
                                    (tt % 2) * 1024 + (oc + 1) * 512],
                                p.pb[pbk].ap()[:, 0:512])
                        t.step("dve", (f"pb{pbk}",),
                               (f"w_z{tt % 2}o{oc}",), f_z)
                if half == 1:
                    t.step("scd", (f"uacc{usl}",), (f"ud{tt}",),
                           (lambda tt, usl: lambda e: e.dma_start(
                               ud[tt * 128:(tt + 1) * 128, :],
                               uacc[:, usl * 1024:(usl + 1) * 1024]))(tt,
                                                                      usl))
                else:
                    z_ap = zsb[:, (tt % 2) * 1024:(tt % 2 + 1) * 1024]
                    zr = (f"w_z{tt % 2}o0", f"w_z{tt % 2}o1")

                    def mko(tt=tt, z_ap=z_ap, zr=zr):
                        def out_fn(negmu, rstd, streads):
                            tmp = p.sbufs["w_tmp"]
                            nsl = tt % 2

                            def f(e):
                                e.tensor_scalar(tmp[:, 0:1024], z_ap, negmu,
                                                rstd, ALU.add, ALU.mult)
                                e.tensor_mul(tmp[:, 0:1024], tmp[:, 0:1024],
                                             g_tab)
                                return e.tensor_add(tmp[:, 0:1024],
                                                    tmp[:, 0:1024], b_tab)
                            t.step("dve", zr + streads + ("gb_g", "gb_b"),
                                   ("w_tmp",), f)

                            def fg(e):
                                return e.activation(
                                    ybf[:, nsl * 1024:(nsl + 1) * 1024],
                                    tmp[:, 0:1024], AF.Gelu, bias=0.0,
                                    scale=1.0)
                            t.step("act", ("w_tmp",), (f"w_nx{nsl}",), fg)
                        return out_fn
                    ln_steps(p, z_ap, zr, (tt % 4) * 4, mko())
                    emit_transposes_tile(
                        p, tt, ybf[:, (tt % 2) * 1024:(tt % 2 + 1) * 1024],
                        f"w_nx{tt % 2}", "attT")
        if blk < 2:
            cast_and_T()


def emit_slot(p, sl, src, dst, src_res, dst_res):
    d = p.dram
    t = p.t
    ud = d["ud"]
    if sl == "l0a":
        p.fence(SCR_NAMES)
        emit_nx_phase(p, "nxT", src, src_res)
        p.tap("dbg_nxT", p.sbufs["nxT"], ("nxT",))
        p.fence(WT_NAMES + SCR_NAMES)
        emit_attn_heads(p, "a0", "rope", "nxT")
        p.tap("dbg_attT", p.sbufs["attT"], ("attT",))
        p.fence(WT_NAMES)
        t.step("gpd", (), ("wo_sb",),
               lambda e: e.dma_start(p.sbufs["wo_sb"], d["a0_wo"]))
        t.step("gpd", (), ("bo_sb",),
               lambda e: e.dma_start(p.sbufs["bo_sb"], d["a0_bo"]))
        p.fence(SCR_NAMES)
        emit_proj(p, "wo_sb", "bo_sb", "sandwich", src_dram=src,
                  dst_dram=dst, src_res=src_res, dst_res=dst_res)
    elif sl == "l0f":
        p.fence(SCR_NAMES)
        emit_u_phase(p, src, ud, src_res)
        t.step("syd", (), ("gb_g",), lambda e: e.dma_start(
            p.sbufs["gb_g"],
            d["f0_ln1w"].partition_broadcast(128).squeeze(1)))
        t.step("syd", (), ("gb_b",), lambda e: e.dma_start(
            p.sbufs["gb_b"],
            d["f0_ln1b"].partition_broadcast(128).squeeze(1)))
        emit_nx_phase(p, "nxT", ud, "ud", gamma="gb_g", beta="gb_b")
        t.step("gpd", (), ("bqk_sb",),
               lambda e: e.dma_start(p.sbufs["bqk_sb"], d["f0_bqk"]))
        t.step("gpd", (), ("bv_sb",),
               lambda e: e.dma_start(p.sbufs["bv_sb"], d["f0_bv"]))
        p.fence(WT_NAMES + SCR_NAMES)
        emit_attn_heads(p, "f0", "plain", "nxT")
        p.fence(WT_NAMES)
        t.step("gpd", (), ("wo_sb",),
               lambda e: e.dma_start(p.sbufs["wo_sb"], d["f0_wo"]))
        t.step("gpd", (), ("bo_sb",),
               lambda e: e.dma_start(p.sbufs["bo_sb"], d["f0_bo"]))
        p.fence(SCR_NAMES)
        emit_proj(p, "wo_sb", "bo_sb", "accum_u", ud=ud)
        t.step("syd", (), ("gb_g",), lambda e: e.dma_start(
            p.sbufs["gb_g"],
            d["f0_ln2w"].partition_broadcast(128).squeeze(1)))
        t.step("syd", (), ("gb_b",), lambda e: e.dma_start(
            p.sbufs["gb_b"],
            d["f0_ln2b"].partition_broadcast(128).squeeze(1)))
        emit_nx_phase(p, "nxT", ud, "ud", gamma="gb_g", beta="gb_b")
        p.fence(WT_NAMES + ATT_NAMES)
        emit_moe(p, "nxT", ud)
        p.fence(SCR_NAMES + ATT_NAMES)
        emit_u_sandwich(p, src, dst, ud, src_res, dst_res)
    elif sl == "l1a":
        p.fence(SCR_NAMES)
        emit_nx_phase(p, "nxT", src, src_res)
        t.step("syd", (), ("l1lntab",), lambda e: e.dma_start(
            p.sbufs["l1lntab"],
            d["a1_lntab"].partition_broadcast(128).squeeze(1)))
        p.fence(WT_NAMES + SCR_NAMES)
        emit_attn_heads(p, "a1", "l1", "nxT")
        p.fence(WT_NAMES)
        t.step("gpd", (), ("wo_sb",),
               lambda e: e.dma_start(p.sbufs["wo_sb"], d["a1_wu"]))
        p.fence(SCR_NAMES)
        emit_proj(p, "wo_sb", None, "sandwich", src_dram=src, dst_dram=dst,
                  src_res=src_res, dst_res=dst_res)
    elif sl == "l1f":
        p.fence(SCR_NAMES)
        emit_u_phase(p, src, ud, src_res)
        p.fence(WT_NAMES + ATT_NAMES)
        emit_qlora_ffn(p, ud)
        p.fence(SCR_NAMES)
        emit_u_sandwich(p, src, dst, ud, src_res, dst_res)


def build_program(seqs=SPC, slots=("l0a", "l0f", "l1a", "l1f"), dbg=()):
    nc = bass.Bass("TRN2", target_bir_lowering=False)
    dram = {}

    def din(name, shape, dtype=F32):
        dram[name] = nc.dram_tensor(name, list(shape), dtype,
                                    kind="ExternalInput").ap()

    def dout(name, shape, dtype=F32):
        dram[name] = nc.dram_tensor(name, list(shape), dtype,
                                    kind="ExternalOutput").ap()

    def dint(name, shape, dtype=F32):
        dram[name] = nc.dram_tensor(name, list(shape), dtype,
                                    kind="Internal").ap()

    din("x", (SPC, T, C))
    dout("y", (SPC, T, C))
    dint("xa", (SPC, T, C))
    dint("xb", (SPC, T, C))
    dint("ud", (T, C))

    din("consts", (1, C_NCONST))
    din("identf", (128, 128))
    din("ropec", (128, T))
    din("ropes", (128, T))
    din("masks", (2, 128, QCH))

    din("a0_wqk", (H, 128, C))
    din("a0_wv", (4, 128, 2048))
    din("a0_wo", (128, CT * 1024))
    din("a0_bo", (1, C))

    din("f0_wqk", (H, 128, C))
    din("f0_bqk", (1, H * 128))
    din("f0_wv", (4, 128, 2048))
    din("f0_bv", (1, 4 * 256))
    din("f0_wo", (128, CT * 1024))
    din("f0_bo", (1, C))
    din("f0_ln1w", (1, C))
    din("f0_ln1b", (1, C))
    din("f0_ln2w", (1, C))
    din("f0_ln2b", (1, C))
    din("f0_gate", (128, CT * NE))
    din("f0_fc1", (NE, FFN // 512, 128, CT * 512))
    din("f0_fc1b", (NE, 128, FFN // 128))
    din("f0_fc2", (NE, FFN // 128, 128, C))
    din("f0_fc2b", (NE, 1, C))
    dint("f0_fc1_bf", (NE, FFN // 512, 128, CT * 512), BF16)
    dint("f0_fc2_bf", (NE, FFN // 128, 128, C), BF16)

    din("a1_wqkv", (H, 128, CT * 192))
    din("a1_wu", (128, CT * 1024))
    din("a1_lntab", (1, 256))

    din("q1_qt", (NMAT, 2, 128, CT * 512), I32)
    din("q1_sc", (NMAT, 2, 128, CT * 512))
    din("q1_b", (NMAT, 1, C))
    din("q1_la", (NMAT, 128, CT * LORA_R))
    din("q1_lb", (NMAT, LORA_R, C))
    din("q1_lng", (3, 1, C))
    din("q1_lnb", (3, 1, C))

    for nm, shp, dt in (("dbg_nxT", (128, CT * T), BF16),
                        ("dbg_qb", (64, T), BF16),
                        ("dbg_kb", (64, T), BF16),
                        ("dbg_attT", (128, CT * T), BF16),
                        ("dbg_st", (128, 16 * QCH), F32),
                        ("dbg_vg", (128, TT * 260), BF16)):
        if nm in dbg:
            dram[nm] = nc.dram_tensor(nm, list(shp), dt,
                                      kind="ExternalOutput").ap()
    trk = Trk()
    with ExitStack() as ctx:
        p = P(nc, ctx, trk, dram, dbg)
        p.sb("cpool", [128, C_NCONST], F32)
        p.sb("identf", [128, 128], F32)
        p.sb("identb", [128, 128], BF16)
        p.sb("onesr", [1, 512], BF16)
        p.sb("mask_sb", [128, 2 * QCH], F32)
        p.sb("nxT", [128, CT * T], BF16)
        att_ar = p.sb("att_ar", [128, CT * T], BF16)
        p.sbufs["attT"] = att_ar
        p.sbufs["moe_hT"] = att_ar
        p.sb("vgrp", [128, TT * 260], BF16)
        p.sb("st_sb", [128, 16 * QCH], F32)

        p.sb("qkc", [128, 512], F32)
        p.sb("rot", [128, 512], F32)
        p.sb("ptbT", [128, 2 * QCH], BF16)
        p.sbufs["ptb"] = p.sbufs["ptbT"]
        p.sb("att2", [128, 128], BF16)
        p.sb("bmx", [128, 32], F32)
        p.sb("bmx2", [128, 32], F32)
        p.sb("chain", [128, 12], F32)
        p.sb("chain_bf", [1, 4], BF16)
        p.sb("l1mu", [128, 64], F32)
        p.sb("l1var", [128, 32], F32)
        p.sb("l1lntab", [128, 256], F32)
        p.sb("gb_g", [128, C], F32)
        p.sb("gb_b", [128, C], F32)
        p.sb("moe_gl", [128, 256], F32)
        p.sb("moe_gw", [128, 512], F32)
        p.sb("fc1b_sb", [128, FFN // 128], F32)
        p.sb("fc2b_sb", [1, C], F32)
        p.sb("fc2b_bf", [1, C], BF16)
        p.sb("bo_sb", [1, C], BF16)
        p.sb("bqk_sb", [1, H * 128], BF16)
        p.sb("bv_sb", [1, 4 * 256], BF16)
        p.sb("qbias_f", [1, C], F32)
        p.sb("qbias", [1, C], BF16)
        p.sb("la_sb", [128, NMAT * CT * LORA_R], BF16)
        p.sb("lb_sb", [LORA_R, C], BF16)
        zqk = p.sb("zqk", [128, 2 * 1024], F32)
        p.sbufs["w_z"] = zqk
        qbkb = zqk.bitcast(BF16)
        p.sbufs["qb"] = qbkb[0:64, 0:2048]
        p.sbufs["kb"] = qbkb[0:64, 2048:4096]
        p.sb("w_xin", [128, 2 * 1024], F32)
        p.sb("w_st", [128, 16], F32)
        p.sb("w_sq", [128, 1024], F32)
        p.sb("w_nx", [128, 2 * 1024], BF16)
        p.sb("w_tmp", [128, 1024], F32)
        uo = p.sb("uacc", [128, 2 * 1024], F32)
        p.sbufs["w_out"] = uo
        p.sbufs["l1sub"] = p.sbufs["uacc"][:, 0:2048]
        p.sbufs["qkall"] = p.sbufs["w_xin"][:, 0:2048]
        p.sbufs["qknorm"] = p.sbufs["w_nx"][:, 0:2048]
        wt = p.sb("wt_ar", [128, 9216], F32)
        p.view("wqk_sb", wt, 0, 128, 3072, BF16)
        p.view("wv_sb", wt, 6144, 128, 4096, BF16)
        p.view("cos2", wt, 14336, 128, 2048, F32)
        p.view("sin2", wt, 22528, 128, 2048, F32)
        p.view("wo_sb", wt, 0, 128, 8192, BF16)
        p.view("fc1sb", wt, 0, 128, 8192, BF16)
        p.view("fc2sb", wt, 20480, 128, 4096, BF16)
        p.view("gate_sb", wt, 28672, 128, 64, F32)
        p.view("gate_bf", wt, 28928, 128, 64, BF16)
        p.view("wdq", wt, 0, 128, 8192, BF16)
        p.view("qt_sb", wt, 16384, 128, 2048, I32)
        p.view("sc_sb", wt, 24576, 128, 2048, F32)
        p.view("sT", wt, 32768, 32, 2048, BF16)

        emit_init(p)
        phase0_done = [False]
        for s in range(seqs):
            cur = dram["x"][s]
            cur_res = f"xinp{s}_"
            nslots = len(slots)
            for idx, sl in enumerate(slots):
                if sl == "l0f" and not phase0_done[0]:
                    emit_phase0_cast(p)
                    phase0_done[0] = True
                if idx == nslots - 1:
                    nxt = dram["y"][s]
                else:
                    nxt = dram["xa"][s] if idx % 2 == 0 else dram["xb"][s]
                nxt_res = f"xch{s}_{idx}_"
                emit_slot(p, sl, cur, nxt, cur_res, nxt_res)
                cur = nxt
                cur_res = nxt_res
        trk.emit(nc, ctx)
    return nc



def _t128(a):
    Cdim, X = a.shape
    return np.ascontiguousarray(
        a.reshape(Cdim // 128, 128, X).transpose(1, 0, 2).reshape(128, -1))


def pack_inputs(inp):
    f32 = np.float32
    g = {}
    g["consts"] = np.array([[1e-5, 1.0 / 1024, -0.5, 0.125, 1.0 / 64,
                             1.0, -1.0 / 1024, 1.0 / 64]], f32)
    g["identf"] = np.eye(128, dtype=f32)
    invf = (1.0 / 10000.0 ** (np.arange(32, dtype=np.float64) / 32))
    tpos = np.arange(T, dtype=np.float64)
    fr = np.outer(invf, tpos)
    g["ropec"] = np.ascontiguousarray(np.tile(np.cos(fr).astype(f32),
                                              (4, 1)))
    g["ropes"] = np.ascontiguousarray(np.tile(np.sin(fr).astype(f32),
                                              (4, 1)))
    kk = np.arange(128)[:, None]
    qq = np.arange(QCH)[None, :]
    g["masks"] = np.stack([(kk <= qq).astype(f32),
                           (kk + 128 <= qq).astype(f32)])

    W = np.asarray(inp["L0_attn_Wqkv"], f32)
    wqk = np.empty((H, 128, C), f32)
    for h in range(H):
        q = W[:, h * 64:(h + 1) * 64]
        k = W[:, 1024 + h * 64:1024 + (h + 1) * 64]
        cols = np.concatenate([q[:, 0::2], k[:, 0::2], q[:, 1::2],
                               k[:, 1::2]], axis=1)
        wqk[h] = _t128(cols)
    g["a0_wqk"] = wqk
    wv = np.empty((4, 128, 2048), f32)
    for gi in range(4):
        wv[gi] = _t128(W[:, 2048 + gi * 256:2048 + (gi + 1) * 256])
    g["a0_wv"] = wv
    g["a0_wo"] = _t128(np.asarray(inp["L0_attn_Wo"], f32))
    g["a0_bo"] = np.asarray(inp["L0_attn_bo"], f32).reshape(1, C)

    W = np.asarray(inp["L0_ffn_Wqkv"], f32)
    bq = np.asarray(inp["L0_ffn_bqkv"], f32)
    wqk = np.empty((H, 128, C), f32)
    bqk = np.empty((1, H * 128), f32)
    for h in range(H):
        cols = np.concatenate([W[:, h * 64:(h + 1) * 64],
                               W[:, 1024 + h * 64:1024 + (h + 1) * 64]],
                              axis=1)
        wqk[h] = _t128(cols)
        bqk[0, h * 128:h * 128 + 64] = bq[h * 64:(h + 1) * 64]
        bqk[0, h * 128 + 64:(h + 1) * 128] = bq[1024 + h * 64:
                                                1024 + (h + 1) * 64]
    g["f0_wqk"] = wqk
    g["f0_bqk"] = bqk
    wv = np.empty((4, 128, 2048), f32)
    for gi in range(4):
        wv[gi] = _t128(W[:, 2048 + gi * 256:2048 + (gi + 1) * 256])
    g["f0_wv"] = wv
    g["f0_bv"] = np.ascontiguousarray(bq[2048:].reshape(1, 1024))
    g["f0_wo"] = _t128(np.asarray(inp["L0_ffn_Wo"], f32))
    g["f0_bo"] = np.asarray(inp["L0_ffn_bo"], f32).reshape(1, C)
    g["f0_ln1w"] = np.asarray(inp["L0_ffn_ln1_w"], f32).reshape(1, C)
    g["f0_ln1b"] = np.asarray(inp["L0_ffn_ln1_b"], f32).reshape(1, C)
    g["f0_ln2w"] = np.asarray(inp["L0_ffn_ln2_w"], f32).reshape(1, C)
    g["f0_ln2b"] = np.asarray(inp["L0_ffn_ln2_b"], f32).reshape(1, C)
    g["f0_gate"] = _t128(np.asarray(inp["L0_ffn_gate_W"], f32))
    fc1 = np.asarray(inp["L0_ffn_fc1_w"], f32)
    a = np.empty((NE, 8, 128, 4096), f32)
    for ei in range(NE):
        At = fc1[ei].T
        a[ei] = At.reshape(8, 128, 8, 512).transpose(2, 1, 0, 3).reshape(
            8, 128, 4096)
    g["f0_fc1"] = a
    g["f0_fc1b"] = np.ascontiguousarray(
        np.asarray(inp["L0_ffn_fc1_b"], f32).reshape(NE, 32, 128)
        .transpose(0, 2, 1))
    fc2 = np.asarray(inp["L0_ffn_fc2_w"], f32)
    a = np.empty((NE, 32, 128, C), f32)
    for ei in range(NE):
        a[ei] = fc2[ei].T.reshape(32, 128, C)
    g["f0_fc2"] = a
    g["f0_fc2b"] = np.asarray(inp["L0_ffn_fc2_b"], f32).reshape(NE, 1, C)

    Wq = np.asarray(inp["L1_attn_Wq"], f32)
    Wk = np.asarray(inp["L1_attn_Wk"], f32)
    Wv = np.asarray(inp["L1_attn_Wv"], f32)
    wqkv = np.empty((H, 128, CT * 192), f32)
    for h in range(H):
        cols = np.concatenate([Wq[h * 64:(h + 1) * 64, :].T,
                               Wk[h * 64:(h + 1) * 64, :].T,
                               Wv[h * 64:(h + 1) * 64, :].T], axis=1)
        wqkv[h] = _t128(cols)
    g["a1_wqkv"] = wqkv
    g["a1_wu"] = _t128(np.asarray(inp["L1_attn_Wu"], f32).T)
    lntab = np.zeros((1, 256), f32)
    lntab[0, 0:64] = np.asarray(inp["L1_attn_qln_w"], f32)
    lntab[0, 64:128] = np.asarray(inp["L1_attn_kln_w"], f32)
    lntab[0, 128:192] = np.asarray(inp["L1_attn_qln_b"], f32)
    lntab[0, 192:256] = np.asarray(inp["L1_attn_kln_b"], f32)
    g["a1_lntab"] = lntab

    qw = np.asarray(inp["L1_ffn_qweights"], np.int32)
    sc = np.asarray(inp["L1_ffn_scales"], f32)
    aq = np.empty((NMAT, 2, 128, 4096), np.int32)
    asc = np.empty((NMAT, 2, 128, 4096), f32)
    for i in range(NMAT):
        A = qw[i]
        aq[i] = A.reshape(2, 512, 8, 128).transpose(0, 3, 2, 1).reshape(
            2, 128, 4096)
        S = np.repeat(sc[i], 16, axis=1)
        asc[i] = S.reshape(2, 512, 8, 128).transpose(0, 3, 2, 1).reshape(
            2, 128, 4096)
    g["q1_qt"] = aq
    g["q1_sc"] = asc
    g["q1_b"] = np.asarray(inp["L1_ffn_biases"], f32).reshape(NMAT, 1, C)
    la = np.asarray(inp["L1_ffn_lora_a"], f32)
    a = np.empty((NMAT, 128, CT * LORA_R), f32)
    for i in range(NMAT):
        a[i] = _t128(la[i].T)
    g["q1_la"] = a
    g["q1_lb"] = np.ascontiguousarray(
        np.asarray(inp["L1_ffn_lora_b"], f32).transpose(0, 2, 1))
    g["q1_lng"] = np.asarray(inp["L1_ffn_ln_gamma"], f32).reshape(3, 1, C)
    g["q1_lnb"] = np.asarray(inp["L1_ffn_ln_beta"], f32).reshape(3, 1, C)
    return g


def kernel(**inp):
    x = np.asarray(inp["L0_attn_x"], np.float32)
    g = pack_inputs(inp)
    nc = build_program()
    in_maps = []
    for core in range(NCORES):
        m = dict(g)
        m["x"] = np.ascontiguousarray(x[core * SPC:(core + 1) * SPC])
        in_maps.append(m)
    res = run_bass_kernel_spmd(nc, in_maps, core_ids=list(range(NCORES)))
    out = np.concatenate([r["y"] for r in res.results], axis=0)
    return out.astype(np.float32)


# version 04
# speedup vs baseline: 1.1168x; 1.0953x over previous
from contextlib import ExitStack

import numpy as np
import ml_dtypes
import concourse.bass as bass
import concourse.mybir as mybir
from concourse.bass_utils import run_bass_kernel_spmd

F32 = mybir.dt.float32
BF16 = mybir.dt.bfloat16
I32 = mybir.dt.int32
AF = mybir.ActivationFunctionType
ALU = mybir.AluOpType
AX = mybir.AxisListType

B, T, C = 32, 2048, 1024
H, HD = 16, 64
NCORES = 8
SPC = B // NCORES
TT = T // 128
CT = C // 128
QCH = 256
NQC = T // QCH
FFN = 4096
NE = 8
NMAT = 6
LORA_R = 32

C_EPS = 0
C_INV1024 = 1
C_NEGHALF = 2
C_ATTSCL = 3
C_INV64 = 4
C_ONE = 5
C_NEGINV1024 = 6
C_MEAN64 = 7
C_NCONST = 8

DMA_ENGS = ("syd", "scd", "gpd")
ENGQ = {"pe": "tensor", "dve": "vector", "act": "scalar", "pool": "gpsimd",
        "syd": "sync", "scd": "scalar", "gpd": "gpsimd"}
QUEUES = ("sync", "tensor", "vector", "scalar", "gpsimd")


class Trk:
    def __init__(self):
        self.steps = []
        self.cnt = {e: 0 for e in ENGQ}
        self.clk = {e: {f: 0 for f in ENGQ} for e in ENGQ}
        self.hist = {e: [] for e in ENGQ}
        self.res = {}

    def step(self, eng, reads=(), writes=(), fn=None, tag=""):
        mycl = self.clk[eng]
        waits = {}

        def need(m):
            s2, k = m
            if s2 == eng or mycl[s2] >= k:
                return
            if waits.get(s2, 0) < k:
                waits[s2] = k
            h = self.hist[s2][k - 1]
            for s3, v3 in h.items():
                if v3 > mycl[s3]:
                    mycl[s3] = v3

        for r in reads:
            ent = self.res.get(r)
            if ent and ent[0]:
                need(ent[0])
        for r in writes:
            ent = self.res.get(r)
            if ent:
                if ent[0]:
                    need(ent[0])
                for m in ent[1]:
                    need(m)
        self.cnt[eng] += 1
        k = self.cnt[eng]
        mycl[eng] = k
        self.hist[eng].append(dict(mycl))
        wset = set(writes)
        for r in wset:
            self.res[r] = [(eng, k), []]
        for r in reads:
            if r in wset:
                continue
            ent = self.res.setdefault(r, [None, []])
            ent[1].append((eng, k))
        self.steps.append((eng, fn, sorted(waits.items()), tag))

    def emit(self, nc, ctx):
        sems = {e: ctx.enter_context(nc.semaphore(f"sem_{e}"))
                for e in ENGQ}
        by_q = {q: [] for q in QUEUES}
        for st in self.steps:
            by_q[ENGQ[st[0]]].append(st)
        with nc.Block() as blk:
            for q in QUEUES:
                steps = by_q[q]

                def mkbody(steps):
                    def body(e):
                        for (eng, fn, waits, tag) in steps:
                            for (s2, k) in waits:
                                e.wait_ge(sems[s2],
                                          k * (16 if s2 in DMA_ENGS else 1))
                            ins = fn(e)
                            ins.then_inc(sems[eng],
                                         16 if eng in DMA_ENGS else 1)
                    return body
                getattr(blk, q)(mkbody(steps))
        final = {e: self.cnt[e] for e in ENGQ}
        with nc.Block() as blk:
            def fin(e):
                for s2, k in final.items():
                    if k:
                        e.wait_ge(sems[s2], k * (16 if s2 in DMA_ENGS else 1))
            blk.sync(fin)


class P:
    def __init__(self, nc, ctx, trk, dram, dbg=()):
        self.dbg = set(dbg)
        self.nc = nc
        self.ctx = ctx
        self.t = trk
        self.dram = dram
        self.pb = [ctx.enter_context(
            nc.psum_tensor(f"pb{i}", [128, 512], F32)) for i in range(8)]
        self.sbufs = {}

    def sb(self, name, shape, dtype):
        h = self.ctx.enter_context(
            self.nc.sbuf_tensor("sb_" + name, shape, dtype))
        self.sbufs[name] = h[:]
        return self.sbufs[name]

    def view(self, name, arena_ap, col_off_bytes, parts, cols, dtype):
        base = arena_ap.bitcast(dtype)
        esz = 2 if dtype == BF16 else 4
        c0 = col_off_bytes // esz
        self.sbufs[name] = base[0:parts, c0:c0 + cols]
        return self.sbufs[name]

    def tap(self, name, src_ap, reads):
        if name not in self.dbg:
            return
        d = self.dram[name]
        self.t.step("scd", tuple(reads), (),
                    lambda e: e.dma_start(d, src_ap))

    def fence(self, names):
        self.t.step("pool", (), tuple(names), lambda e: e.engine_nop(),
                    "fence")


def nxc(ct, tcol):
    return ct * T + tcol


WT_NAMES = ("wqk0", "wqk1", "wv0", "wv1", "cos2", "sin2", "wo_sb",
            "fc1w0", "fc1w1", "fc2w0", "fc2w1", "gate_sb", "gate_bf",
            "qt_sb", "sc_sb", "wdq0", "wdq1", "sT")
ATT_NAMES = ("attT", "attT0", "attT1", "attT2", "attT3") + tuple(
    f"moe_hT{i}" for i in range(32))
SCR_NAMES = ("w_xin0", "w_xin1", "w_nx0", "w_nx1", "uacc0", "uacc1",
             "qkall", "qknorm", "l1sub",
             "w_z0o0", "w_z0o1", "w_z1o0", "w_z1o1", "w_out0", "w_out1",
             "w_tmp", "w_tmp2", "w_sqscr",
             "st_sb") + tuple(f"stq{i}" for i in range(16)) + (
             "qb", "kb", "qkc", "rot", "ptb0", "ptb1",
             "att2_0", "att2_1", "bmx0", "bmx1")


def emit_init(p):
    t, d = p.t, p.dram
    t.step("syd", (), ("cpool",), lambda e: e.dma_start(
        p.sbufs["cpool"], d["consts"].partition_broadcast(128).squeeze(1)))
    t.step("syd", (), ("identf",),
           lambda e: e.dma_start(p.sbufs["identf"], d["identf"]))
    t.step("syd", (), ("mask_sb",), lambda e: e.dma_start(
        p.sbufs["mask_sb"].rearrange("p (i q) -> p i q", q=QCH),
        d["masks"].transpose([1, 0, 2])))
    t.step("pool", (), ("onesr",),
           lambda e: e.memset(p.sbufs["onesr"], 1.0))
    t.step("dve", ("identf",), ("identb",),
           lambda e: e.tensor_copy(p.sbufs["identb"], p.sbufs["identf"]))


def emit_phase0_cast(p):
    t, d = p.t, p.dram
    for ei in range(NE):
        for fc in range(FFN // 512):
            t.step("gpd", (), (f"fc1bf_{ei}",),
                   (lambda ei, fc: lambda e: e.dma_start(
                       d["f0_fc1_bf"][ei, fc], d["f0_fc1"][ei, fc]))(ei, fc))
        for f8 in range(4):
            t.step("gpd", (), (f"fc2bf_{ei}",),
                   (lambda ei, f8: lambda e: e.dma_start(
                       d["f0_fc2_bf"][ei, f8 * 8:(f8 + 1) * 8],
                       d["f0_fc2"][ei, f8 * 8:(f8 + 1) * 8]))(ei, f8))


def ln_steps(p, src_ap, src_reads, sc, out_fn):
    t = p.t
    stats = p.sbufs["w_st"]
    cpool = p.sbufs["cpool"]
    W = src_ap.shape[-1]
    sq = p.sbufs["w_sq"]

    def f_mu(e):
        return e.reduce_sum(stats[:, sc:sc + 1], src_ap, axis=AX.X)
    t.step("dve", tuple(src_reads), (f"w_st{sc}",), f_mu)

    def f_neg(e):
        return e.tensor_scalar(stats[:, sc + 1:sc + 2], stats[:, sc:sc + 1],
                               -1.0 / W, None, ALU.mult)
    t.step("pool", (f"w_st{sc}",), (f"w_st{sc}n",), f_neg)

    def f_sq(e):
        e.activation(sq[:, 0:W], src_ap, AF.Square,
                     bias=stats[:, sc + 1:sc + 2], scale=1.0,
                     accum_out=stats[:, sc + 2:sc + 3])
        e.activation(stats[:, sc + 3:sc + 4], stats[:, sc + 2:sc + 3],
                     AF.Ln, bias=cpool[:, C_EPS:C_EPS + 1],
                     scale=cpool[:, C_INV1024:C_INV1024 + 1])
        return e.activation(stats[:, sc + 3:sc + 4], stats[:, sc + 3:sc + 4],
                            AF.Exp, bias=0.0,
                            scale=cpool[:, C_NEGHALF:C_NEGHALF + 1])
    t.step("act", tuple(src_reads) + (f"w_st{sc}", f"w_st{sc}n", "cpool"),
           (f"w_st{sc}b", "w_sqscr"), f_sq)
    out_fn(stats[:, sc + 1:sc + 2], stats[:, sc + 3:sc + 4],
           (f"w_st{sc}", f"w_st{sc}b"))


def emit_u_phase(p, src, ud, src_res):
    t = p.t
    xin = p.sbufs["w_xin"]
    out = p.sbufs["w_out"]
    for tt in range(TT):
        sl = tt % 2
        t.step("scd", (f"{src_res}{tt}",), (f"w_xin{sl}",),
               (lambda tt, sl: lambda e: e.dma_start(
                   xin[:, sl * 1024:(sl + 1) * 1024],
                   src[tt * 128:(tt + 1) * 128, :]))(tt, sl))
        src_ap = xin[:, sl * 1024:(sl + 1) * 1024]

        def mko(tt=tt, sl=sl, src_ap=src_ap):
            def out_fn(negmu, rstd, streads):
                def f(e):
                    return e.tensor_scalar(
                        out[:, sl * 1024:(sl + 1) * 1024], src_ap,
                        negmu, rstd, ALU.add, ALU.mult)
                t.step("dve", (f"w_xin{sl}",) + streads, (f"w_out{sl}",), f)
                t.step("scd", (f"w_out{sl}",), (f"ud{tt}",),
                       (lambda tt, sl: lambda e: e.dma_start(
                           ud[tt * 128:(tt + 1) * 128, :],
                           out[:, sl * 1024:(sl + 1) * 1024]))(tt, sl))
            return out_fn
        ln_steps(p, src_ap, (f"w_xin{sl}",), (tt % 4) * 4, mko())


def emit_transposes_tile(p, tt, src_ap, src_res, dstT_name, dst_res=None):
    t = p.t
    identb = p.sbufs["identb"]
    dstT = p.sbufs[dstT_name]
    if dst_res is None:
        dst_res = f"{dstT_name}{tt // 4}"
    pbk = 6 + (tt % 2)
    pt = p.pb[pbk].ap().bitcast(BF16)

    def f_tr(e):
        ins = None
        for ct in range(CT):
            ins = e.transpose(pt[:, ct * 128:(ct + 1) * 128],
                              src_ap[:, ct * 128:(ct + 1) * 128], identb[:])
        return ins
    t.step("pe", (src_res, "identb"), (f"pb{pbk}",), f_tr)

    def f_ev(e):
        ins = None
        for half in range(2):
            outap = dstT.rearrange("p (ct tx) -> p ct tx", tx=T)[
                :, half * 4:(half + 1) * 4, tt * 128:(tt + 1) * 128]
            ins = e.tensor_copy(
                outap, pt[:, half * 512:(half + 1) * 512].rearrange(
                    "p (ct tx) -> p ct tx", tx=128))
        return ins
    t.step("dve", (f"pb{pbk}",), (dst_res,), f_ev)


def emit_nx_phase(p, nxT_name, src_dram, src_res_base, gamma=None, beta=None):
    t = p.t
    nxbf = p.sbufs["w_nx"]
    xin = p.sbufs["w_xin"]
    for tt in range(TT):
        sl = tt % 2
        t.step("scd", (f"{src_res_base}{tt}",), (f"w_xin{sl}",),
               (lambda tt, sl: lambda e: e.dma_start(
                   xin[:, sl * 1024:(sl + 1) * 1024],
                   src_dram[tt * 128:(tt + 1) * 128, :]))(tt, sl))
        src_ap = xin[:, sl * 1024:(sl + 1) * 1024]

        def mko(tt=tt, sl=sl, src_ap=src_ap):
            def out_fn(negmu, rstd, streads):
                if gamma is None:
                    def f(e):
                        return e.tensor_scalar(
                            nxbf[:, sl * 1024:(sl + 1) * 1024], src_ap,
                            negmu, rstd, ALU.add, ALU.mult)
                    t.step("dve", (f"w_xin{sl}",) + streads,
                           (f"w_nx{sl}",), f)
                else:
                    g_tab = p.sbufs[gamma]
                    b_tab = p.sbufs[beta]
                    tmp = p.sbufs["w_tmp"]

                    def f(e):
                        e.tensor_scalar(tmp[:, 0:1024], src_ap, negmu, rstd,
                                        ALU.add, ALU.mult)
                        e.tensor_mul(tmp[:, 0:1024], tmp[:, 0:1024], g_tab)
                        return e.tensor_add(
                            nxbf[:, sl * 1024:(sl + 1) * 1024],
                            tmp[:, 0:1024], b_tab)
                    t.step("dve", (f"w_xin{sl}",) + streads + (gamma, beta),
                           (f"w_nx{sl}", "w_tmp"), f)
            return out_fn
        ln_steps(p, src_ap, (f"w_xin{sl}",), (tt % 4) * 4, mko())
        emit_transposes_tile(p, tt, p.sbufs["w_nx"][:, (tt % 2) * 1024:
                                                    (tt % 2 + 1) * 1024],
                             f"w_nx{tt % 2}", nxT_name)


def emit_attn_heads(p, wp, mode, nxT_name):
    t, d = p.t, p.dram
    nxT = p.sbufs[nxT_name]
    attT = p.sbufs["attT"]
    identb = p.sbufs["identb"]
    identf = p.sbufs["identf"]
    onesr = p.sbufs["onesr"]
    cpool = p.sbufs["cpool"]
    mask_sb = p.sbufs["mask_sb"]
    qb, kb = p.sbufs["qb"], p.sbufs["kb"]
    vgrp = p.sbufs["vgrp"]
    st_sb = p.sbufs["st_sb"]
    bmx = p.sbufs["bmx"]
    chain = p.sbufs["chain"]
    chain_bf = p.sbufs["chain_bf"]
    ptb = p.sbufs["ptb"]
    att2 = p.sbufs["att2"]
    wqk_sb = p.sbufs["wqk_sb"]
    wv_sb = p.sbufs["wv_sb"]
    bias = (mode == "plain")

    if mode == "rope":
        t.step("syd", (), ("cos2",), lambda e: e.dma_start(
            p.sbufs["cos2"], d["ropec"]))
        t.step("syd", (), ("sin2",), lambda e: e.dma_start(
            p.sbufs["sin2"], d["ropes"]))

    def f_ones(e):
        ins = None
        for tt in range(TT):
            for j in range(4):
                ins = e.memset(vgrp[:, tt * 260 + j * 65 + 64:
                                    tt * 260 + j * 65 + 65], 1.0)
        return ins
    t.step("pool", (), ("vgrp", "vgrp_ones"), f_ones)

    for h in range(H):
        wslot = h % 2
        if mode in ("rope", "plain"):
            t.step("gpd", (), (f"wqk{wslot}",),
                   (lambda h, wslot: lambda e: e.dma_start(
                       wqk_sb[:, wslot * 1024:(wslot + 1) * 1024],
                       d[wp + "_wqk"][h]))(h, wslot))
            if h % 4 == 0:
                g = h // 4
                gs = g % 2
                t.step("gpd", (), (f"wv{gs}",),
                       (lambda g, gs: lambda e: e.dma_start(
                           wv_sb[:, gs * 2048:(gs + 1) * 2048],
                           d[wp + "_wv"][g]))(g, gs))
            for chunk in range(4):
                pbk = chunk % 2

                def f_qk(e, h=h, chunk=chunk, pbk=pbk, wslot=wslot):
                    ins = None
                    for ct in range(CT):
                        ins = e.matmul(
                            p.pb[pbk].ap()[:, 0:512],
                            wqk_sb[:, wslot * 1024 + ct * 128:
                                   wslot * 1024 + (ct + 1) * 128],
                            nxT[:, nxc(ct, chunk * 512):
                                nxc(ct, (chunk + 1) * 512)],
                            start=(ct == 0), stop=(ct == CT - 1 and not bias))
                    if bias:
                        ins = e.matmul(
                            p.pb[pbk].ap()[:, 0:512],
                            p.sbufs["bqk_sb"][0:1, h * 128:(h + 1) * 128],
                            onesr[0:1, 0:512], start=False, stop=True)
                    return ins
                rd = (f"wqk{wslot}", f"{nxT_name}{chunk}", "onesr")
                if bias:
                    rd = rd + ("bqk_sb",)
                t.step("pe", rd, (f"pb{pbk}",), f_qk)
                if mode == "rope":
                    qkc = p.sbufs["qkc"]
                    rot = p.sbufs["rot"]

                    def f_rope(e, chunk=chunk, pbk=pbk):
                        cs = slice(chunk * 512, (chunk + 1) * 512)
                        e.tensor_copy(qkc[:, 0:512], p.pb[pbk].ap()[:, 0:512])
                        e.tensor_scalar(rot[0:64, 0:512], qkc[64:128, 0:512],
                                        -1.0, None, ALU.mult)
                        e.tensor_copy(rot[64:128, 0:512], qkc[0:64, 0:512])
                        e.tensor_mul(qkc[:, 0:512], qkc[:, 0:512],
                                     p.sbufs["cos2"][:, cs])
                        e.tensor_mul(rot[:, 0:512], rot[:, 0:512],
                                     p.sbufs["sin2"][:, cs])
                        e.tensor_add(qkc[:, 0:512], qkc[:, 0:512],
                                     rot[:, 0:512])
                        e.tensor_copy(qb[0:32, cs], qkc[0:32, 0:512])
                        e.tensor_copy(kb[0:32, cs], qkc[32:64, 0:512])
                        e.tensor_copy(qb[32:64, cs], qkc[64:96, 0:512])
                        return e.tensor_copy(kb[32:64, cs],
                                             qkc[96:128, 0:512])
                    t.step("dve", (f"pb{pbk}", "cos2", "sin2"),
                           ("qb", "kb", "qkc", "rot"), f_rope)
                else:
                    def f_ev(e, chunk=chunk, pbk=pbk):
                        cs = slice(chunk * 512, (chunk + 1) * 512)
                        e.tensor_copy(qb[:, cs], p.pb[pbk].ap()[0:64, 0:512])
                        return e.tensor_copy(kb[:, cs],
                                             p.pb[pbk].ap()[64:128, 0:512])
                    t.step("dve", (f"pb{pbk}",), ("qb", "kb"), f_ev)
            if h == 0 and mode == "rope":
                p.tap("dbg_qb", p.sbufs["qb"], ("qb",))
                p.tap("dbg_kb", p.sbufs["kb"], ("kb",))
            if h % 4 == 0:
                g = h // 4
                gs = g % 2
                for tt in range(TT):
                    def f_v(e, gs=gs, tt=tt, g=g):
                        ins = None
                        for ct in range(CT):
                            ins = e.matmul(
                                p.pb[2].ap()[:, 0:256],
                                nxT[:, nxc(ct, tt * 128):
                                    nxc(ct, (tt + 1) * 128)],
                                wv_sb[:, gs * 2048 + ct * 256:
                                      gs * 2048 + (ct + 1) * 256],
                                start=(ct == 0),
                                stop=(ct == CT - 1 and not bias))
                        if bias:
                            ins = e.matmul(
                                p.pb[2].ap()[:, 0:256], onesr[0:1, 0:128],
                                p.sbufs["bv_sb"][0:1, g * 256:(g + 1) * 256],
                                start=False, stop=True)
                        return ins
                    rd = (f"wv{gs}", f"{nxT_name}{tt // 4}", "onesr")
                    if bias:
                        rd = rd + ("bv_sb",)
                    t.step("pe", rd, ("pb2",), f_v)

                    def f_vev(e, tt=tt):
                        return e.tensor_copy(
                            vgrp[:, tt * 260:tt * 260 + 260].rearrange(
                                "p (j dd) -> p j dd", dd=65)[:, :, 0:64],
                            p.pb[2].ap()[:, 0:256].rearrange(
                                "p (j dd) -> p j dd", dd=64))
                    t.step("dve", ("pb2", "vgrp_ones"), ("vgrp",), f_vev)
        else:
            qkall = p.sbufs["qkall"]
            qkn = p.sbufs["qknorm"]
            lntab = p.sbufs["l1lntab"]
            t.step("gpd", (), (f"wqk{wslot}",),
                   (lambda h, wslot: lambda e: e.dma_start(
                       wqk_sb[:, wslot * 1536:(wslot + 1) * 1536],
                       d["a1_wqkv"][h]))(h, wslot))
            for tt in range(TT):
                pbk = 3 + (tt % 2)

                def f_qkv(e, tt=tt, pbk=pbk, wslot=wslot):
                    ins = None
                    for ct in range(CT):
                        ins = e.matmul(
                            p.pb[pbk].ap()[:, 0:192],
                            nxT[:, nxc(ct, tt * 128):nxc(ct, (tt + 1) * 128)],
                            wqk_sb[:, wslot * 1536 + ct * 192:
                                   wslot * 1536 + (ct + 1) * 192],
                            start=(ct == 0), stop=(ct == CT - 1))
                    return ins
                t.step("pe", (f"wqk{wslot}", f"{nxT_name}{tt // 4}"),
                       (f"pb{pbk}",), f_qkv)

                def f_ev(e, tt=tt, pbk=pbk, h=h):
                    e.tensor_copy(qkall[:, tt * 128:(tt + 1) * 128],
                                  p.pb[pbk].ap()[:, 0:128])
                    return e.tensor_copy(
                        vgrp[:, tt * 260 + (h % 4) * 65:
                             tt * 260 + (h % 4) * 65 + 64],
                        p.pb[pbk].ap()[:, 128:192])
                t.step("dve", (f"pb{pbk}", "vgrp_ones"), ("qkall", "vgrp"),
                       f_ev)

            def f_mu(e):
                return e.reduce_sum(
                    p.sbufs["l1mu"][:, 0:32],
                    qkall.rearrange("p (g dd) -> p g dd", dd=64), axis=AX.X)
            t.step("dve", ("qkall",), ("l1mu",), f_mu)

            def f_mul(e):
                return e.activation(p.sbufs["l1mu"][:, 32:64],
                                    p.sbufs["l1mu"][:, 0:32], AF.Copy,
                                    bias=0.0,
                                    scale=cpool[:, C_MEAN64:C_MEAN64 + 1])
            t.step("act", ("l1mu", "cpool"), ("l1mu2",), f_mul)

            def f_sub(e):
                mub = p.sbufs["l1mu"][:, 32:64].unsqueeze(2).broadcast_to(
                    (128, 32, 64))
                e.tensor_tensor(
                    p.sbufs["l1sub"].rearrange("p (g dd) -> p g dd", dd=64),
                    qkall.rearrange("p (g dd) -> p g dd", dd=64),
                    mub, ALU.subtract)
                e.tensor_mul(p.sbufs["w_sq"][:, 0:1024],
                             p.sbufs["l1sub"][:, 0:1024],
                             p.sbufs["l1sub"][:, 0:1024])
                e.tensor_mul(p.sbufs["w_tmp"][:, 0:1024],
                             p.sbufs["l1sub"][:, 1024:2048],
                             p.sbufs["l1sub"][:, 1024:2048])
                e.reduce_sum(
                    p.sbufs["l1var"][:, 0:16],
                    p.sbufs["w_sq"][:, 0:1024].rearrange(
                        "p (g dd) -> p g dd", dd=64), axis=AX.X)
                return e.reduce_sum(
                    p.sbufs["l1var"][:, 16:32],
                    p.sbufs["w_tmp"][:, 0:1024].rearrange(
                        "p (g dd) -> p g dd", dd=64), axis=AX.X)
            t.step("dve", ("qkall", "l1mu2"),
                   ("l1sub", "l1var", "w_sqscr", "w_tmp"), f_sub)

            def f_rstd(e):
                e.activation(p.sbufs["l1var"][:, 0:32],
                             p.sbufs["l1var"][:, 0:32], AF.Ln,
                             bias=cpool[:, C_EPS:C_EPS + 1],
                             scale=cpool[:, C_INV64:C_INV64 + 1])
                return e.activation(p.sbufs["l1var"][:, 0:32],
                                    p.sbufs["l1var"][:, 0:32], AF.Exp,
                                    bias=0.0,
                                    scale=cpool[:, C_NEGHALF:C_NEGHALF + 1])
            t.step("act", ("l1var", "cpool"), ("l1var",), f_rstd)

            def f_norm(e):
                rb = p.sbufs["l1var"][:, 0:32].unsqueeze(2).broadcast_to(
                    (128, 32, 64))
                e.tensor_tensor(
                    p.sbufs["l1sub"].rearrange("p (g dd) -> p g dd", dd=64),
                    p.sbufs["l1sub"].rearrange("p (g dd) -> p g dd", dd=64),
                    rb, ALU.mult)
                gtab = lntab[:, 0:128].unsqueeze(1).broadcast_to(
                    (128, 16, 128))
                e.tensor_tensor(
                    p.sbufs["l1sub"].rearrange("p (g dd) -> p g dd", dd=128),
                    p.sbufs["l1sub"].rearrange("p (g dd) -> p g dd", dd=128),
                    gtab, ALU.mult)
                btab = lntab[:, 128:256].unsqueeze(1).broadcast_to(
                    (128, 16, 128))
                return e.tensor_tensor(
                    qkn.rearrange("p (g dd) -> p g dd", dd=128),
                    p.sbufs["l1sub"].rearrange("p (g dd) -> p g dd", dd=128),
                    btab, ALU.add)
            t.step("dve", ("l1sub", "l1var", "l1lntab"), ("qknorm", "l1sub"),
                   f_norm)
            for tt in range(TT):
                pbk = 6 + (tt % 2)
                ptq = p.pb[pbk].ap().bitcast(BF16)

                def f_tr(e, tt=tt, ptq=ptq):
                    return e.transpose(ptq[:, 0:128],
                                       qkn[:, tt * 128:(tt + 1) * 128],
                                       identb[:])
                t.step("pe", ("qknorm", "identb"), (f"pb{pbk}",), f_tr)

                def f_cp(e, tt=tt, ptq=ptq):
                    e.tensor_copy(qb[:, tt * 128:(tt + 1) * 128],
                                  ptq[0:64, 0:128])
                    return e.tensor_copy(kb[:, tt * 128:(tt + 1) * 128],
                                         ptq[64:128, 0:128])
                t.step("dve", (f"pb{pbk}",), ("qb", "kb"), f_cp)

        OB = {0: 5, 1: 2}

        def em_score(qc, i):
            pbk = 3 + (i % 2)

            def f_s(e, i=i, qc=qc, pbk=pbk):
                return e.matmul(p.pb[pbk].ap()[:, 0:QCH],
                                kb[:, i * 128:(i + 1) * 128],
                                qb[:, qc * QCH:(qc + 1) * QCH],
                                start=True, stop=True)
            t.step("pe", ("qb", "kb"), (f"pb{pbk}",), f_s)

        def em_evac(qc, i):
            pbk = 3 + (i % 2)
            cp = qc % 2

            def f_sev(e, i=i, pbk=pbk):
                return e.tensor_copy(st_sb[:, i * QCH:(i + 1) * QCH],
                                     p.pb[pbk].ap()[:, 0:QCH])
            t.step("dve", (f"pb{pbk}",), (f"stq{i}",), f_sev)

            def f_bm(e, i=i, cp=cp):
                return e.reduce_max(bmx[:, cp * 16 + i:cp * 16 + i + 1],
                                    st_sb[:, i * QCH:(i + 1) * QCH],
                                    axis=AX.X)
            t.step("dve", (f"stq{i}",), (f"bmx{cp}",), f_bm)

        def em_chain(qc):
            nbl = 2 * qc + 2
            cp = qc % 2
            ob = OB[cp]
            t.step("act", (f"bmx{cp}",), (f"bmx2_{cp}",),
                   (lambda nbl, cp: lambda e: e.activation(
                       p.sbufs["bmx2"][:, cp * 16:cp * 16 + nbl],
                       bmx[:, cp * 16:cp * 16 + nbl], AF.Copy,
                       bias=0.0, scale=1.0))(nbl, cp))
            t.step("dve", (f"bmx2_{cp}",), (f"chcm{cp}",),
                   (lambda nbl, cp: lambda e: e.reduce_max(
                       chain[:, cp:cp + 1],
                       p.sbufs["bmx2"][:, cp * 16:cp * 16 + nbl],
                       axis=AX.X))(nbl, cp))
            t.step("pe", (f"chcm{cp}", "identf"), (f"pb{ob}",),
                   (lambda cp, ob: lambda e: e.transpose(
                       p.pb[ob].ap()[0:1, 256:384], chain[:, cp:cp + 1],
                       identf[:]))(cp, ob))
            t.step("dve", (f"pb{ob}",), (f"chred{cp}",),
                   (lambda cp, ob: lambda e: e.reduce_max(
                       chain[0:1, 2 + cp:3 + cp],
                       p.pb[ob].ap()[0:1, 256:384],
                       axis=AX.X, negate=True))(cp, ob))
            t.step("act", (f"chred{cp}", "cpool"), (f"chbf{cp}",),
                   (lambda cp: lambda e: e.activation(
                       chain_bf[0:1, cp:cp + 1], chain[0:1, 2 + cp:3 + cp],
                       AF.Copy, bias=0.0,
                       scale=cpool[0:1, C_ATTSCL:C_ATTSCL + 1]))(cp))
            t.step("pe", (f"chbf{cp}", "onesr"), (f"pb{ob}",),
                   (lambda cp, ob: lambda e: e.matmul(
                       p.pb[ob].ap()[:, 440:441], onesr[0:1, 0:128],
                       chain_bf[0:1, cp:cp + 1],
                       start=True, stop=True))(cp, ob))
            t.step("dve", (f"pb{ob}",), (f"chainM{cp}",),
                   (lambda cp, ob: lambda e: e.tensor_copy(
                       chain[:, 4 + cp:5 + cp],
                       p.pb[ob].ap()[:, 440:441]))(cp, ob))

        def em_expav(qc, i, first_av):
            nbl = 2 * qc + 2
            cp = qc % 2
            ob = OB[cp]
            psl = i % 2

            def f_exp(e, i=i, psl=psl, cp=cp):
                return e.activation(ptb[:, psl * QCH:(psl + 1) * QCH],
                                    st_sb[:, i * QCH:(i + 1) * QCH],
                                    AF.Exp, bias=chain[:, 4 + cp:5 + cp],
                                    scale=cpool[:, C_ATTSCL:C_ATTSCL + 1])
            t.step("act", (f"stq{i}", f"chainM{cp}", "cpool"),
                   (f"ptb{psl}",), f_exp)
            if i >= nbl - 2:
                mi = 0 if i == nbl - 2 else 1

                def f_msk(e, psl=psl, mi=mi):
                    return e.tensor_mul(
                        ptb[:, psl * QCH:(psl + 1) * QCH],
                        ptb[:, psl * QCH:(psl + 1) * QCH],
                        mask_sb[:, mi * QCH:(mi + 1) * QCH])
                t.step("dve", ("mask_sb", f"ptb{psl}"), (f"ptb{psl}",),
                       f_msk)

            def f_av(e, i=i, psl=psl, nbl=nbl, h=h, ob=ob):
                ins = None
                st0 = first_av[0]
                for qs in range(2):
                    if i == nbl - 1 and qs == 0:
                        continue
                    ins = e.matmul(
                        p.pb[ob].ap()[:, qs * 65:qs * 65 + 65],
                        ptb[:, psl * QCH + qs * 128:
                            psl * QCH + (qs + 1) * 128],
                        vgrp[:, i * 260 + (h % 4) * 65:
                             i * 260 + (h % 4) * 65 + 65],
                        start=st0 and (qs == (0 if i < nbl - 1 else 1)),
                        stop=(i == nbl - 1 and qs == 1))
                first_av[0] = False
                return ins
            t.step("pe", (f"ptb{psl}", "vgrp"), (f"pb{ob}",), f_av)

        def em_ofinal(qc):
            cp = qc % 2
            ob = OB[cp]
            for qs in range(2):
                def f_rec(e, qs=qs, cp=cp, ob=ob):
                    e.activation(chain[:, 6 + 2 * cp + qs:7 + 2 * cp + qs],
                                 p.pb[ob].ap()[:, qs * 65 + 64:qs * 65 + 65],
                                 AF.Ln, bias=0.0, scale=1.0)
                    return e.activation(
                        chain[:, 6 + 2 * cp + qs:7 + 2 * cp + qs],
                        chain[:, 6 + 2 * cp + qs:7 + 2 * cp + qs], AF.Exp,
                        bias=0.0, scale=-1.0)
                t.step("act", (f"pb{ob}",), (f"rec{cp}{qs}",), f_rec)

                def f_on(e, qs=qs, cp=cp, ob=ob):
                    return e.tensor_scalar(
                        att2[:, qs * 64:(qs + 1) * 64],
                        p.pb[ob].ap()[:, qs * 65:qs * 65 + 64],
                        chain[:, 6 + 2 * cp + qs:7 + 2 * cp + qs],
                        None, ALU.mult)
                t.step("dve", (f"pb{ob}", f"rec{cp}{qs}"), (f"att2_{qs}",),
                       f_on)
                pbk = 6 + qs
                pta = p.pb[pbk].ap().bitcast(BF16)

                def f_atr(e, qs=qs, pta=pta):
                    return e.transpose(pta[0:64, 0:128],
                                       att2[:, qs * 64:(qs + 1) * 64],
                                       identb[:])
                t.step("pe", (f"att2_{qs}", "identb"), (f"pb{pbk}",), f_atr)

                def f_acp(e, qs=qs, qc=qc, h=h, pta=pta):
                    tt = qc * 2 + qs
                    pbase = (h % 2) * 64
                    return e.tensor_copy(
                        attT[pbase:pbase + 64,
                             (h // 2) * T + tt * 128:
                             (h // 2) * T + (tt + 1) * 128], pta[0:64, 0:128])
                t.step("dve", (f"pb{pbk}",), ("attT",), f_acp)

        for i in range(2):
            em_score(0, i)
            em_evac(0, i)
        em_chain(0)
        for qc in range(1, NQC):
            nprev = 2 * qc
            ncur = 2 * qc + 2
            fav = [True]
            for i in range(ncur):
                em_score(qc, i)
                if i < nprev:
                    em_expav(qc - 1, i, fav)
                em_evac(qc, i)
            em_ofinal(qc - 1)
            em_chain(qc)
        fav = [True]
        for i in range(2 * NQC):
            em_expav(NQC - 1, i, fav)
        em_ofinal(NQC - 1)


def emit_proj(p, wo_name, bo_name, epilogue, src_dram=None, dst_dram=None,
              ud=None, src_res=None, dst_res=None):
    t = p.t
    attT = p.sbufs["attT"]
    wo_sb = p.sbufs[wo_name]
    bo_sb = p.sbufs[bo_name] if bo_name else None
    onesr = p.sbufs["onesr"]
    uacc = p.sbufs["uacc"]
    zsb = p.sbufs["w_z"]

    for tt in range(TT):
        usl = tt % 2
        if epilogue == "accum_u":
            t.step("scd", (f"ud{tt}",), (f"uacc{usl}",),
                   (lambda tt, usl: lambda e: e.dma_start(
                       uacc[:, usl * 1024:(usl + 1) * 1024],
                       ud[tt * 128:(tt + 1) * 128, :]))(tt, usl))
        for oc in range(2):
            pbk = oc

            def f_p(e, tt=tt, oc=oc, pbk=pbk):
                ins = None
                for ct in range(CT):
                    ins = e.matmul(
                        p.pb[pbk].ap()[:, 0:512],
                        attT[:, ct * T + tt * 128:ct * T + (tt + 1) * 128],
                        wo_sb[:, ct * 1024 + oc * 512:
                              ct * 1024 + (oc + 1) * 512],
                        start=(ct == 0),
                        stop=(ct == CT - 1 and bo_sb is None))
                if bo_sb is not None:
                    ins = e.matmul(p.pb[pbk].ap()[:, 0:512], onesr[0:1, 0:128],
                                   bo_sb[0:1, oc * 512:(oc + 1) * 512],
                                   start=False, stop=True)
                return ins
            rd = ("attT", wo_name, "onesr")
            if bo_name:
                rd = rd + (bo_name,)
            t.step("pe", rd, (f"pb{pbk}",), f_p)
            if epilogue == "accum_u":
                def f_a(e, oc=oc, pbk=pbk, usl=usl):
                    sl = slice(usl * 1024 + oc * 512,
                               usl * 1024 + (oc + 1) * 512)
                    return e.tensor_add(uacc[:, sl], uacc[:, sl],
                                        p.pb[pbk].ap()[:, 0:512])
                t.step("dve", (f"pb{pbk}", f"uacc{usl}"), (f"uacc{usl}",),
                       f_a)
            else:
                def f_z(e, tt=tt, oc=oc, pbk=pbk):
                    return e.tensor_copy(
                        zsb[:, (tt % 2) * 1024 + oc * 512:
                            (tt % 2) * 1024 + (oc + 1) * 512],
                        p.pb[pbk].ap()[:, 0:512])
                t.step("dve", (f"pb{pbk}",), (f"w_z{tt % 2}o{oc}",), f_z)
        if epilogue == "accum_u":
            t.step("scd", (f"uacc{usl}",), (f"ud{tt}",),
                   (lambda tt, usl: lambda e: e.dma_start(
                       ud[tt * 128:(tt + 1) * 128, :],
                       uacc[:, usl * 1024:(usl + 1) * 1024]))(tt, usl))
        else:
            emit_sandwich_tile(
                p, tt, zsb[:, (tt % 2) * 1024:(tt % 2 + 1) * 1024],
                (f"w_z{tt % 2}o0", f"w_z{tt % 2}o1"), src_dram, dst_dram,
                src_res, dst_res)


def emit_sandwich_tile(p, tt, z_ap, z_reads, src_dram, dst_dram,
                       src_res, dst_res):
    t = p.t
    xin = p.sbufs["w_xin"]
    out_sb = p.sbufs["w_out"]
    sl = tt % 2

    t.step("scd", (f"{src_res}{tt}",), (f"w_xin{sl}",),
           (lambda tt, sl: lambda e: e.dma_start(
               xin[:, sl * 1024:(sl + 1) * 1024],
               src_dram[tt * 128:(tt + 1) * 128, :]))(tt, sl))

    def out_fn(negmu, rstd, streads):
        def f(e):
            e.tensor_scalar(out_sb[:, sl * 1024:(sl + 1) * 1024], z_ap,
                            negmu, rstd, ALU.add, ALU.mult)
            return e.tensor_add(out_sb[:, sl * 1024:(sl + 1) * 1024],
                                out_sb[:, sl * 1024:(sl + 1) * 1024],
                                xin[:, sl * 1024:(sl + 1) * 1024])
        t.step("dve", tuple(z_reads) + streads + (f"w_xin{sl}",),
               (f"w_out{sl}",), f)
        t.step("scd", (f"w_out{sl}",), (f"{dst_res}{tt}",),
               (lambda tt, sl: lambda e: e.dma_start(
                   dst_dram[tt * 128:(tt + 1) * 128, :],
                   out_sb[:, sl * 1024:(sl + 1) * 1024]))(tt, sl))
    ln_steps(p, z_ap, tuple(z_reads), (tt % 4) * 4, out_fn)


def emit_u_sandwich(p, src, dst, ud, src_res, dst_res):
    zsb = p.sbufs["w_z"]
    t = p.t
    for tt in range(TT):
        zl = tt % 2
        t.step("scd", (f"ud{tt}",), (f"w_z{zl}o0", f"w_z{zl}o1"),
               (lambda tt, zl: lambda e: e.dma_start(
                   zsb[:, zl * 1024:(zl + 1) * 1024],
                   ud[tt * 128:(tt + 1) * 128, :]))(tt, zl))
        emit_sandwich_tile(p, tt, zsb[:, zl * 1024:(zl + 1) * 1024],
                           (f"w_z{zl}o0", f"w_z{zl}o1"), src, dst,
                           src_res, dst_res)


def emit_moe(p, nxT_name, ud):
    t, d = p.t, p.dram
    nxT = p.sbufs[nxT_name]
    onesr = p.sbufs["onesr"]
    cpool = p.sbufs["cpool"]
    gl = p.sbufs["moe_gl"]
    gw = p.sbufs["moe_gw"]
    gate_sb = p.sbufs["gate_sb"]
    gate_bf = p.sbufs["gate_bf"]
    fc1sb = p.sbufs["fc1sb"]
    fc2sb = p.sbufs["fc2sb"]
    fc1b_sb = p.sbufs["fc1b_sb"]
    fc2b_sb = p.sbufs["fc2b_sb"]
    fc2b_bf = p.sbufs["fc2b_bf"]
    hT = p.sbufs["moe_hT"]
    uacc = p.sbufs["uacc"]

    t.step("syd", (), ("gate_sb",),
           lambda e: e.dma_start(gate_sb, d["f0_gate"]))
    t.step("dve", ("gate_sb",), ("gate_bf",),
           lambda e: e.tensor_copy(gate_bf, gate_sb))

    for tt in range(TT):
        def f_g(e, tt=tt):
            ins = None
            for ct in range(CT):
                ins = e.matmul(
                    p.pb[2].ap()[:, tt * 8:(tt + 1) * 8],
                    nxT[:, nxc(ct, tt * 128):nxc(ct, (tt + 1) * 128)],
                    gate_bf[:, ct * 8:(ct + 1) * 8],
                    start=(ct == 0 and tt == 0),
                    stop=(ct == CT - 1 and tt == TT - 1))
            return ins
        t.step("pe", (f"{nxT_name}{tt // 4}", "gate_bf"), ("pb2",), f_g)
    t.step("dve", ("pb2",), ("moe_gl",),
           lambda e: e.tensor_copy(gl[:, 0:128], p.pb[2].ap()[:, 0:128]))

    def f_m1(e):
        return e.reduce_max(gw[:, 0:16],
                            gl[:, 0:128].rearrange("p (g k) -> p g k", k=8),
                            axis=AX.X)
    t.step("dve", ("moe_gl",), ("moe_gw",), f_m1)
    t.step("act", ("moe_gw",), ("moe_m1",),
           lambda e: e.activation(gw[:, 16:32], gw[:, 0:16], AF.Copy,
                                  bias=0.0, scale=1.0))

    def f_eq1(e):
        m1b = gw[:, 16:32].unsqueeze(2).broadcast_to((128, 16, 8))
        e.tensor_tensor(gw[:, 128:256].rearrange("p (g k) -> p g k", k=8),
                        gl[:, 0:128].rearrange("p (g k) -> p g k", k=8),
                        m1b, ALU.is_equal)
        e.scalar_tensor_tensor(gw[:, 256:384], gw[:, 128:256], -1e30,
                               gl[:, 0:128], ALU.mult, ALU.add)
        return e.reduce_max(gw[:, 32:48],
                            gw[:, 256:384].rearrange("p (g k) -> p g k", k=8),
                            axis=AX.X)
    t.step("dve", ("moe_gl", "moe_m1"), ("moe_gw",), f_eq1)
    t.step("act", ("moe_gw",), ("moe_m2",),
           lambda e: e.activation(gw[:, 48:64], gw[:, 32:48], AF.Copy,
                                  bias=0.0, scale=1.0))
    t.step("dve", ("moe_m1", "moe_m2"), ("moe_d",),
           lambda e: e.tensor_sub(gw[:, 64:80], gw[:, 48:64], gw[:, 16:32]))

    def f_pa(e):
        e.activation(gw[:, 80:96], gw[:, 64:80], AF.Exp, bias=0.0, scale=1.0)
        e.activation(gw[:, 96:112], gw[:, 80:96], AF.Ln,
                     bias=cpool[:, C_ONE:C_ONE + 1], scale=1.0)
        return e.activation(gw[:, 96:112], gw[:, 96:112], AF.Exp,
                            bias=0.0, scale=-1.0)
    t.step("act", ("moe_d", "cpool"), ("moe_gw2",), f_pa)

    def f_dg(e):
        e.tensor_mul(gw[:, 112:128], gw[:, 80:96], gw[:, 96:112])
        m2b = gw[:, 48:64].unsqueeze(2).broadcast_to((128, 16, 8))
        e.tensor_tensor(gw[:, 384:512].rearrange("p (g k) -> p g k", k=8),
                        gl[:, 0:128].rearrange("p (g k) -> p g k", k=8),
                        m2b, ALU.is_equal)
        p1b = gw[:, 96:112].unsqueeze(2).broadcast_to((128, 16, 8))
        e.tensor_tensor(gw[:, 128:256].rearrange("p (g k) -> p g k", k=8),
                        gw[:, 128:256].rearrange("p (g k) -> p g k", k=8),
                        p1b, ALU.mult)
        p2b = gw[:, 112:128].unsqueeze(2).broadcast_to((128, 16, 8))
        e.tensor_tensor(gw[:, 384:512].rearrange("p (g k) -> p g k", k=8),
                        gw[:, 384:512].rearrange("p (g k) -> p g k", k=8),
                        p2b, ALU.mult)
        return e.tensor_add(gl[:, 128:256], gw[:, 128:256], gw[:, 384:512])
    t.step("dve", ("moe_gl", "moe_gw", "moe_gw2", "moe_m1", "moe_m2"),
           ("moe_gw", "moe_g"), f_dg)

    for ei in range(NE):
        t.step("syd", (), ("fc1b_sb",),
               (lambda ei: lambda e: e.dma_start(
                   fc1b_sb, d["f0_fc1b"][ei]))(ei))
        t.step("syd", (), ("fc2b_sb",),
               (lambda ei: lambda e: e.dma_start(
                   fc2b_sb, d["f0_fc2b"][ei]))(ei))
        t.step("dve", ("fc2b_sb",), ("fc2b_bf",),
               lambda e: e.tensor_copy(fc2b_bf, fc2b_sb))
        for tc in range(4):
            for fch in range(FFN // 512):
                wsl = fch % 2
                t.step("syd", (f"fc1bf_{ei}",), (f"fc1w{wsl}",),
                       (lambda ei, fch, wsl: lambda e: e.dma_start(
                           fc1sb[:, wsl * 4096:(wsl + 1) * 4096],
                           d["f0_fc1_bf"][ei, fch]))(ei, fch, wsl))
                for ft in range(4):
                    fglob = fch * 4 + ft
                    pbk = 3 + (fglob % 2)

                    def f_f1(e, tc=tc, wsl=wsl, ft=ft, pbk=pbk):
                        ins = None
                        for ct in range(CT):
                            ins = e.matmul(
                                p.pb[pbk].ap()[:, 0:512],
                                fc1sb[:, wsl * 4096 + ct * 512 + ft * 128:
                                      wsl * 4096 + ct * 512 + (ft + 1) * 128],
                                nxT[:, nxc(ct, tc * 512):
                                    nxc(ct, (tc + 1) * 512)],
                                start=(ct == 0), stop=(ct == CT - 1))
                        return ins
                    t.step("pe", (f"fc1w{wsl}", f"{nxT_name}{tc}"),
                           (f"pb{pbk}",), f_f1)

                    def f_g1(e, fglob=fglob, pbk=pbk):
                        return e.activation(
                            hT[:, fglob * 512:(fglob + 1) * 512],
                            p.pb[pbk].ap()[:, 0:512], AF.Gelu,
                            bias=fc1b_sb[:, fglob:fglob + 1], scale=1.0)
                    t.step("act", (f"pb{pbk}", "fc1b_sb"),
                           (f"moe_hT{fglob}",), f_g1)
            for ft in range(FFN // 128):
                wsl = (ft // 2) % 2
                if ft % 2 == 0:
                    t.step("scd", (f"fc2bf_{ei}",), (f"fc2w{wsl}",),
                           (lambda ei, ft, wsl: lambda e: e.dma_start(
                               fc2sb[:, wsl * 2048:(wsl + 1) * 2048]
                               .rearrange("p (j c) -> p j c", c=1024),
                               d["f0_fc2_bf"][ei, ft:ft + 2]
                               .transpose([1, 0, 2])))(ei, ft, wsl))

                def f_f2(e, ft=ft, wsl=wsl):
                    ins = None
                    off = wsl * 2048 + (ft % 2) * 1024
                    for ts in range(4):
                        for oc in range(2):
                            ins = e.matmul(
                                p.pb[ts * 2 + oc].ap()[:, 0:512],
                                hT[:, ft * 512 + ts * 128:
                                   ft * 512 + (ts + 1) * 128],
                                fc2sb[:, off + oc * 512:off + (oc + 1) * 512],
                                start=(ft == 0), stop=False)
                    return ins
                t.step("pe", (f"fc2w{wsl}", f"moe_hT{ft}"),
                       tuple(f"pb{bk}" for bk in range(8)), f_f2)

            def f_b2(e, ei=ei):
                ins = None
                for ts in range(4):
                    for oc in range(2):
                        ins = e.matmul(
                            p.pb[ts * 2 + oc].ap()[:, 0:512],
                            onesr[0:1, 0:128],
                            fc2b_bf[0:1, oc * 512:(oc + 1) * 512],
                            start=False, stop=True)
                return ins
            t.step("pe", ("fc2b_bf", "onesr"),
                   tuple(f"pb{bk}" for bk in range(8)), f_b2)
            for ts in range(4):
                tt = tc * 4 + ts
                usl = ts % 2
                t.step("scd", (f"ud{tt}",), (f"uacc{usl}",),
                       (lambda tt, usl: lambda e: e.dma_start(
                           uacc[:, usl * 1024:(usl + 1) * 1024],
                           ud[tt * 128:(tt + 1) * 128, :]))(tt, usl))
                for oc in range(2):
                    bk = ts * 2 + oc

                    def f_u(e, bk=bk, tt=tt, oc=oc, ei=ei, usl=usl):
                        sl = slice(usl * 1024 + oc * 512,
                                   usl * 1024 + (oc + 1) * 512)
                        return e.scalar_tensor_tensor(
                            uacc[:, sl], p.pb[bk].ap()[:, 0:512],
                            gl[:, 128 + tt * 8 + ei:129 + tt * 8 + ei],
                            uacc[:, sl], ALU.mult, ALU.add)
                    t.step("dve", (f"pb{bk}", "moe_g", f"uacc{usl}"),
                           (f"uacc{usl}",), f_u)
                t.step("scd", (f"uacc{usl}",), (f"ud{tt}",),
                       (lambda tt, usl: lambda e: e.dma_start(
                           ud[tt * 128:(tt + 1) * 128, :],
                           uacc[:, usl * 1024:(usl + 1) * 1024]))(tt, usl))


def emit_qlora_ffn(p, ud):
    t, d = p.t, p.dram
    onesr = p.sbufs["onesr"]
    identb = p.sbufs["identb"]
    hT = p.sbufs["nxT"]
    ygT = p.sbufs["attT"]
    qt_sb = p.sbufs["qt_sb"]
    sc_sb = p.sbufs["sc_sb"]
    wdq = p.sbufs["wdq"]
    la_sb = p.sbufs["la_sb"]
    lb_sb = p.sbufs["lb_sb"]
    qbias = p.sbufs["qbias"]
    qbias_f = p.sbufs["qbias_f"]
    sT = p.sbufs["sT"]
    g_tab = p.sbufs["gb_g"]
    b_tab = p.sbufs["gb_b"]
    zsb = p.sbufs["w_z"]
    sq = p.sbufs["w_sq"]
    ybf = p.sbufs["w_nx"]
    uacc = p.sbufs["uacc"]
    nxbf = p.sbufs["w_nx"]

    t.step("gpd", (), ("la_sb",), lambda e: e.dma_start(
        la_sb.rearrange("p (i r) -> p i r", r=CT * LORA_R),
        d["q1_la"].transpose([1, 0, 2])))

    def cast_and_T():
        for tt in range(TT):
            sl = tt % 2
            t.step("gpd", (f"ud{tt}",), (f"w_nx{sl}",),
                   (lambda tt, sl: lambda e: e.dma_start(
                       nxbf[:, sl * 1024:(sl + 1) * 1024],
                       ud[tt * 128:(tt + 1) * 128, :]))(tt, sl))
            emit_transposes_tile(
                p, tt, nxbf[:, sl * 1024:(sl + 1) * 1024], f"w_nx{sl}",
                "nxT")

    cast_and_T()

    for blk in range(3):
        t.step("syd", (), ("gb_g",),
               (lambda blk: lambda e: e.dma_start(
                   g_tab,
                   d["q1_lng"][blk].partition_broadcast(128).squeeze(1)))(blk))
        t.step("syd", (), ("gb_b",),
               (lambda blk: lambda e: e.dma_start(
                   b_tab,
                   d["q1_lnb"][blk].partition_broadcast(128).squeeze(1)))(blk))
        for half in range(2):
            mi = 2 * blk + half
            inT, inT_name = (hT, "nxT") if half == 0 else (ygT, "attT")
            t.step("gpd", (), ("lb_sb",),
                   (lambda mi: lambda e: e.dma_start(
                       lb_sb, d["q1_lb"][mi]))(mi))
            t.step("syd", (), ("qbias_f",),
                   (lambda mi: lambda e: e.dma_start(
                       qbias_f, d["q1_b"][mi]))(mi))
            t.step("dve", ("qbias_f",), ("qbias",),
                   lambda e: e.tensor_copy(qbias, qbias_f))
            for chunk in range(4):
                def f_sl(e, mi=mi, chunk=chunk, inT=inT):
                    ins = None
                    for ct in range(CT):
                        ins = e.matmul(
                            p.pb[2].ap()[0:32, 0:512],
                            la_sb[:, mi * (CT * LORA_R) + ct * LORA_R:
                                  mi * (CT * LORA_R) + (ct + 1) * LORA_R],
                            inT[:, nxc(ct, chunk * 512):
                                nxc(ct, (chunk + 1) * 512)],
                            start=(ct == 0), stop=(ct == CT - 1))
                    return ins
                t.step("pe", ("la_sb", f"{inT_name}{chunk}"), ("pb2",),
                       f_sl)

                def f_se(e, chunk=chunk):
                    return e.tensor_copy(
                        sT[0:32, chunk * 512:(chunk + 1) * 512],
                        p.pb[2].ap()[0:32, 0:512])
                t.step("dve", ("pb2",), ("sT",), f_se)
            for oc in range(2):
                for hf in range(2):
                    t.step("syd", (), ("qt_sb",),
                           (lambda mi, oc, hf: lambda e: e.dma_start(
                               qt_sb, d["q1_qt"][mi, oc][
                                   :, hf * 2048:(hf + 1) * 2048]))(mi, oc, hf))
                    t.step("syd", (), ("sc_sb",),
                           (lambda mi, oc, hf: lambda e: e.dma_start(
                               sc_sb, d["q1_sc"][mi, oc][
                                   :, hf * 2048:(hf + 1) * 2048]))(mi, oc, hf))

                    def f_dq(e, oc=oc, hf=hf):
                        ins = None
                        for dt4 in range(4):
                            slc = slice(dt4 * 512, (dt4 + 1) * 512)
                            e.tensor_copy(sq[:, 0:512], qt_sb[:, slc])
                            e.tensor_scalar(sq[:, 512:1024], sq[:, 0:512],
                                            1.0 / 7.5, -1.0, ALU.mult,
                                            ALU.add)
                            ins = e.tensor_mul(
                                wdq[:, oc * 4096 + hf * 2048 + dt4 * 512:
                                    oc * 4096 + hf * 2048 + (dt4 + 1) * 512],
                                sq[:, 512:1024], sc_sb[:, slc])
                        return ins
                    t.step("dve", ("qt_sb", "sc_sb"),
                           (f"wdq{oc}", "w_sqscr"), f_dq)
            for tt in range(TT):
                usl = tt % 2
                if half == 1:
                    t.step("scd", (f"ud{tt}",), (f"uacc{usl}",),
                           (lambda tt, usl: lambda e: e.dma_start(
                               uacc[:, usl * 1024:(usl + 1) * 1024],
                               ud[tt * 128:(tt + 1) * 128, :]))(tt, usl))
                for oc in range(2):
                    pbk = oc

                    def f_gm(e, mi=mi, oc=oc, tt=tt, pbk=pbk, inT=inT):
                        for ct in range(CT):
                            e.matmul(
                                p.pb[pbk].ap()[:, 0:512],
                                inT[:, nxc(ct, tt * 128):
                                    nxc(ct, (tt + 1) * 128)],
                                wdq[:, oc * 4096 + ct * 512:
                                    oc * 4096 + (ct + 1) * 512],
                                start=(ct == 0), stop=False)
                        e.matmul(
                            p.pb[pbk].ap()[:, 0:512],
                            sT[0:32, tt * 128:(tt + 1) * 128],
                            lb_sb[0:32, oc * 512:(oc + 1) * 512],
                            start=False, stop=False)
                        return e.matmul(
                            p.pb[pbk].ap()[:, 0:512], onesr[0:1, 0:128],
                            qbias[0:1, oc * 512:(oc + 1) * 512],
                            start=False, stop=True)
                    t.step("pe", (f"wdq{oc}", f"{inT_name}{tt // 4}", "sT",
                                  "lb_sb", "qbias", "onesr"),
                           (f"pb{pbk}",), f_gm)
                    if half == 1:
                        def f_h(e, oc=oc, pbk=pbk, usl=usl):
                            sl = slice(usl * 1024 + oc * 512,
                                       usl * 1024 + (oc + 1) * 512)
                            return e.tensor_add(uacc[:, sl], uacc[:, sl],
                                                p.pb[pbk].ap()[:, 0:512])
                        t.step("dve", (f"pb{pbk}", f"uacc{usl}"),
                               (f"uacc{usl}",), f_h)
                    else:
                        def f_z(e, tt=tt, oc=oc, pbk=pbk):
                            return e.tensor_copy(
                                zsb[:, (tt % 2) * 1024 + oc * 512:
                                    (tt % 2) * 1024 + (oc + 1) * 512],
                                p.pb[pbk].ap()[:, 0:512])
                        t.step("dve", (f"pb{pbk}",),
                               (f"w_z{tt % 2}o{oc}",), f_z)
                if half == 1:
                    t.step("scd", (f"uacc{usl}",), (f"ud{tt}",),
                           (lambda tt, usl: lambda e: e.dma_start(
                               ud[tt * 128:(tt + 1) * 128, :],
                               uacc[:, usl * 1024:(usl + 1) * 1024]))(tt,
                                                                      usl))
                else:
                    z_ap = zsb[:, (tt % 2) * 1024:(tt % 2 + 1) * 1024]
                    zr = (f"w_z{tt % 2}o0", f"w_z{tt % 2}o1")

                    def mko(tt=tt, z_ap=z_ap, zr=zr):
                        def out_fn(negmu, rstd, streads):
                            tmp = p.sbufs["w_tmp"]
                            nsl = tt % 2

                            def f(e):
                                e.tensor_scalar(tmp[:, 0:1024], z_ap, negmu,
                                                rstd, ALU.add, ALU.mult)
                                e.tensor_mul(tmp[:, 0:1024], tmp[:, 0:1024],
                                             g_tab)
                                return e.tensor_add(tmp[:, 0:1024],
                                                    tmp[:, 0:1024], b_tab)
                            t.step("dve", zr + streads + ("gb_g", "gb_b"),
                                   ("w_tmp",), f)

                            def fg(e):
                                return e.activation(
                                    ybf[:, nsl * 1024:(nsl + 1) * 1024],
                                    tmp[:, 0:1024], AF.Gelu, bias=0.0,
                                    scale=1.0)
                            t.step("act", ("w_tmp",), (f"w_nx{nsl}",), fg)
                        return out_fn
                    ln_steps(p, z_ap, zr, (tt % 4) * 4, mko())
                    emit_transposes_tile(
                        p, tt, ybf[:, (tt % 2) * 1024:(tt % 2 + 1) * 1024],
                        f"w_nx{tt % 2}", "attT")
        if blk < 2:
            cast_and_T()


def emit_slot(p, sl, src, dst, src_res, dst_res):
    d = p.dram
    t = p.t
    ud = d["ud"]
    if sl == "l0a":
        p.fence(SCR_NAMES)
        emit_nx_phase(p, "nxT", src, src_res)
        p.tap("dbg_nxT", p.sbufs["nxT"], ("nxT",))
        p.fence(WT_NAMES + SCR_NAMES)
        emit_attn_heads(p, "a0", "rope", "nxT")
        p.tap("dbg_attT", p.sbufs["attT"], ("attT",))
        p.fence(WT_NAMES)
        t.step("gpd", (), ("wo_sb",),
               lambda e: e.dma_start(p.sbufs["wo_sb"], d["a0_wo"]))
        t.step("gpd", (), ("bo_sb",),
               lambda e: e.dma_start(p.sbufs["bo_sb"], d["a0_bo"]))
        p.fence(SCR_NAMES)
        emit_proj(p, "wo_sb", "bo_sb", "sandwich", src_dram=src,
                  dst_dram=dst, src_res=src_res, dst_res=dst_res)
    elif sl == "l0f":
        p.fence(SCR_NAMES)
        emit_u_phase(p, src, ud, src_res)
        t.step("syd", (), ("gb_g",), lambda e: e.dma_start(
            p.sbufs["gb_g"],
            d["f0_ln1w"].partition_broadcast(128).squeeze(1)))
        t.step("syd", (), ("gb_b",), lambda e: e.dma_start(
            p.sbufs["gb_b"],
            d["f0_ln1b"].partition_broadcast(128).squeeze(1)))
        emit_nx_phase(p, "nxT", ud, "ud", gamma="gb_g", beta="gb_b")
        t.step("gpd", (), ("bqk_sb",),
               lambda e: e.dma_start(p.sbufs["bqk_sb"], d["f0_bqk"]))
        t.step("gpd", (), ("bv_sb",),
               lambda e: e.dma_start(p.sbufs["bv_sb"], d["f0_bv"]))
        p.fence(WT_NAMES + SCR_NAMES)
        emit_attn_heads(p, "f0", "plain", "nxT")
        p.fence(WT_NAMES)
        t.step("gpd", (), ("wo_sb",),
               lambda e: e.dma_start(p.sbufs["wo_sb"], d["f0_wo"]))
        t.step("gpd", (), ("bo_sb",),
               lambda e: e.dma_start(p.sbufs["bo_sb"], d["f0_bo"]))
        p.fence(SCR_NAMES)
        emit_proj(p, "wo_sb", "bo_sb", "accum_u", ud=ud)
        t.step("syd", (), ("gb_g",), lambda e: e.dma_start(
            p.sbufs["gb_g"],
            d["f0_ln2w"].partition_broadcast(128).squeeze(1)))
        t.step("syd", (), ("gb_b",), lambda e: e.dma_start(
            p.sbufs["gb_b"],
            d["f0_ln2b"].partition_broadcast(128).squeeze(1)))
        emit_nx_phase(p, "nxT", ud, "ud", gamma="gb_g", beta="gb_b")
        p.fence(WT_NAMES + ATT_NAMES)
        emit_moe(p, "nxT", ud)
        p.fence(SCR_NAMES + ATT_NAMES)
        emit_u_sandwich(p, src, dst, ud, src_res, dst_res)
    elif sl == "l1a":
        p.fence(SCR_NAMES)
        emit_nx_phase(p, "nxT", src, src_res)
        t.step("syd", (), ("l1lntab",), lambda e: e.dma_start(
            p.sbufs["l1lntab"],
            d["a1_lntab"].partition_broadcast(128).squeeze(1)))
        p.fence(WT_NAMES + SCR_NAMES)
        emit_attn_heads(p, "a1", "l1", "nxT")
        p.fence(WT_NAMES)
        t.step("gpd", (), ("wo_sb",),
               lambda e: e.dma_start(p.sbufs["wo_sb"], d["a1_wu"]))
        p.fence(SCR_NAMES)
        emit_proj(p, "wo_sb", None, "sandwich", src_dram=src, dst_dram=dst,
                  src_res=src_res, dst_res=dst_res)
    elif sl == "l1f":
        p.fence(SCR_NAMES)
        emit_u_phase(p, src, ud, src_res)
        p.fence(WT_NAMES + ATT_NAMES)
        emit_qlora_ffn(p, ud)
        p.fence(SCR_NAMES)
        emit_u_sandwich(p, src, dst, ud, src_res, dst_res)


def build_program(seqs=SPC, slots=("l0a", "l0f", "l1a", "l1f"), dbg=()):
    nc = bass.Bass("TRN2", target_bir_lowering=False)
    dram = {}

    def din(name, shape, dtype=F32):
        dram[name] = nc.dram_tensor(name, list(shape), dtype,
                                    kind="ExternalInput").ap()

    def dout(name, shape, dtype=F32):
        dram[name] = nc.dram_tensor(name, list(shape), dtype,
                                    kind="ExternalOutput").ap()

    def dint(name, shape, dtype=F32):
        dram[name] = nc.dram_tensor(name, list(shape), dtype,
                                    kind="Internal").ap()

    din("x", (SPC, T, C))
    dout("y", (SPC, T, C))
    dint("xa", (SPC, T, C))
    dint("xb", (SPC, T, C))
    dint("ud", (T, C))

    din("consts", (1, C_NCONST))
    din("identf", (128, 128))
    din("ropec", (128, T))
    din("ropes", (128, T))
    din("masks", (2, 128, QCH))

    din("a0_wqk", (H, 128, C))
    din("a0_wv", (4, 128, 2048))
    din("a0_wo", (128, CT * 1024))
    din("a0_bo", (1, C))

    din("f0_wqk", (H, 128, C))
    din("f0_bqk", (1, H * 128))
    din("f0_wv", (4, 128, 2048))
    din("f0_bv", (1, 4 * 256))
    din("f0_wo", (128, CT * 1024))
    din("f0_bo", (1, C))
    din("f0_ln1w", (1, C))
    din("f0_ln1b", (1, C))
    din("f0_ln2w", (1, C))
    din("f0_ln2b", (1, C))
    din("f0_gate", (128, CT * NE))
    din("f0_fc1", (NE, FFN // 512, 128, CT * 512))
    din("f0_fc1b", (NE, 128, FFN // 128))
    din("f0_fc2", (NE, FFN // 128, 128, C))
    din("f0_fc2b", (NE, 1, C))
    dint("f0_fc1_bf", (NE, FFN // 512, 128, CT * 512), BF16)
    dint("f0_fc2_bf", (NE, FFN // 128, 128, C), BF16)

    din("a1_wqkv", (H, 128, CT * 192))
    din("a1_wu", (128, CT * 1024))
    din("a1_lntab", (1, 256))

    din("q1_qt", (NMAT, 2, 128, CT * 512), I32)
    din("q1_sc", (NMAT, 2, 128, CT * 512))
    din("q1_b", (NMAT, 1, C))
    din("q1_la", (NMAT, 128, CT * LORA_R))
    din("q1_lb", (NMAT, LORA_R, C))
    din("q1_lng", (3, 1, C))
    din("q1_lnb", (3, 1, C))

    for nm, shp, dt in (("dbg_nxT", (128, CT * T), BF16),
                        ("dbg_qb", (64, T), BF16),
                        ("dbg_kb", (64, T), BF16),
                        ("dbg_attT", (128, CT * T), BF16),
                        ("dbg_st", (128, 16 * QCH), F32),
                        ("dbg_vg", (128, TT * 260), BF16)):
        if nm in dbg:
            dram[nm] = nc.dram_tensor(nm, list(shp), dt,
                                      kind="ExternalOutput").ap()
    trk = Trk()
    with ExitStack() as ctx:
        p = P(nc, ctx, trk, dram, dbg)
        p.sb("cpool", [128, C_NCONST], F32)
        p.sb("identf", [128, 128], F32)
        p.sb("identb", [128, 128], BF16)
        p.sb("onesr", [1, 512], BF16)
        p.sb("mask_sb", [128, 2 * QCH], F32)
        p.sb("nxT", [128, CT * T], BF16)
        att_ar = p.sb("att_ar", [128, CT * T], BF16)
        p.sbufs["attT"] = att_ar
        p.sbufs["moe_hT"] = att_ar
        p.sb("vgrp", [128, TT * 260], BF16)
        p.sb("st_sb", [128, 16 * QCH], F32)

        p.sb("qkc", [128, 512], F32)
        p.sb("rot", [128, 512], F32)
        p.sb("ptbT", [128, 2 * QCH], BF16)
        p.sbufs["ptb"] = p.sbufs["ptbT"]
        p.sb("att2", [128, 128], BF16)
        p.sb("bmx", [128, 32], F32)
        p.sb("bmx2", [128, 32], F32)
        p.sb("chain", [128, 12], F32)
        p.sb("chain_bf", [1, 4], BF16)
        p.sb("l1mu", [128, 64], F32)
        p.sb("l1var", [128, 32], F32)
        p.sb("l1lntab", [128, 256], F32)
        p.sb("gb_g", [128, C], F32)
        p.sb("gb_b", [128, C], F32)
        p.sb("moe_gl", [128, 256], F32)
        p.sb("moe_gw", [128, 512], F32)
        p.sb("fc1b_sb", [128, FFN // 128], F32)
        p.sb("fc2b_sb", [1, C], F32)
        p.sb("fc2b_bf", [1, C], BF16)
        p.sb("bo_sb", [1, C], BF16)
        p.sb("bqk_sb", [1, H * 128], BF16)
        p.sb("bv_sb", [1, 4 * 256], BF16)
        p.sb("qbias_f", [1, C], F32)
        p.sb("qbias", [1, C], BF16)
        p.sb("la_sb", [128, NMAT * CT * LORA_R], BF16)
        p.sb("lb_sb", [LORA_R, C], BF16)
        zqk = p.sb("zqk", [128, 2 * 1024], F32)
        p.sbufs["w_z"] = zqk
        qbkb = zqk.bitcast(BF16)
        p.sbufs["qb"] = qbkb[0:64, 0:2048]
        p.sbufs["kb"] = qbkb[0:64, 2048:4096]
        p.sb("w_xin", [128, 2 * 1024], F32)
        p.sb("w_st", [128, 16], F32)
        p.sb("w_sq", [128, 1024], F32)
        p.sb("w_nx", [128, 2 * 1024], BF16)
        p.sb("w_tmp", [128, 1024], F32)
        uo = p.sb("uacc", [128, 2 * 1024], F32)
        p.sbufs["w_out"] = uo
        p.sbufs["l1sub"] = p.sbufs["uacc"][:, 0:2048]
        p.sbufs["qkall"] = p.sbufs["w_xin"][:, 0:2048]
        p.sbufs["qknorm"] = p.sbufs["w_nx"][:, 0:2048]
        wt = p.sb("wt_ar", [128, 9216], F32)
        p.view("wqk_sb", wt, 0, 128, 3072, BF16)
        p.view("wv_sb", wt, 6144, 128, 4096, BF16)
        p.view("cos2", wt, 14336, 128, 2048, F32)
        p.view("sin2", wt, 22528, 128, 2048, F32)
        p.view("wo_sb", wt, 0, 128, 8192, BF16)
        p.view("fc1sb", wt, 0, 128, 8192, BF16)
        p.view("fc2sb", wt, 20480, 128, 4096, BF16)
        p.view("gate_sb", wt, 28672, 128, 64, F32)
        p.view("gate_bf", wt, 28928, 128, 64, BF16)
        p.view("wdq", wt, 0, 128, 8192, BF16)
        p.view("qt_sb", wt, 16384, 128, 2048, I32)
        p.view("sc_sb", wt, 24576, 128, 2048, F32)
        p.view("sT", wt, 32768, 32, 2048, BF16)

        emit_init(p)
        phase0_done = [False]
        for s in range(seqs):
            cur = dram["x"][s]
            cur_res = f"xinp{s}_"
            nslots = len(slots)
            for idx, sl in enumerate(slots):
                if sl == "l0f" and not phase0_done[0]:
                    emit_phase0_cast(p)
                    phase0_done[0] = True
                if idx == nslots - 1:
                    nxt = dram["y"][s]
                else:
                    nxt = dram["xa"][s] if idx % 2 == 0 else dram["xb"][s]
                nxt_res = f"xch{s}_{idx}_"
                emit_slot(p, sl, cur, nxt, cur_res, nxt_res)
                cur = nxt
                cur_res = nxt_res
        trk.emit(nc, ctx)
    return nc



def _t128(a):
    Cdim, X = a.shape
    return np.ascontiguousarray(
        a.reshape(Cdim // 128, 128, X).transpose(1, 0, 2).reshape(128, -1))


def pack_inputs(inp):
    f32 = np.float32
    g = {}
    g["consts"] = np.array([[1e-5, 1.0 / 1024, -0.5, 0.125, 1.0 / 64,
                             1.0, -1.0 / 1024, 1.0 / 64]], f32)
    g["identf"] = np.eye(128, dtype=f32)
    invf = (1.0 / 10000.0 ** (np.arange(32, dtype=np.float64) / 32))
    tpos = np.arange(T, dtype=np.float64)
    fr = np.outer(invf, tpos)
    g["ropec"] = np.ascontiguousarray(np.tile(np.cos(fr).astype(f32),
                                              (4, 1)))
    g["ropes"] = np.ascontiguousarray(np.tile(np.sin(fr).astype(f32),
                                              (4, 1)))
    kk = np.arange(128)[:, None]
    qq = np.arange(QCH)[None, :]
    g["masks"] = np.stack([(kk <= qq).astype(f32),
                           (kk + 128 <= qq).astype(f32)])

    W = np.asarray(inp["L0_attn_Wqkv"], f32)
    wqk = np.empty((H, 128, C), f32)
    for h in range(H):
        q = W[:, h * 64:(h + 1) * 64]
        k = W[:, 1024 + h * 64:1024 + (h + 1) * 64]
        cols = np.concatenate([q[:, 0::2], k[:, 0::2], q[:, 1::2],
                               k[:, 1::2]], axis=1)
        wqk[h] = _t128(cols)
    g["a0_wqk"] = wqk
    wv = np.empty((4, 128, 2048), f32)
    for gi in range(4):
        wv[gi] = _t128(W[:, 2048 + gi * 256:2048 + (gi + 1) * 256])
    g["a0_wv"] = wv
    g["a0_wo"] = _t128(np.asarray(inp["L0_attn_Wo"], f32))
    g["a0_bo"] = np.asarray(inp["L0_attn_bo"], f32).reshape(1, C)

    W = np.asarray(inp["L0_ffn_Wqkv"], f32)
    bq = np.asarray(inp["L0_ffn_bqkv"], f32)
    wqk = np.empty((H, 128, C), f32)
    bqk = np.empty((1, H * 128), f32)
    for h in range(H):
        cols = np.concatenate([W[:, h * 64:(h + 1) * 64],
                               W[:, 1024 + h * 64:1024 + (h + 1) * 64]],
                              axis=1)
        wqk[h] = _t128(cols)
        bqk[0, h * 128:h * 128 + 64] = bq[h * 64:(h + 1) * 64]
        bqk[0, h * 128 + 64:(h + 1) * 128] = bq[1024 + h * 64:
                                                1024 + (h + 1) * 64]
    g["f0_wqk"] = wqk
    g["f0_bqk"] = bqk
    wv = np.empty((4, 128, 2048), f32)
    for gi in range(4):
        wv[gi] = _t128(W[:, 2048 + gi * 256:2048 + (gi + 1) * 256])
    g["f0_wv"] = wv
    g["f0_bv"] = np.ascontiguousarray(bq[2048:].reshape(1, 1024))
    g["f0_wo"] = _t128(np.asarray(inp["L0_ffn_Wo"], f32))
    g["f0_bo"] = np.asarray(inp["L0_ffn_bo"], f32).reshape(1, C)
    g["f0_ln1w"] = np.asarray(inp["L0_ffn_ln1_w"], f32).reshape(1, C)
    g["f0_ln1b"] = np.asarray(inp["L0_ffn_ln1_b"], f32).reshape(1, C)
    g["f0_ln2w"] = np.asarray(inp["L0_ffn_ln2_w"], f32).reshape(1, C)
    g["f0_ln2b"] = np.asarray(inp["L0_ffn_ln2_b"], f32).reshape(1, C)
    g["f0_gate"] = _t128(np.asarray(inp["L0_ffn_gate_W"], f32))
    fc1 = np.asarray(inp["L0_ffn_fc1_w"], f32)
    a = np.empty((NE, 8, 128, 4096), f32)
    for ei in range(NE):
        At = fc1[ei].T
        a[ei] = At.reshape(8, 128, 8, 512).transpose(2, 1, 0, 3).reshape(
            8, 128, 4096)
    g["f0_fc1"] = a
    g["f0_fc1b"] = np.ascontiguousarray(
        np.asarray(inp["L0_ffn_fc1_b"], f32).reshape(NE, 32, 128)
        .transpose(0, 2, 1))
    fc2 = np.asarray(inp["L0_ffn_fc2_w"], f32)
    a = np.empty((NE, 32, 128, C), f32)
    for ei in range(NE):
        a[ei] = fc2[ei].T.reshape(32, 128, C)
    g["f0_fc2"] = a
    g["f0_fc2b"] = np.asarray(inp["L0_ffn_fc2_b"], f32).reshape(NE, 1, C)

    Wq = np.asarray(inp["L1_attn_Wq"], f32)
    Wk = np.asarray(inp["L1_attn_Wk"], f32)
    Wv = np.asarray(inp["L1_attn_Wv"], f32)
    wqkv = np.empty((H, 128, CT * 192), f32)
    for h in range(H):
        cols = np.concatenate([Wq[h * 64:(h + 1) * 64, :].T,
                               Wk[h * 64:(h + 1) * 64, :].T,
                               Wv[h * 64:(h + 1) * 64, :].T], axis=1)
        wqkv[h] = _t128(cols)
    g["a1_wqkv"] = wqkv
    g["a1_wu"] = _t128(np.asarray(inp["L1_attn_Wu"], f32).T)
    lntab = np.zeros((1, 256), f32)
    lntab[0, 0:64] = np.asarray(inp["L1_attn_qln_w"], f32)
    lntab[0, 64:128] = np.asarray(inp["L1_attn_kln_w"], f32)
    lntab[0, 128:192] = np.asarray(inp["L1_attn_qln_b"], f32)
    lntab[0, 192:256] = np.asarray(inp["L1_attn_kln_b"], f32)
    g["a1_lntab"] = lntab

    qw = np.asarray(inp["L1_ffn_qweights"], np.int32)
    sc = np.asarray(inp["L1_ffn_scales"], f32)
    aq = np.empty((NMAT, 2, 128, 4096), np.int32)
    asc = np.empty((NMAT, 2, 128, 4096), f32)
    for i in range(NMAT):
        A = qw[i]
        aq[i] = A.reshape(2, 512, 8, 128).transpose(0, 3, 2, 1).reshape(
            2, 128, 4096)
        S = np.repeat(sc[i], 16, axis=1)
        asc[i] = S.reshape(2, 512, 8, 128).transpose(0, 3, 2, 1).reshape(
            2, 128, 4096)
    g["q1_qt"] = aq
    g["q1_sc"] = asc
    g["q1_b"] = np.asarray(inp["L1_ffn_biases"], f32).reshape(NMAT, 1, C)
    la = np.asarray(inp["L1_ffn_lora_a"], f32)
    a = np.empty((NMAT, 128, CT * LORA_R), f32)
    for i in range(NMAT):
        a[i] = _t128(la[i].T)
    g["q1_la"] = a
    g["q1_lb"] = np.ascontiguousarray(
        np.asarray(inp["L1_ffn_lora_b"], f32).transpose(0, 2, 1))
    g["q1_lng"] = np.asarray(inp["L1_ffn_ln_gamma"], f32).reshape(3, 1, C)
    g["q1_lnb"] = np.asarray(inp["L1_ffn_ln_beta"], f32).reshape(3, 1, C)
    return g


def kernel(**inp):
    x = np.asarray(inp["L0_attn_x"], np.float32)
    g = pack_inputs(inp)
    nc = build_program()
    in_maps = []
    for core in range(NCORES):
        m = dict(g)
        m["x"] = np.ascontiguousarray(x[core * SPC:(core + 1) * SPC])
        in_maps.append(m)
    res = run_bass_kernel_spmd(nc, in_maps, core_ids=list(range(NCORES)))
    out = np.concatenate([r["y"] for r in res.results], axis=0)
    return out.astype(np.float32)


# version 07 (final)
# speedup vs baseline: 1.1893x; 1.0649x over previous
from contextlib import ExitStack

import numpy as np
import ml_dtypes
import concourse.bass as bass
import concourse.mybir as mybir
from concourse.bass_utils import run_bass_kernel_spmd

F32 = mybir.dt.float32
BF16 = mybir.dt.bfloat16
I32 = mybir.dt.int32
AF = mybir.ActivationFunctionType
ALU = mybir.AluOpType
AX = mybir.AxisListType

B, T, C = 32, 2048, 1024
H, HD = 16, 64
NCORES = 8
SPC = B // NCORES
TT = T // 128
CT = C // 128
QCH = 512
NQC = T // QCH
FFN = 4096
NE = 8
NMAT = 6
LORA_R = 32

C_EPS = 0
C_INV1024 = 1
C_NEGHALF = 2
C_ATTSCL = 3
C_INV64 = 4
C_ONE = 5
C_NEGINV1024 = 6
C_MEAN64 = 7
C_NCONST = 8

DMA_ENGS = ("syd", "scd", "gpd")
ENGQ = {"pe": "tensor", "dve": "vector", "act": "scalar", "pool": "gpsimd",
        "syd": "sync", "scd": "scalar", "gpd": "gpsimd"}
QUEUES = ("sync", "tensor", "vector", "scalar", "gpsimd")


class Trk:
    def __init__(self):
        self.steps = []
        self.cnt = {e: 0 for e in ENGQ}
        self.clk = {e: {f: 0 for f in ENGQ} for e in ENGQ}
        self.hist = {e: [] for e in ENGQ}
        self.res = {}

    def step(self, eng, reads=(), writes=(), fn=None, tag=""):
        mycl = self.clk[eng]
        waits = {}

        def need(m):
            s2, k = m
            if s2 == eng or mycl[s2] >= k:
                return
            if waits.get(s2, 0) < k:
                waits[s2] = k
            h = self.hist[s2][k - 1]
            for s3, v3 in h.items():
                if v3 > mycl[s3]:
                    mycl[s3] = v3

        for r in reads:
            ent = self.res.get(r)
            if ent and ent[0]:
                need(ent[0])
        for r in writes:
            ent = self.res.get(r)
            if ent:
                if ent[0]:
                    need(ent[0])
                for m in ent[1]:
                    need(m)
        self.cnt[eng] += 1
        k = self.cnt[eng]
        mycl[eng] = k
        self.hist[eng].append(dict(mycl))
        wset = set(writes)
        for r in wset:
            self.res[r] = [(eng, k), []]
        for r in reads:
            if r in wset:
                continue
            ent = self.res.setdefault(r, [None, []])
            ent[1].append((eng, k))
        self.steps.append((eng, fn, sorted(waits.items()), tag))

    def emit(self, nc, ctx):
        sems = {e: ctx.enter_context(nc.semaphore(f"sem_{e}"))
                for e in ENGQ}
        by_q = {q: [] for q in QUEUES}
        for st in self.steps:
            by_q[ENGQ[st[0]]].append(st)
        with nc.Block() as blk:
            for q in QUEUES:
                steps = by_q[q]

                def mkbody(steps):
                    def body(e):
                        for (eng, fn, waits, tag) in steps:
                            for (s2, k) in waits:
                                e.wait_ge(sems[s2],
                                          k * (16 if s2 in DMA_ENGS else 1))
                            ins = fn(e)
                            ins.then_inc(sems[eng],
                                         16 if eng in DMA_ENGS else 1)
                    return body
                getattr(blk, q)(mkbody(steps))
        final = {e: self.cnt[e] for e in ENGQ}
        with nc.Block() as blk:
            def fin(e):
                for s2, k in final.items():
                    if k:
                        e.wait_ge(sems[s2], k * (16 if s2 in DMA_ENGS else 1))
            blk.sync(fin)


class P:
    def __init__(self, nc, ctx, trk, dram, dbg=()):
        self.dbg = set(dbg)
        self.nc = nc
        self.ctx = ctx
        self.t = trk
        self.dram = dram
        self.pb = [ctx.enter_context(
            nc.psum_tensor(f"pb{i}", [128, 512], F32)) for i in range(8)]
        self.sbufs = {}

    def sb(self, name, shape, dtype):
        h = self.ctx.enter_context(
            self.nc.sbuf_tensor("sb_" + name, shape, dtype))
        self.sbufs[name] = h[:]
        return self.sbufs[name]

    def view(self, name, arena_ap, col_off_bytes, parts, cols, dtype):
        base = arena_ap.bitcast(dtype)
        esz = 2 if dtype == BF16 else 4
        c0 = col_off_bytes // esz
        self.sbufs[name] = base[0:parts, c0:c0 + cols]
        return self.sbufs[name]

    def tap(self, name, src_ap, reads):
        if name not in self.dbg:
            return
        d = self.dram[name]
        self.t.step("scd", tuple(reads), (),
                    lambda e: e.dma_start(d, src_ap))

    def fence(self, names):
        self.t.step("pool", (), tuple(names), lambda e: e.engine_nop(),
                    "fence")


def nxc(ct, tcol):
    return ct * T + tcol


WT_NAMES = ("wqk0", "wqk1", "wv0", "wv1", "cos2", "sin2", "wo_sb",
            "fc1w0", "fc1w1", "fc2w0", "fc2w1", "gate_sb", "gate_bf",
            "qt_sb", "sc_sb", "wdq0", "wdq1", "sT")
ATT_NAMES = ("attT", "attT0", "attT1", "attT2", "attT3") + tuple(
    f"moe_hT{i}" for i in range(32))
SCR_NAMES = ("w_xin0", "w_xin1", "w_nx0", "w_nx1", "uacc0", "uacc1",
             "qkall", "qknorm", "l1sub",
             "w_z0o0", "w_z0o1", "w_z1o0", "w_z1o1", "w_out0", "w_out1",
             "w_tmp", "w_tmp2", "w_sqscr",
             "qb", "kb", "qkc", "rot", "ptb0", "ptb1",
             "att2_0", "att2_1", "bmx0", "bmx1")


def emit_init(p):
    t, d = p.t, p.dram
    t.step("syd", (), ("cpool",), lambda e: e.dma_start(
        p.sbufs["cpool"], d["consts"].partition_broadcast(128).squeeze(1)))
    t.step("syd", (), ("identf",),
           lambda e: e.dma_start(p.sbufs["identf"], d["identf"]))
    t.step("gpd", (), ("mask_sb",), lambda e: e.dma_start(
        p.sbufs["mask_sb"].rearrange("p (i q) -> p i q", q=QCH),
        d["masks"].transpose([1, 0, 2])))
    t.step("pool", (), ("onesr",),
           lambda e: e.memset(p.sbufs["onesr"], 1.0))
    t.step("dve", ("identf",), ("identb",),
           lambda e: e.tensor_copy(p.sbufs["identb"], p.sbufs["identf"]))


def emit_phase0_cast(p):
    t, d = p.t, p.dram
    for ei in range(NE):
        for fc in range(FFN // 512):
            t.step("gpd", (), (f"fc1bf_{ei}",),
                   (lambda ei, fc: lambda e: e.dma_start(
                       d["f0_fc1_bf"][ei, fc], d["f0_fc1"][ei, fc]))(ei, fc))
        for f8 in range(4):
            t.step("gpd", (), (f"fc2bf_{ei}",),
                   (lambda ei, f8: lambda e: e.dma_start(
                       d["f0_fc2_bf"][ei, f8 * 8:(f8 + 1) * 8],
                       d["f0_fc2"][ei, f8 * 8:(f8 + 1) * 8]))(ei, f8))


def ln_steps(p, src_ap, src_reads, sc, out_fn):
    t = p.t
    stats = p.sbufs["w_st"]
    cpool = p.sbufs["cpool"]
    W = src_ap.shape[-1]
    sq = p.sbufs["w_sq"]

    def f_mu(e):
        return e.reduce_sum(stats[:, sc:sc + 1], src_ap, axis=AX.X)
    t.step("dve", tuple(src_reads), (f"w_st{sc}",), f_mu)

    def f_neg(e):
        return e.tensor_scalar(stats[:, sc + 1:sc + 2], stats[:, sc:sc + 1],
                               -1.0 / W, None, ALU.mult)
    t.step("pool", (f"w_st{sc}",), (f"w_st{sc}n",), f_neg)

    def f_sq(e):
        e.activation(sq[:, 0:W], src_ap, AF.Square,
                     bias=stats[:, sc + 1:sc + 2], scale=1.0,
                     accum_out=stats[:, sc + 2:sc + 3])
        e.activation(stats[:, sc + 3:sc + 4], stats[:, sc + 2:sc + 3],
                     AF.Ln, bias=cpool[:, C_EPS:C_EPS + 1],
                     scale=cpool[:, C_INV1024:C_INV1024 + 1])
        return e.activation(stats[:, sc + 3:sc + 4], stats[:, sc + 3:sc + 4],
                            AF.Exp, bias=0.0,
                            scale=cpool[:, C_NEGHALF:C_NEGHALF + 1])
    t.step("act", tuple(src_reads) + (f"w_st{sc}", f"w_st{sc}n", "cpool"),
           (f"w_st{sc}b", "w_sqscr"), f_sq)
    out_fn(stats[:, sc + 1:sc + 2], stats[:, sc + 3:sc + 4],
           (f"w_st{sc}", f"w_st{sc}b"))


def emit_u_phase(p, src, ud, src_res):
    t = p.t
    xin = p.sbufs["w_xin"]
    out = p.sbufs["w_out"]
    for tt in range(TT):
        sl = tt % 2
        t.step("scd", (f"{src_res}{tt}",), (f"w_xin{sl}",),
               (lambda tt, sl: lambda e: e.dma_start(
                   xin[:, sl * 1024:(sl + 1) * 1024],
                   src[tt * 128:(tt + 1) * 128, :]))(tt, sl))
        src_ap = xin[:, sl * 1024:(sl + 1) * 1024]

        def mko(tt=tt, sl=sl, src_ap=src_ap):
            def out_fn(negmu, rstd, streads):
                def f(e):
                    return e.tensor_scalar(
                        out[:, sl * 1024:(sl + 1) * 1024], src_ap,
                        negmu, rstd, ALU.add, ALU.mult)
                t.step("dve", (f"w_xin{sl}",) + streads, (f"w_out{sl}",), f)
                t.step("scd", (f"w_out{sl}",), (f"ud{tt}",),
                       (lambda tt, sl: lambda e: e.dma_start(
                           ud[tt * 128:(tt + 1) * 128, :],
                           out[:, sl * 1024:(sl + 1) * 1024]))(tt, sl))
            return out_fn
        ln_steps(p, src_ap, (f"w_xin{sl}",), (tt % 4) * 4, mko())


def emit_transposes_tile(p, tt, src_ap, src_res, dstT_name, dst_res=None):
    t = p.t
    identb = p.sbufs["identb"]
    dstT = p.sbufs[dstT_name]
    if dst_res is None:
        dst_res = f"{dstT_name}{tt // 4}"
    pbk = 6 + (tt % 2)
    pt = p.pb[pbk].ap().bitcast(BF16)

    def f_tr(e):
        ins = None
        for ct in range(CT):
            ins = e.transpose(pt[:, ct * 128:(ct + 1) * 128],
                              src_ap[:, ct * 128:(ct + 1) * 128], identb[:])
        return ins
    t.step("pe", (src_res, "identb"), (f"pb{pbk}",), f_tr)

    def f_ev(e):
        ins = None
        for half in range(2):
            outap = dstT.rearrange("p (ct tx) -> p ct tx", tx=T)[
                :, half * 4:(half + 1) * 4, tt * 128:(tt + 1) * 128]
            ins = e.tensor_copy(
                outap, pt[:, half * 512:(half + 1) * 512].rearrange(
                    "p (ct tx) -> p ct tx", tx=128))
        return ins
    t.step("dve", (f"pb{pbk}",), (dst_res,), f_ev)


def emit_nx_phase(p, nxT_name, src_dram, src_res_base, gamma=None, beta=None):
    t = p.t
    nxbf = p.sbufs["w_nx"]
    xin = p.sbufs["w_xin"]
    for tt in range(TT):
        sl = tt % 2
        t.step("scd", (f"{src_res_base}{tt}",), (f"w_xin{sl}",),
               (lambda tt, sl: lambda e: e.dma_start(
                   xin[:, sl * 1024:(sl + 1) * 1024],
                   src_dram[tt * 128:(tt + 1) * 128, :]))(tt, sl))
        src_ap = xin[:, sl * 1024:(sl + 1) * 1024]

        def mko(tt=tt, sl=sl, src_ap=src_ap):
            def out_fn(negmu, rstd, streads):
                if gamma is None:
                    def f(e):
                        return e.tensor_scalar(
                            nxbf[:, sl * 1024:(sl + 1) * 1024], src_ap,
                            negmu, rstd, ALU.add, ALU.mult)
                    t.step("dve", (f"w_xin{sl}",) + streads,
                           (f"w_nx{sl}",), f)
                else:
                    g_tab = p.sbufs[gamma]
                    b_tab = p.sbufs[beta]
                    tmp = p.sbufs["w_tmp"]

                    def f(e):
                        e.tensor_scalar(tmp[:, 0:1024], src_ap, negmu, rstd,
                                        ALU.add, ALU.mult)
                        e.tensor_mul(tmp[:, 0:1024], tmp[:, 0:1024], g_tab)
                        return e.tensor_add(
                            nxbf[:, sl * 1024:(sl + 1) * 1024],
                            tmp[:, 0:1024], b_tab)
                    t.step("dve", (f"w_xin{sl}",) + streads + (gamma, beta),
                           (f"w_nx{sl}", "w_tmp"), f)
            return out_fn
        ln_steps(p, src_ap, (f"w_xin{sl}",), (tt % 4) * 4, mko())
        emit_transposes_tile(p, tt, p.sbufs["w_nx"][:, (tt % 2) * 1024:
                                                    (tt % 2 + 1) * 1024],
                             f"w_nx{tt % 2}", nxT_name)


def emit_attn_heads(p, wp, mode, nxT_name):
    t, d = p.t, p.dram
    nxT = p.sbufs[nxT_name]
    attT = p.sbufs["attT"]
    identb = p.sbufs["identb"]
    identf = p.sbufs["identf"]
    onesr = p.sbufs["onesr"]
    cpool = p.sbufs["cpool"]
    mask_sb = p.sbufs["mask_sb"]
    qb, kb = p.sbufs["qb"], p.sbufs["kb"]
    vgrp = p.sbufs["vgrp"]
    bmx = p.sbufs["bmx"]
    chain = p.sbufs["chain"]
    chain_bf = p.sbufs["chain_bf"]
    ptb = p.sbufs["ptb"]
    att2 = p.sbufs["att2"]
    wqk_sb = p.sbufs["wqk_sb"]
    wv_sb = p.sbufs["wv_sb"]
    bias = (mode == "plain")

    if mode == "rope":
        t.step("syd", (), ("cos2",), lambda e: e.dma_start(
            p.sbufs["cos2"], d["ropec"]))
        t.step("syd", (), ("sin2",), lambda e: e.dma_start(
            p.sbufs["sin2"], d["ropes"]))

    def f_ones(e):
        ins = None
        for tt in range(TT):
            for j in range(4):
                ins = e.memset(vgrp[:, tt * 260 + j * 65 + 64:
                                    tt * 260 + j * 65 + 65], 1.0)
        return ins
    t.step("pool", (), ("vgrp", "vgrp_ones"), f_ones)

    def emit_qkv(h):
        wslot = h % 2
        hp = h % 2
        qbh = qb[:, hp * T:(hp + 1) * T]
        kbh = kb[:, hp * T:(hp + 1) * T]
        if mode in ("rope", "plain"):
            t.step("gpd", (), (f"wqk{wslot}",),
                   (lambda h, wslot: lambda e: e.dma_start(
                       wqk_sb[:, wslot * 1024:(wslot + 1) * 1024],
                       d[wp + "_wqk"][h]))(h, wslot))
            for chunk in range(4):
                pbk = chunk % 2

                def f_qk(e, h=h, chunk=chunk, pbk=pbk, wslot=wslot):
                    ins = None
                    for ct in range(CT):
                        ins = e.matmul(
                            p.pb[pbk].ap()[:, 0:512],
                            wqk_sb[:, wslot * 1024 + ct * 128:
                                   wslot * 1024 + (ct + 1) * 128],
                            nxT[:, nxc(ct, chunk * 512):
                                nxc(ct, (chunk + 1) * 512)],
                            start=(ct == 0), stop=(ct == CT - 1 and not bias))
                    if bias:
                        ins = e.matmul(
                            p.pb[pbk].ap()[:, 0:512],
                            p.sbufs["bqk_sb"][0:1, h * 128:(h + 1) * 128],
                            onesr[0:1, 0:512], start=False, stop=True)
                    return ins
                rd = (f"wqk{wslot}", f"{nxT_name}{chunk}", "onesr")
                if bias:
                    rd = rd + ("bqk_sb",)
                t.step("pe", rd, (f"pb{pbk}",), f_qk)
                if mode == "rope":
                    qkc = p.sbufs["qkc"]
                    rot = p.sbufs["rot"]

                    def f_rope(e, chunk=chunk, pbk=pbk, qbh=qbh, kbh=kbh):
                        cs = slice(chunk * 512, (chunk + 1) * 512)
                        e.tensor_copy(qkc[:, 0:512], p.pb[pbk].ap()[:, 0:512])
                        e.tensor_scalar(rot[0:64, 0:512], qkc[64:128, 0:512],
                                        -1.0, None, ALU.mult)
                        e.tensor_copy(rot[64:128, 0:512], qkc[0:64, 0:512])
                        e.tensor_mul(qkc[:, 0:512], qkc[:, 0:512],
                                     p.sbufs["cos2"][:, cs])
                        e.tensor_mul(rot[:, 0:512], rot[:, 0:512],
                                     p.sbufs["sin2"][:, cs])
                        e.tensor_add(qkc[:, 0:512], qkc[:, 0:512],
                                     rot[:, 0:512])
                        e.tensor_copy(qbh[0:32, cs], qkc[0:32, 0:512])
                        e.tensor_copy(kbh[0:32, cs], qkc[32:64, 0:512])
                        e.tensor_copy(qbh[32:64, cs], qkc[64:96, 0:512])
                        return e.tensor_copy(kbh[32:64, cs],
                                             qkc[96:128, 0:512])
                    t.step("dve", (f"pb{pbk}", "cos2", "sin2"),
                           (f"qb{hp}", f"kb{hp}", "qkc", "rot"), f_rope)
                else:
                    def f_ev(e, chunk=chunk, pbk=pbk, qbh=qbh, kbh=kbh):
                        cs = slice(chunk * 512, (chunk + 1) * 512)
                        e.tensor_copy(qbh[:, cs], p.pb[pbk].ap()[0:64, 0:512])
                        return e.tensor_copy(kbh[:, cs],
                                             p.pb[pbk].ap()[64:128, 0:512])
                    t.step("dve", (f"pb{pbk}",), (f"qb{hp}", f"kb{hp}"),
                           f_ev)
            if h == 0 and mode == "rope":
                p.tap("dbg_qb", p.sbufs["qb"], ("qb",))
                p.tap("dbg_kb", p.sbufs["kb"], ("kb",))
        else:
            qkall = p.sbufs["qkall"]
            qkn = p.sbufs["qknorm"]
            lntab = p.sbufs["l1lntab"]
            t.step("gpd", (), (f"wqk{wslot}",),
                   (lambda h, wslot: lambda e: e.dma_start(
                       wqk_sb[:, wslot * 1536:(wslot + 1) * 1536],
                       d["a1_wqkv"][h]))(h, wslot))
            for tt in range(TT):
                pbk = 3 + (tt % 2)

                def f_qkv(e, tt=tt, pbk=pbk, wslot=wslot):
                    ins = None
                    for ct in range(CT):
                        ins = e.matmul(
                            p.pb[pbk].ap()[:, 0:192],
                            nxT[:, nxc(ct, tt * 128):nxc(ct, (tt + 1) * 128)],
                            wqk_sb[:, wslot * 1536 + ct * 192:
                                   wslot * 1536 + (ct + 1) * 192],
                            start=(ct == 0), stop=(ct == CT - 1))
                    return ins
                t.step("pe", (f"wqk{wslot}", f"{nxT_name}{tt // 4}"),
                       (f"pb{pbk}",), f_qkv)

                def f_ev(e, tt=tt, pbk=pbk, h=h):
                    e.tensor_copy(qkall[:, tt * 128:(tt + 1) * 128],
                                  p.pb[pbk].ap()[:, 0:128])
                    return e.tensor_copy(
                        vgrp[:, tt * 260 + (h % 4) * 65:
                             tt * 260 + (h % 4) * 65 + 64],
                        p.pb[pbk].ap()[:, 128:192])
                t.step("dve", (f"pb{pbk}", "vgrp_ones"), ("qkall", "vgrp"),
                       f_ev)

            def f_mu(e):
                return e.reduce_sum(
                    p.sbufs["l1mu"][:, 0:32],
                    qkall.rearrange("p (g dd) -> p g dd", dd=64), axis=AX.X)
            t.step("dve", ("qkall",), ("l1mu",), f_mu)

            def f_mul(e):
                return e.activation(p.sbufs["l1mu"][:, 32:64],
                                    p.sbufs["l1mu"][:, 0:32], AF.Copy,
                                    bias=0.0,
                                    scale=cpool[:, C_MEAN64:C_MEAN64 + 1])
            t.step("act", ("l1mu", "cpool"), ("l1mu2",), f_mul)

            def f_sub(e):
                mub = p.sbufs["l1mu"][:, 32:64].unsqueeze(2).broadcast_to(
                    (128, 32, 64))
                e.tensor_tensor(
                    p.sbufs["l1sub"].rearrange("p (g dd) -> p g dd", dd=64),
                    qkall.rearrange("p (g dd) -> p g dd", dd=64),
                    mub, ALU.subtract)
                e.tensor_mul(p.sbufs["w_sq"][:, 0:1024],
                             p.sbufs["l1sub"][:, 0:1024],
                             p.sbufs["l1sub"][:, 0:1024])
                e.tensor_mul(p.sbufs["w_tmp"][:, 0:1024],
                             p.sbufs["l1sub"][:, 1024:2048],
                             p.sbufs["l1sub"][:, 1024:2048])
                e.reduce_sum(
                    p.sbufs["l1var"][:, 0:16],
                    p.sbufs["w_sq"][:, 0:1024].rearrange(
                        "p (g dd) -> p g dd", dd=64), axis=AX.X)
                return e.reduce_sum(
                    p.sbufs["l1var"][:, 16:32],
                    p.sbufs["w_tmp"][:, 0:1024].rearrange(
                        "p (g dd) -> p g dd", dd=64), axis=AX.X)
            t.step("dve", ("qkall", "l1mu2"),
                   ("l1sub", "l1var", "w_sqscr", "w_tmp"), f_sub)

            def f_rstd(e):
                e.activation(p.sbufs["l1var"][:, 0:32],
                             p.sbufs["l1var"][:, 0:32], AF.Ln,
                             bias=cpool[:, C_EPS:C_EPS + 1],
                             scale=cpool[:, C_INV64:C_INV64 + 1])
                return e.activation(p.sbufs["l1var"][:, 0:32],
                                    p.sbufs["l1var"][:, 0:32], AF.Exp,
                                    bias=0.0,
                                    scale=cpool[:, C_NEGHALF:C_NEGHALF + 1])
            t.step("act", ("l1var", "cpool"), ("l1var",), f_rstd)

            def f_norm(e):
                rb = p.sbufs["l1var"][:, 0:32].unsqueeze(2).broadcast_to(
                    (128, 32, 64))
                e.tensor_tensor(
                    p.sbufs["l1sub"].rearrange("p (g dd) -> p g dd", dd=64),
                    p.sbufs["l1sub"].rearrange("p (g dd) -> p g dd", dd=64),
                    rb, ALU.mult)
                gtab = lntab[:, 0:128].unsqueeze(1).broadcast_to(
                    (128, 16, 128))
                e.tensor_tensor(
                    p.sbufs["l1sub"].rearrange("p (g dd) -> p g dd", dd=128),
                    p.sbufs["l1sub"].rearrange("p (g dd) -> p g dd", dd=128),
                    gtab, ALU.mult)
                btab = lntab[:, 128:256].unsqueeze(1).broadcast_to(
                    (128, 16, 128))
                return e.tensor_tensor(
                    qkn.rearrange("p (g dd) -> p g dd", dd=128),
                    p.sbufs["l1sub"].rearrange("p (g dd) -> p g dd", dd=128),
                    btab, ALU.add)
            t.step("dve", ("l1sub", "l1var", "l1lntab"), ("qknorm", "l1sub"),
                   f_norm)
            for tt in range(TT):
                pbk = 6 + (tt % 2)
                ptq = p.pb[pbk].ap().bitcast(BF16)

                def f_tr(e, tt=tt, ptq=ptq):
                    return e.transpose(ptq[:, 0:128],
                                       qkn[:, tt * 128:(tt + 1) * 128],
                                       identb[:])
                t.step("pe", ("qknorm", "identb"), (f"pb{pbk}",), f_tr)

                def f_cp(e, tt=tt, ptq=ptq, qbh=qbh, kbh=kbh):
                    e.tensor_copy(qbh[:, tt * 128:(tt + 1) * 128],
                                  ptq[0:64, 0:128])
                    return e.tensor_copy(kbh[:, tt * 128:(tt + 1) * 128],
                                         ptq[64:128, 0:128])
                t.step("dve", (f"pb{pbk}",), (f"qb{hp}", f"kb{hp}"), f_cp)

    def emit_vgroup(h):
        if mode in ("rope", "plain"):
            if True:
                g = h // 4
                gs = g % 2
                t.step("gpd", (), (f"wv{gs}",),
                       (lambda g, gs: lambda e: e.dma_start(
                           wv_sb[:, gs * 2048:(gs + 1) * 2048],
                           d[wp + "_wv"][g]))(g, gs))
                for tt in range(TT):
                    def f_v(e, gs=gs, tt=tt, g=g):
                        ins = None
                        for ct in range(CT):
                            ins = e.matmul(
                                p.pb[2].ap()[:, 0:256],
                                nxT[:, nxc(ct, tt * 128):
                                    nxc(ct, (tt + 1) * 128)],
                                wv_sb[:, gs * 2048 + ct * 256:
                                      gs * 2048 + (ct + 1) * 256],
                                start=(ct == 0),
                                stop=(ct == CT - 1 and not bias))
                        if bias:
                            ins = e.matmul(
                                p.pb[2].ap()[:, 0:256], onesr[0:1, 0:128],
                                p.sbufs["bv_sb"][0:1, g * 256:(g + 1) * 256],
                                start=False, stop=True)
                        return ins
                    rd = (f"wv{gs}", f"{nxT_name}{tt // 4}", "onesr")
                    if bias:
                        rd = rd + ("bv_sb",)
                    t.step("pe", rd, ("pb2",), f_v)

                    def f_vev(e, tt=tt):
                        return e.tensor_copy(
                            vgrp[:, tt * 260:tt * 260 + 260].rearrange(
                                "p (j dd) -> p j dd", dd=65)[:, :, 0:64],
                            p.pb[2].ap()[:, 0:256].rearrange(
                                "p (j dd) -> p j dd", dd=64))
                    t.step("dve", ("pb2", "vgrp_ones"), ("vgrp",), f_vev)
    def emit_scores(h):
        hp = h % 2
        qbh = qb[:, hp * T:(hp + 1) * T]
        kbh = kb[:, hp * T:(hp + 1) * T]
        OB = {0: 5, 1: 2}

        def em_s1(qc, i):
            pbk = 3 + (i % 2)
            cp = qc % 2

            def f_s(e, i=i, qc=qc, pbk=pbk):
                return e.matmul(p.pb[pbk].ap()[:, 0:QCH],
                                kbh[:, i * 128:(i + 1) * 128],
                                qbh[:, qc * QCH:(qc + 1) * QCH],
                                start=True, stop=True)
            t.step("pe", (f"qb{hp}", f"kb{hp}"), (f"pb{pbk}",), f_s)

            def f_bm(e, i=i, cp=cp, pbk=pbk):
                return e.reduce_max(bmx[:, cp * 16 + i:cp * 16 + i + 1],
                                    p.pb[pbk].ap()[:, 0:QCH], axis=AX.X)
            t.step("dve", (f"pb{pbk}",), (f"bmx{cp}",), f_bm)

        def em_chain(qc):
            nbl = 4 * qc + 4
            cp = qc % 2
            ob = OB[cp]
            t.step("act", (f"bmx{cp}",), (f"bmx2_{cp}",),
                   (lambda nbl, cp: lambda e: e.activation(
                       p.sbufs["bmx2"][:, cp * 16:cp * 16 + nbl],
                       bmx[:, cp * 16:cp * 16 + nbl], AF.Copy,
                       bias=0.0, scale=1.0))(nbl, cp))
            t.step("dve", (f"bmx2_{cp}",), (f"chcm{cp}",),
                   (lambda nbl, cp: lambda e: e.reduce_max(
                       chain[:, cp:cp + 1],
                       p.sbufs["bmx2"][:, cp * 16:cp * 16 + nbl],
                       axis=AX.X))(nbl, cp))
            t.step("pe", (f"chcm{cp}", "identf"), (f"pb{ob}",),
                   (lambda cp, ob: lambda e: e.transpose(
                       p.pb[ob].ap()[0:1, 280:408], chain[:, cp:cp + 1],
                       identf[:]))(cp, ob))
            t.step("dve", (f"pb{ob}",), (f"chred{cp}",),
                   (lambda cp, ob: lambda e: e.reduce_max(
                       chain[0:1, 2 + cp:3 + cp],
                       p.pb[ob].ap()[0:1, 280:408],
                       axis=AX.X, negate=True))(cp, ob))
            t.step("act", (f"chred{cp}", "cpool"), (f"chbf{cp}",),
                   (lambda cp: lambda e: e.activation(
                       chain_bf[0:1, cp:cp + 1], chain[0:1, 2 + cp:3 + cp],
                       AF.Copy, bias=0.0,
                       scale=cpool[0:1, C_ATTSCL:C_ATTSCL + 1]))(cp))
            t.step("pe", (f"chbf{cp}", "onesr"), (f"pb{ob}",),
                   (lambda cp, ob: lambda e: e.matmul(
                       p.pb[ob].ap()[:, 440:441], onesr[0:1, 0:128],
                       chain_bf[0:1, cp:cp + 1],
                       start=True, stop=True))(cp, ob))
            t.step("dve", (f"pb{ob}",), (f"chainM{cp}",),
                   (lambda cp, ob: lambda e: e.tensor_copy(
                       chain[:, 4 + cp:5 + cp],
                       p.pb[ob].ap()[:, 440:441]))(cp, ob))

        def em_s2exp(qc, i):
            pbk = 6 + (i % 2)
            cp = qc % 2
            psl = i % 2
            dj = i - 4 * qc

            def f_s(e, i=i, qc=qc, pbk=pbk):
                return e.matmul(p.pb[pbk].ap()[:, 0:QCH],
                                kbh[:, i * 128:(i + 1) * 128],
                                qbh[:, qc * QCH:(qc + 1) * QCH],
                                start=True, stop=True)
            t.step("pe", (f"qb{hp}", f"kb{hp}"), (f"pb{pbk}",), f_s)

            def f_exp(e, psl=psl, cp=cp, pbk=pbk):
                return e.activation(ptb[:, psl * QCH:(psl + 1) * QCH],
                                    p.pb[pbk].ap()[:, 0:QCH],
                                    AF.Exp, bias=chain[:, 4 + cp:5 + cp],
                                    scale=cpool[:, C_ATTSCL:C_ATTSCL + 1])
            t.step("act", (f"pb{pbk}", f"chainM{cp}", "cpool"),
                   (f"ptb{psl}",), f_exp)
            if dj >= 0:
                def f_msk(e, psl=psl, mi=dj):
                    return e.tensor_mul(
                        ptb[:, psl * QCH:(psl + 1) * QCH],
                        ptb[:, psl * QCH:(psl + 1) * QCH],
                        mask_sb[:, mi * QCH:(mi + 1) * QCH])
                t.step("dve", ("mask_sb", f"ptb{psl}"), (f"ptb{psl}",),
                       f_msk)

        def em_av(qc, i, first_av):
            nbl = 4 * qc + 4
            cp = qc % 2
            ob = OB[cp]
            psl = i % 2
            dj = i - 4 * qc

            def f_av(e, i=i, psl=psl, h=h, ob=ob, dj=dj, nbl=nbl):
                ins = None
                st0 = first_av[0]
                first_qs = 0 if dj < 0 else dj
                for qs in range(4):
                    if qs < first_qs:
                        continue
                    ins = e.matmul(
                        p.pb[ob].ap()[:, qs * 65:qs * 65 + 65],
                        ptb[:, psl * QCH + qs * 128:
                            psl * QCH + (qs + 1) * 128],
                        vgrp[:, i * 260 + (h % 4) * 65:
                             i * 260 + (h % 4) * 65 + 65],
                        start=st0 and qs == first_qs,
                        stop=(i == nbl - 1 and qs == 3))
                first_av[0] = False
                return ins
            t.step("pe", (f"ptb{psl}", "vgrp"), (f"pb{ob}",), f_av)

        def em_ofinal(qc):
            cp = qc % 2
            ob = OB[cp]
            for qs in range(4):
                qp = qs % 2

                def f_rec(e, qs=qs, cp=cp, ob=ob, qp=qp):
                    e.activation(chain[:, 6 + 2 * cp + qp:7 + 2 * cp + qp],
                                 p.pb[ob].ap()[:, qs * 65 + 64:qs * 65 + 65],
                                 AF.Ln, bias=0.0, scale=1.0)
                    return e.activation(
                        chain[:, 6 + 2 * cp + qp:7 + 2 * cp + qp],
                        chain[:, 6 + 2 * cp + qp:7 + 2 * cp + qp],
                        AF.Exp, bias=0.0, scale=-1.0)
                t.step("act", (f"pb{ob}", f"att2_{qp}", f"oo{cp}"),
                       (f"rec{cp}{qp}",), f_rec)

                def f_on(e, qs=qs, cp=cp, ob=ob, qp=qp):
                    return e.tensor_scalar(
                        att2[:, qp * 64:qp * 64 + 64],
                        p.pb[ob].ap()[:, qs * 65:qs * 65 + 64],
                        chain[:, 6 + 2 * cp + qp:7 + 2 * cp + qp],
                        None, ALU.mult)
                t.step("dve", (f"pb{ob}", f"rec{cp}{qp}"),
                       (f"att2_{qp}", f"oo{cp}"), f_on)
                pbk = 3 + qp
                pta = p.pb[pbk].ap().bitcast(BF16)

                def f_atr(e, pta=pta, qp=qp):
                    return e.transpose(pta[0:64, 0:128],
                                       att2[:, qp * 64:qp * 64 + 64],
                                       identb[:])
                t.step("pe", (f"att2_{qp}", "identb"), (f"pb{pbk}",), f_atr)

                def f_acp(e, qs=qs, qc=qc, h=h, pta=pta):
                    tt = qc * 4 + qs
                    pbase = (h % 2) * 64
                    return e.tensor_copy(
                        attT[pbase:pbase + 64,
                             (h // 2) * T + tt * 128:
                             (h // 2) * T + (tt + 1) * 128], pta[0:64, 0:128])
                t.step("dve", (f"pb{pbk}",), ("attT",), f_acp)

        for i in range(4):
            em_s1(0, i)
        em_chain(0)
        for qc in range(1, NQC):
            nprev = 4 * qc
            fav = [True]
            for i in range(4 * qc + 4):
                em_s1(qc, i)
                if i < nprev:
                    em_s2exp(qc - 1, i)
                if 0 <= i - 1 < nprev:
                    em_av(qc - 1, i - 1, fav)
            em_ofinal(qc - 1)
            em_chain(qc)
        fav = [True]
        for i in range(4 * NQC):
            em_s2exp(NQC - 1, i)
            if i > 0:
                em_av(NQC - 1, i - 1, fav)
        em_av(NQC - 1, 4 * NQC - 1, fav)
        em_ofinal(NQC - 1)

    if mode == "l1":
        for h in range(H):
            emit_qkv(h)
            emit_vgroup_l1 = None
            emit_scores(h)
    else:
        emit_vgroup(0)
        emit_qkv(0)
        for h in range(H):
            if h + 1 < H:
                emit_qkv(h + 1)
            emit_scores(h)
            if (h + 1) % 4 == 0 and h + 1 < H:
                emit_vgroup(h + 1)


def emit_proj(p, wo_name, bo_name, epilogue, src_dram=None, dst_dram=None,
              ud=None, src_res=None, dst_res=None):
    t = p.t
    attT = p.sbufs["attT"]
    wo_sb = p.sbufs[wo_name]
    bo_sb = p.sbufs[bo_name] if bo_name else None
    onesr = p.sbufs["onesr"]
    uacc = p.sbufs["uacc"]
    zsb = p.sbufs["w_z"]

    for tt in range(TT):
        usl = tt % 2
        if epilogue == "accum_u":
            t.step("scd", (f"ud{tt}",), (f"uacc{usl}",),
                   (lambda tt, usl: lambda e: e.dma_start(
                       uacc[:, usl * 1024:(usl + 1) * 1024],
                       ud[tt * 128:(tt + 1) * 128, :]))(tt, usl))
        for oc in range(2):
            pbk = oc

            def f_p(e, tt=tt, oc=oc, pbk=pbk):
                ins = None
                for ct in range(CT):
                    ins = e.matmul(
                        p.pb[pbk].ap()[:, 0:512],
                        attT[:, ct * T + tt * 128:ct * T + (tt + 1) * 128],
                        wo_sb[:, ct * 1024 + oc * 512:
                              ct * 1024 + (oc + 1) * 512],
                        start=(ct == 0),
                        stop=(ct == CT - 1 and bo_sb is None))
                if bo_sb is not None:
                    ins = e.matmul(p.pb[pbk].ap()[:, 0:512], onesr[0:1, 0:128],
                                   bo_sb[0:1, oc * 512:(oc + 1) * 512],
                                   start=False, stop=True)
                return ins
            rd = ("attT", wo_name, "onesr")
            if bo_name:
                rd = rd + (bo_name,)
            t.step("pe", rd, (f"pb{pbk}",), f_p)
            if epilogue == "accum_u":
                def f_a(e, oc=oc, pbk=pbk, usl=usl):
                    sl = slice(usl * 1024 + oc * 512,
                               usl * 1024 + (oc + 1) * 512)
                    return e.tensor_add(uacc[:, sl], uacc[:, sl],
                                        p.pb[pbk].ap()[:, 0:512])
                t.step("dve", (f"pb{pbk}", f"uacc{usl}"), (f"uacc{usl}",),
                       f_a)
            else:
                def f_z(e, tt=tt, oc=oc, pbk=pbk):
                    return e.tensor_copy(
                        zsb[:, (tt % 2) * 1024 + oc * 512:
                            (tt % 2) * 1024 + (oc + 1) * 512],
                        p.pb[pbk].ap()[:, 0:512])
                t.step("dve", (f"pb{pbk}",), (f"w_z{tt % 2}o{oc}",), f_z)
        if epilogue == "accum_u":
            t.step("scd", (f"uacc{usl}",), (f"ud{tt}",),
                   (lambda tt, usl: lambda e: e.dma_start(
                       ud[tt * 128:(tt + 1) * 128, :],
                       uacc[:, usl * 1024:(usl + 1) * 1024]))(tt, usl))
        else:
            emit_sandwich_tile(
                p, tt, zsb[:, (tt % 2) * 1024:(tt % 2 + 1) * 1024],
                (f"w_z{tt % 2}o0", f"w_z{tt % 2}o1"), src_dram, dst_dram,
                src_res, dst_res)


def emit_sandwich_tile(p, tt, z_ap, z_reads, src_dram, dst_dram,
                       src_res, dst_res):
    t = p.t
    xin = p.sbufs["w_xin"]
    out_sb = p.sbufs["w_out"]
    sl = tt % 2

    t.step("scd", (f"{src_res}{tt}",), (f"w_xin{sl}",),
           (lambda tt, sl: lambda e: e.dma_start(
               xin[:, sl * 1024:(sl + 1) * 1024],
               src_dram[tt * 128:(tt + 1) * 128, :]))(tt, sl))

    def out_fn(negmu, rstd, streads):
        def f(e):
            e.tensor_scalar(out_sb[:, sl * 1024:(sl + 1) * 1024], z_ap,
                            negmu, rstd, ALU.add, ALU.mult)
            return e.tensor_add(out_sb[:, sl * 1024:(sl + 1) * 1024],
                                out_sb[:, sl * 1024:(sl + 1) * 1024],
                                xin[:, sl * 1024:(sl + 1) * 1024])
        t.step("dve", tuple(z_reads) + streads + (f"w_xin{sl}",),
               (f"w_out{sl}",), f)
        t.step("scd", (f"w_out{sl}",), (f"{dst_res}{tt}",),
               (lambda tt, sl: lambda e: e.dma_start(
                   dst_dram[tt * 128:(tt + 1) * 128, :],
                   out_sb[:, sl * 1024:(sl + 1) * 1024]))(tt, sl))
    ln_steps(p, z_ap, tuple(z_reads), (tt % 4) * 4, out_fn)


def emit_u_sandwich(p, src, dst, ud, src_res, dst_res):
    zsb = p.sbufs["w_z"]
    t = p.t
    for tt in range(TT):
        zl = tt % 2
        t.step("scd", (f"ud{tt}",), (f"w_z{zl}o0", f"w_z{zl}o1"),
               (lambda tt, zl: lambda e: e.dma_start(
                   zsb[:, zl * 1024:(zl + 1) * 1024],
                   ud[tt * 128:(tt + 1) * 128, :]))(tt, zl))
        emit_sandwich_tile(p, tt, zsb[:, zl * 1024:(zl + 1) * 1024],
                           (f"w_z{zl}o0", f"w_z{zl}o1"), src, dst,
                           src_res, dst_res)


def emit_moe(p, nxT_name, ud):
    t, d = p.t, p.dram
    nxT = p.sbufs[nxT_name]
    onesr = p.sbufs["onesr"]
    cpool = p.sbufs["cpool"]
    gl = p.sbufs["moe_gl"]
    gw = p.sbufs["moe_gw"]
    gate_sb = p.sbufs["gate_sb"]
    gate_bf = p.sbufs["gate_bf"]
    fc1sb = p.sbufs["fc1sb"]
    fc2sb = p.sbufs["fc2sb"]
    fc1b_sb = p.sbufs["fc1b_sb"]
    fc2b_sb = p.sbufs["fc2b_sb"]
    fc2b_bf = p.sbufs["fc2b_bf"]
    hT = p.sbufs["moe_hT"]
    uacc = p.sbufs["uacc"]

    t.step("syd", (), ("gate_sb",),
           lambda e: e.dma_start(gate_sb, d["f0_gate"]))
    t.step("dve", ("gate_sb",), ("gate_bf",),
           lambda e: e.tensor_copy(gate_bf, gate_sb))

    for tt in range(TT):
        def f_g(e, tt=tt):
            ins = None
            for ct in range(CT):
                ins = e.matmul(
                    p.pb[2].ap()[:, tt * 8:(tt + 1) * 8],
                    nxT[:, nxc(ct, tt * 128):nxc(ct, (tt + 1) * 128)],
                    gate_bf[:, ct * 8:(ct + 1) * 8],
                    start=(ct == 0 and tt == 0),
                    stop=(ct == CT - 1 and tt == TT - 1))
            return ins
        t.step("pe", (f"{nxT_name}{tt // 4}", "gate_bf"), ("pb2",), f_g)
    t.step("dve", ("pb2",), ("moe_gl",),
           lambda e: e.tensor_copy(gl[:, 0:128], p.pb[2].ap()[:, 0:128]))

    def f_m1(e):
        return e.reduce_max(gw[:, 0:16],
                            gl[:, 0:128].rearrange("p (g k) -> p g k", k=8),
                            axis=AX.X)
    t.step("dve", ("moe_gl",), ("moe_gw",), f_m1)
    t.step("act", ("moe_gw",), ("moe_m1",),
           lambda e: e.activation(gw[:, 16:32], gw[:, 0:16], AF.Copy,
                                  bias=0.0, scale=1.0))

    def f_eq1(e):
        m1b = gw[:, 16:32].unsqueeze(2).broadcast_to((128, 16, 8))
        e.tensor_tensor(gw[:, 128:256].rearrange("p (g k) -> p g k", k=8),
                        gl[:, 0:128].rearrange("p (g k) -> p g k", k=8),
                        m1b, ALU.is_equal)
        e.scalar_tensor_tensor(gw[:, 256:384], gw[:, 128:256], -1e30,
                               gl[:, 0:128], ALU.mult, ALU.add)
        return e.reduce_max(gw[:, 32:48],
                            gw[:, 256:384].rearrange("p (g k) -> p g k", k=8),
                            axis=AX.X)
    t.step("dve", ("moe_gl", "moe_m1"), ("moe_gw",), f_eq1)
    t.step("act", ("moe_gw",), ("moe_m2",),
           lambda e: e.activation(gw[:, 48:64], gw[:, 32:48], AF.Copy,
                                  bias=0.0, scale=1.0))
    t.step("dve", ("moe_m1", "moe_m2"), ("moe_d",),
           lambda e: e.tensor_sub(gw[:, 64:80], gw[:, 48:64], gw[:, 16:32]))

    def f_pa(e):
        e.activation(gw[:, 80:96], gw[:, 64:80], AF.Exp, bias=0.0, scale=1.0)
        e.activation(gw[:, 96:112], gw[:, 80:96], AF.Ln,
                     bias=cpool[:, C_ONE:C_ONE + 1], scale=1.0)
        return e.activation(gw[:, 96:112], gw[:, 96:112], AF.Exp,
                            bias=0.0, scale=-1.0)
    t.step("act", ("moe_d", "cpool"), ("moe_gw2",), f_pa)

    def f_dg(e):
        e.tensor_mul(gw[:, 112:128], gw[:, 80:96], gw[:, 96:112])
        m2b = gw[:, 48:64].unsqueeze(2).broadcast_to((128, 16, 8))
        e.tensor_tensor(gw[:, 384:512].rearrange("p (g k) -> p g k", k=8),
                        gl[:, 0:128].rearrange("p (g k) -> p g k", k=8),
                        m2b, ALU.is_equal)
        p1b = gw[:, 96:112].unsqueeze(2).broadcast_to((128, 16, 8))
        e.tensor_tensor(gw[:, 128:256].rearrange("p (g k) -> p g k", k=8),
                        gw[:, 128:256].rearrange("p (g k) -> p g k", k=8),
                        p1b, ALU.mult)
        p2b = gw[:, 112:128].unsqueeze(2).broadcast_to((128, 16, 8))
        e.tensor_tensor(gw[:, 384:512].rearrange("p (g k) -> p g k", k=8),
                        gw[:, 384:512].rearrange("p (g k) -> p g k", k=8),
                        p2b, ALU.mult)
        return e.tensor_add(gl[:, 128:256], gw[:, 128:256], gw[:, 384:512])
    t.step("dve", ("moe_gl", "moe_gw", "moe_gw2", "moe_m1", "moe_m2"),
           ("moe_gw", "moe_g"), f_dg)

    for ei in range(NE):
        t.step("syd", (), ("fc1b_sb",),
               (lambda ei: lambda e: e.dma_start(
                   fc1b_sb, d["f0_fc1b"][ei]))(ei))
        t.step("syd", (), ("fc2b_sb",),
               (lambda ei: lambda e: e.dma_start(
                   fc2b_sb, d["f0_fc2b"][ei]))(ei))
        t.step("dve", ("fc2b_sb",), ("fc2b_bf",),
               lambda e: e.tensor_copy(fc2b_bf, fc2b_sb))
        for tc in range(4):
            for fch in range(FFN // 512):
                wsl = fch % 2
                t.step("syd", (f"fc1bf_{ei}",), (f"fc1w{wsl}",),
                       (lambda ei, fch, wsl: lambda e: e.dma_start(
                           fc1sb[:, wsl * 4096:(wsl + 1) * 4096],
                           d["f0_fc1_bf"][ei, fch]))(ei, fch, wsl))
                for ft in range(4):
                    fglob = fch * 4 + ft
                    pbk = 3 + (fglob % 2)

                    def f_f1(e, tc=tc, wsl=wsl, ft=ft, pbk=pbk):
                        ins = None
                        for ct in range(CT):
                            ins = e.matmul(
                                p.pb[pbk].ap()[:, 0:512],
                                fc1sb[:, wsl * 4096 + ct * 512 + ft * 128:
                                      wsl * 4096 + ct * 512 + (ft + 1) * 128],
                                nxT[:, nxc(ct, tc * 512):
                                    nxc(ct, (tc + 1) * 512)],
                                start=(ct == 0), stop=(ct == CT - 1))
                        return ins
                    t.step("pe", (f"fc1w{wsl}", f"{nxT_name}{tc}"),
                           (f"pb{pbk}",), f_f1)

                    def f_g1(e, fglob=fglob, pbk=pbk):
                        return e.activation(
                            hT[:, fglob * 512:(fglob + 1) * 512],
                            p.pb[pbk].ap()[:, 0:512], AF.Gelu,
                            bias=fc1b_sb[:, fglob:fglob + 1], scale=1.0)
                    t.step("act", (f"pb{pbk}", "fc1b_sb"),
                           (f"moe_hT{fglob}",), f_g1)
            for ft in range(FFN // 128):
                wsl = (ft // 2) % 2
                if ft % 2 == 0:
                    t.step("scd", (f"fc2bf_{ei}",), (f"fc2w{wsl}",),
                           (lambda ei, ft, wsl: lambda e: e.dma_start(
                               fc2sb[:, wsl * 2048:(wsl + 1) * 2048]
                               .rearrange("p (j c) -> p j c", c=1024),
                               d["f0_fc2_bf"][ei, ft:ft + 2]
                               .transpose([1, 0, 2])))(ei, ft, wsl))

                def f_f2(e, ft=ft, wsl=wsl):
                    ins = None
                    off = wsl * 2048 + (ft % 2) * 1024
                    for ts in range(4):
                        for oc in range(2):
                            ins = e.matmul(
                                p.pb[ts * 2 + oc].ap()[:, 0:512],
                                hT[:, ft * 512 + ts * 128:
                                   ft * 512 + (ts + 1) * 128],
                                fc2sb[:, off + oc * 512:off + (oc + 1) * 512],
                                start=(ft == 0), stop=False)
                    return ins
                t.step("pe", (f"fc2w{wsl}", f"moe_hT{ft}"),
                       tuple(f"pb{bk}" for bk in range(8)), f_f2)

            def f_b2(e, ei=ei):
                ins = None
                for ts in range(4):
                    for oc in range(2):
                        ins = e.matmul(
                            p.pb[ts * 2 + oc].ap()[:, 0:512],
                            onesr[0:1, 0:128],
                            fc2b_bf[0:1, oc * 512:(oc + 1) * 512],
                            start=False, stop=True)
                return ins
            t.step("pe", ("fc2b_bf", "onesr"),
                   tuple(f"pb{bk}" for bk in range(8)), f_b2)
            for ts in range(4):
                tt = tc * 4 + ts
                usl = ts % 2
                t.step("scd", (f"ud{tt}",), (f"uacc{usl}",),
                       (lambda tt, usl: lambda e: e.dma_start(
                           uacc[:, usl * 1024:(usl + 1) * 1024],
                           ud[tt * 128:(tt + 1) * 128, :]))(tt, usl))
                for oc in range(2):
                    bk = ts * 2 + oc

                    def f_u(e, bk=bk, tt=tt, oc=oc, ei=ei, usl=usl):
                        sl = slice(usl * 1024 + oc * 512,
                                   usl * 1024 + (oc + 1) * 512)
                        return e.scalar_tensor_tensor(
                            uacc[:, sl], p.pb[bk].ap()[:, 0:512],
                            gl[:, 128 + tt * 8 + ei:129 + tt * 8 + ei],
                            uacc[:, sl], ALU.mult, ALU.add)
                    t.step("dve", (f"pb{bk}", "moe_g", f"uacc{usl}"),
                           (f"uacc{usl}",), f_u)
                t.step("scd", (f"uacc{usl}",), (f"ud{tt}",),
                       (lambda tt, usl: lambda e: e.dma_start(
                           ud[tt * 128:(tt + 1) * 128, :],
                           uacc[:, usl * 1024:(usl + 1) * 1024]))(tt, usl))


def emit_qlora_ffn(p, ud):
    t, d = p.t, p.dram
    onesr = p.sbufs["onesr"]
    identb = p.sbufs["identb"]
    hT = p.sbufs["nxT"]
    ygT = p.sbufs["attT"]
    qt_sb = p.sbufs["qt_sb"]
    sc_sb = p.sbufs["sc_sb"]
    wdq = p.sbufs["wdq"]
    la_sb = p.sbufs["la_sb"]
    lb_sb = p.sbufs["lb_sb"]
    qbias = p.sbufs["qbias"]
    qbias_f = p.sbufs["qbias_f"]
    sT = p.sbufs["sT"]
    g_tab = p.sbufs["gb_g"]
    b_tab = p.sbufs["gb_b"]
    zsb = p.sbufs["w_z"]
    sq = p.sbufs["w_sq"]
    ybf = p.sbufs["w_nx"]
    uacc = p.sbufs["uacc"]
    nxbf = p.sbufs["w_nx"]

    t.step("gpd", (), ("la_sb",), lambda e: e.dma_start(
        la_sb.rearrange("p (i r) -> p i r", r=CT * LORA_R),
        d["q1_la"].transpose([1, 0, 2])))

    def cast_and_T():
        for tt in range(TT):
            sl = tt % 2
            t.step("gpd", (f"ud{tt}",), (f"w_nx{sl}",),
                   (lambda tt, sl: lambda e: e.dma_start(
                       nxbf[:, sl * 1024:(sl + 1) * 1024],
                       ud[tt * 128:(tt + 1) * 128, :]))(tt, sl))
            emit_transposes_tile(
                p, tt, nxbf[:, sl * 1024:(sl + 1) * 1024], f"w_nx{sl}",
                "nxT")

    cast_and_T()

    for blk in range(3):
        t.step("syd", (), ("gb_g",),
               (lambda blk: lambda e: e.dma_start(
                   g_tab,
                   d["q1_lng"][blk].partition_broadcast(128).squeeze(1)))(blk))
        t.step("syd", (), ("gb_b",),
               (lambda blk: lambda e: e.dma_start(
                   b_tab,
                   d["q1_lnb"][blk].partition_broadcast(128).squeeze(1)))(blk))
        for half in range(2):
            mi = 2 * blk + half
            inT, inT_name = (hT, "nxT") if half == 0 else (ygT, "attT")
            t.step("gpd", (), ("lb_sb",),
                   (lambda mi: lambda e: e.dma_start(
                       lb_sb, d["q1_lb"][mi]))(mi))
            t.step("syd", (), ("qbias_f",),
                   (lambda mi: lambda e: e.dma_start(
                       qbias_f, d["q1_b"][mi]))(mi))
            t.step("dve", ("qbias_f",), ("qbias",),
                   lambda e: e.tensor_copy(qbias, qbias_f))
            for chunk in range(4):
                def f_sl(e, mi=mi, chunk=chunk, inT=inT):
                    ins = None
                    for ct in range(CT):
                        ins = e.matmul(
                            p.pb[2].ap()[0:32, 0:512],
                            la_sb[:, mi * (CT * LORA_R) + ct * LORA_R:
                                  mi * (CT * LORA_R) + (ct + 1) * LORA_R],
                            inT[:, nxc(ct, chunk * 512):
                                nxc(ct, (chunk + 1) * 512)],
                            start=(ct == 0), stop=(ct == CT - 1))
                    return ins
                t.step("pe", ("la_sb", f"{inT_name}{chunk}"), ("pb2",),
                       f_sl)

                def f_se(e, chunk=chunk):
                    return e.tensor_copy(
                        sT[0:32, chunk * 512:(chunk + 1) * 512],
                        p.pb[2].ap()[0:32, 0:512])
                t.step("dve", ("pb2",), ("sT",), f_se)
            for oc in range(2):
                for hf in range(2):
                    t.step("syd", (), ("qt_sb",),
                           (lambda mi, oc, hf: lambda e: e.dma_start(
                               qt_sb, d["q1_qt"][mi, oc][
                                   :, hf * 2048:(hf + 1) * 2048]))(mi, oc, hf))
                    t.step("syd", (), ("sc_sb",),
                           (lambda mi, oc, hf: lambda e: e.dma_start(
                               sc_sb, d["q1_sc"][mi, oc][
                                   :, hf * 2048:(hf + 1) * 2048]))(mi, oc, hf))

                    def f_dq(e, oc=oc, hf=hf):
                        ins = None
                        for dt4 in range(4):
                            slc = slice(dt4 * 512, (dt4 + 1) * 512)
                            e.tensor_copy(sq[:, 0:512], qt_sb[:, slc])
                            e.tensor_scalar(sq[:, 512:1024], sq[:, 0:512],
                                            1.0 / 7.5, -1.0, ALU.mult,
                                            ALU.add)
                            ins = e.tensor_mul(
                                wdq[:, oc * 4096 + hf * 2048 + dt4 * 512:
                                    oc * 4096 + hf * 2048 + (dt4 + 1) * 512],
                                sq[:, 512:1024], sc_sb[:, slc])
                        return ins
                    t.step("dve", ("qt_sb", "sc_sb"),
                           (f"wdq{oc}", "w_sqscr"), f_dq)
            for tt in range(TT):
                usl = tt % 2
                if half == 1:
                    t.step("scd", (f"ud{tt}",), (f"uacc{usl}",),
                           (lambda tt, usl: lambda e: e.dma_start(
                               uacc[:, usl * 1024:(usl + 1) * 1024],
                               ud[tt * 128:(tt + 1) * 128, :]))(tt, usl))
                for oc in range(2):
                    pbk = oc

                    def f_gm(e, mi=mi, oc=oc, tt=tt, pbk=pbk, inT=inT):
                        for ct in range(CT):
                            e.matmul(
                                p.pb[pbk].ap()[:, 0:512],
                                inT[:, nxc(ct, tt * 128):
                                    nxc(ct, (tt + 1) * 128)],
                                wdq[:, oc * 4096 + ct * 512:
                                    oc * 4096 + (ct + 1) * 512],
                                start=(ct == 0), stop=False)
                        e.matmul(
                            p.pb[pbk].ap()[:, 0:512],
                            sT[0:32, tt * 128:(tt + 1) * 128],
                            lb_sb[0:32, oc * 512:(oc + 1) * 512],
                            start=False, stop=False)
                        return e.matmul(
                            p.pb[pbk].ap()[:, 0:512], onesr[0:1, 0:128],
                            qbias[0:1, oc * 512:(oc + 1) * 512],
                            start=False, stop=True)
                    t.step("pe", (f"wdq{oc}", f"{inT_name}{tt // 4}", "sT",
                                  "lb_sb", "qbias", "onesr"),
                           (f"pb{pbk}",), f_gm)
                    if half == 1:
                        def f_h(e, oc=oc, pbk=pbk, usl=usl):
                            sl = slice(usl * 1024 + oc * 512,
                                       usl * 1024 + (oc + 1) * 512)
                            return e.tensor_add(uacc[:, sl], uacc[:, sl],
                                                p.pb[pbk].ap()[:, 0:512])
                        t.step("dve", (f"pb{pbk}", f"uacc{usl}"),
                               (f"uacc{usl}",), f_h)
                    else:
                        def f_z(e, tt=tt, oc=oc, pbk=pbk):
                            return e.tensor_copy(
                                zsb[:, (tt % 2) * 1024 + oc * 512:
                                    (tt % 2) * 1024 + (oc + 1) * 512],
                                p.pb[pbk].ap()[:, 0:512])
                        t.step("dve", (f"pb{pbk}",),
                               (f"w_z{tt % 2}o{oc}",), f_z)
                if half == 1:
                    t.step("scd", (f"uacc{usl}",), (f"ud{tt}",),
                           (lambda tt, usl: lambda e: e.dma_start(
                               ud[tt * 128:(tt + 1) * 128, :],
                               uacc[:, usl * 1024:(usl + 1) * 1024]))(tt,
                                                                      usl))
                else:
                    z_ap = zsb[:, (tt % 2) * 1024:(tt % 2 + 1) * 1024]
                    zr = (f"w_z{tt % 2}o0", f"w_z{tt % 2}o1")

                    def mko(tt=tt, z_ap=z_ap, zr=zr):
                        def out_fn(negmu, rstd, streads):
                            tmp = p.sbufs["w_tmp"]
                            nsl = tt % 2

                            def f(e):
                                e.tensor_scalar(tmp[:, 0:1024], z_ap, negmu,
                                                rstd, ALU.add, ALU.mult)
                                e.tensor_mul(tmp[:, 0:1024], tmp[:, 0:1024],
                                             g_tab)
                                return e.tensor_add(tmp[:, 0:1024],
                                                    tmp[:, 0:1024], b_tab)
                            t.step("dve", zr + streads + ("gb_g", "gb_b"),
                                   ("w_tmp",), f)

                            def fg(e):
                                return e.activation(
                                    ybf[:, nsl * 1024:(nsl + 1) * 1024],
                                    tmp[:, 0:1024], AF.Gelu, bias=0.0,
                                    scale=1.0)
                            t.step("act", ("w_tmp",), (f"w_nx{nsl}",), fg)
                        return out_fn
                    ln_steps(p, z_ap, zr, (tt % 4) * 4, mko())
                    emit_transposes_tile(
                        p, tt, ybf[:, (tt % 2) * 1024:(tt % 2 + 1) * 1024],
                        f"w_nx{tt % 2}", "attT")
        if blk < 2:
            cast_and_T()


def emit_slot(p, sl, src, dst, src_res, dst_res):
    d = p.dram
    t = p.t
    ud = d["ud"]
    if sl == "l0a":
        p.fence(SCR_NAMES)
        emit_nx_phase(p, "nxT", src, src_res)
        p.tap("dbg_nxT", p.sbufs["nxT"], ("nxT",))
        p.fence(WT_NAMES + SCR_NAMES)
        emit_attn_heads(p, "a0", "rope", "nxT")
        p.tap("dbg_attT", p.sbufs["attT"], ("attT",))
        p.fence(WT_NAMES)
        t.step("gpd", (), ("wo_sb",),
               lambda e: e.dma_start(p.sbufs["wo_sb"], d["a0_wo"]))
        t.step("gpd", (), ("bo_sb",),
               lambda e: e.dma_start(p.sbufs["bo_sb"], d["a0_bo"]))
        p.fence(SCR_NAMES)
        emit_proj(p, "wo_sb", "bo_sb", "sandwich", src_dram=src,
                  dst_dram=dst, src_res=src_res, dst_res=dst_res)
    elif sl == "l0f":
        p.fence(SCR_NAMES)
        emit_u_phase(p, src, ud, src_res)
        t.step("syd", (), ("gb_g",), lambda e: e.dma_start(
            p.sbufs["gb_g"],
            d["f0_ln1w"].partition_broadcast(128).squeeze(1)))
        t.step("syd", (), ("gb_b",), lambda e: e.dma_start(
            p.sbufs["gb_b"],
            d["f0_ln1b"].partition_broadcast(128).squeeze(1)))
        emit_nx_phase(p, "nxT", ud, "ud", gamma="gb_g", beta="gb_b")
        t.step("gpd", (), ("bqk_sb",),
               lambda e: e.dma_start(p.sbufs["bqk_sb"], d["f0_bqk"]))
        t.step("gpd", (), ("bv_sb",),
               lambda e: e.dma_start(p.sbufs["bv_sb"], d["f0_bv"]))
        p.fence(WT_NAMES + SCR_NAMES)
        emit_attn_heads(p, "f0", "plain", "nxT")
        p.fence(WT_NAMES)
        t.step("gpd", (), ("wo_sb",),
               lambda e: e.dma_start(p.sbufs["wo_sb"], d["f0_wo"]))
        t.step("gpd", (), ("bo_sb",),
               lambda e: e.dma_start(p.sbufs["bo_sb"], d["f0_bo"]))
        p.fence(SCR_NAMES)
        emit_proj(p, "wo_sb", "bo_sb", "accum_u", ud=ud)
        t.step("syd", (), ("gb_g",), lambda e: e.dma_start(
            p.sbufs["gb_g"],
            d["f0_ln2w"].partition_broadcast(128).squeeze(1)))
        t.step("syd", (), ("gb_b",), lambda e: e.dma_start(
            p.sbufs["gb_b"],
            d["f0_ln2b"].partition_broadcast(128).squeeze(1)))
        emit_nx_phase(p, "nxT", ud, "ud", gamma="gb_g", beta="gb_b")
        p.fence(WT_NAMES + ATT_NAMES)
        emit_moe(p, "nxT", ud)
        p.fence(SCR_NAMES + ATT_NAMES)
        emit_u_sandwich(p, src, dst, ud, src_res, dst_res)
    elif sl == "l1a":
        p.fence(SCR_NAMES)
        emit_nx_phase(p, "nxT", src, src_res)
        t.step("syd", (), ("l1lntab",), lambda e: e.dma_start(
            p.sbufs["l1lntab"],
            d["a1_lntab"].partition_broadcast(128).squeeze(1)))
        p.fence(WT_NAMES + SCR_NAMES)
        emit_attn_heads(p, "a1", "l1", "nxT")
        p.fence(WT_NAMES)
        t.step("gpd", (), ("wo_sb",),
               lambda e: e.dma_start(p.sbufs["wo_sb"], d["a1_wu"]))
        p.fence(SCR_NAMES)
        emit_proj(p, "wo_sb", None, "sandwich", src_dram=src, dst_dram=dst,
                  src_res=src_res, dst_res=dst_res)
    elif sl == "l1f":
        p.fence(SCR_NAMES)
        emit_u_phase(p, src, ud, src_res)
        p.fence(WT_NAMES + ATT_NAMES)
        emit_qlora_ffn(p, ud)
        p.fence(SCR_NAMES)
        emit_u_sandwich(p, src, dst, ud, src_res, dst_res)


def build_program(seqs=SPC, slots=("l0a", "l0f", "l1a", "l1f"), dbg=()):
    nc = bass.Bass("TRN2", target_bir_lowering=False)
    dram = {}

    def din(name, shape, dtype=F32):
        dram[name] = nc.dram_tensor(name, list(shape), dtype,
                                    kind="ExternalInput").ap()

    def dout(name, shape, dtype=F32):
        dram[name] = nc.dram_tensor(name, list(shape), dtype,
                                    kind="ExternalOutput").ap()

    def dint(name, shape, dtype=F32):
        dram[name] = nc.dram_tensor(name, list(shape), dtype,
                                    kind="Internal").ap()

    din("x", (SPC, T, C))
    dout("y", (SPC, T, C))
    dint("xa", (SPC, T, C))
    dint("xb", (SPC, T, C))
    dint("ud", (T, C))

    din("consts", (1, C_NCONST))
    din("identf", (128, 128))
    din("ropec", (128, T))
    din("ropes", (128, T))
    din("masks", (4, 128, QCH))

    din("a0_wqk", (H, 128, C))
    din("a0_wv", (4, 128, 2048))
    din("a0_wo", (128, CT * 1024))
    din("a0_bo", (1, C))

    din("f0_wqk", (H, 128, C))
    din("f0_bqk", (1, H * 128))
    din("f0_wv", (4, 128, 2048))
    din("f0_bv", (1, 4 * 256))
    din("f0_wo", (128, CT * 1024))
    din("f0_bo", (1, C))
    din("f0_ln1w", (1, C))
    din("f0_ln1b", (1, C))
    din("f0_ln2w", (1, C))
    din("f0_ln2b", (1, C))
    din("f0_gate", (128, CT * NE))
    din("f0_fc1", (NE, FFN // 512, 128, CT * 512))
    din("f0_fc1b", (NE, 128, FFN // 128))
    din("f0_fc2", (NE, FFN // 128, 128, C))
    din("f0_fc2b", (NE, 1, C))
    dint("f0_fc1_bf", (NE, FFN // 512, 128, CT * 512), BF16)
    dint("f0_fc2_bf", (NE, FFN // 128, 128, C), BF16)

    din("a1_wqkv", (H, 128, CT * 192))
    din("a1_wu", (128, CT * 1024))
    din("a1_lntab", (1, 256))

    din("q1_qt", (NMAT, 2, 128, CT * 512), I32)
    din("q1_sc", (NMAT, 2, 128, CT * 512))
    din("q1_b", (NMAT, 1, C))
    din("q1_la", (NMAT, 128, CT * LORA_R))
    din("q1_lb", (NMAT, LORA_R, C))
    din("q1_lng", (3, 1, C))
    din("q1_lnb", (3, 1, C))

    for nm, shp, dt in (("dbg_nxT", (128, CT * T), BF16),
                        ("dbg_qb", (64, T), BF16),
                        ("dbg_kb", (64, T), BF16),
                        ("dbg_attT", (128, CT * T), BF16),
                        ("dbg_st", (128, 16 * QCH), F32),
                        ("dbg_vg", (128, TT * 260), BF16)):
        if nm in dbg:
            dram[nm] = nc.dram_tensor(nm, list(shp), dt,
                                      kind="ExternalOutput").ap()
    trk = Trk()
    with ExitStack() as ctx:
        p = P(nc, ctx, trk, dram, dbg)
        p.sb("cpool", [128, C_NCONST], F32)
        p.sb("identf", [128, 128], F32)
        p.sb("identb", [128, 128], BF16)
        p.sb("onesr", [1, 512], BF16)
        p.sb("mask_sb", [128, 4 * QCH], BF16)
        p.sb("nxT", [128, CT * T], BF16)
        att_ar = p.sb("att_ar", [128, CT * T], BF16)
        p.sbufs["attT"] = att_ar
        p.sbufs["moe_hT"] = att_ar
        p.sb("vgrp", [128, TT * 260], BF16)

        p.sb("qkc", [128, 512], F32)
        p.sb("rot", [128, 512], F32)
        p.sb("ptbT", [128, 2 * QCH], BF16)
        p.sbufs["ptb"] = p.sbufs["ptbT"]
        p.sb("att2", [128, 128], BF16)
        p.sb("bmx", [128, 32], F32)
        p.sb("bmx2", [128, 32], F32)
        p.sb("chain", [128, 12], F32)
        p.sb("chain_bf", [1, 4], BF16)
        p.sb("l1mu", [128, 64], F32)
        p.sb("l1var", [128, 32], F32)
        p.sb("l1lntab", [128, 256], F32)
        p.sb("gb_g", [128, C], F32)
        p.sb("gb_b", [128, C], F32)
        p.sb("fc1b_sb", [128, FFN // 128], F32)
        p.sb("fc2b_sb", [1, C], F32)
        p.sb("fc2b_bf", [1, C], BF16)
        p.sb("bo_sb", [1, C], BF16)
        p.sb("bqk_sb", [1, H * 128], BF16)
        p.sb("bv_sb", [1, 4 * 256], BF16)
        p.sb("qbias_f", [1, C], F32)
        p.sb("qbias", [1, C], BF16)
        p.sb("la_sb", [128, NMAT * CT * LORA_R], BF16)
        p.sb("lb_sb", [LORA_R, C], BF16)
        p.sb("w_z", [128, 2 * 1024], F32)
        qkbuf = p.sb("qkbuf", [64, 8192], BF16)
        p.sbufs["qb"] = qkbuf[0:64, 0:4096]
        p.sbufs["kb"] = qkbuf[0:64, 4096:8192]
        p.sb("w_xin", [128, 2 * 1024], F32)
        p.sb("w_st", [128, 16], F32)
        p.sb("w_sq", [128, 1024], F32)
        p.sb("w_nx", [128, 2 * 1024], BF16)
        p.sb("w_tmp", [128, 1024], F32)
        uo = p.sb("uacc", [128, 2 * 1024], F32)
        p.sbufs["w_out"] = uo
        p.sbufs["l1sub"] = p.sbufs["uacc"][:, 0:2048]
        p.sbufs["qkall"] = p.sbufs["w_xin"][:, 0:2048]
        p.sbufs["qknorm"] = p.sbufs["w_nx"][:, 0:2048]
        wt = p.sb("wt_ar", [128, 9216], F32)
        p.view("wqk_sb", wt, 0, 128, 3072, BF16)
        p.view("wv_sb", wt, 6144, 128, 4096, BF16)
        p.view("cos2", wt, 14336, 128, 2048, F32)
        p.view("sin2", wt, 22528, 128, 2048, F32)
        p.view("wo_sb", wt, 0, 128, 8192, BF16)
        p.view("fc1sb", wt, 0, 128, 8192, BF16)
        p.view("fc2sb", wt, 20480, 128, 4096, BF16)
        p.view("gate_sb", wt, 28672, 128, 64, F32)
        p.view("gate_bf", wt, 28928, 128, 64, BF16)
        p.view("moe_gl", wt, 29184, 128, 256, F32)
        p.view("moe_gw", wt, 30720, 128, 512, F32)
        p.view("wdq", wt, 0, 128, 8192, BF16)
        p.view("qt_sb", wt, 16384, 128, 2048, I32)
        p.view("sc_sb", wt, 24576, 128, 2048, F32)
        p.view("sT", wt, 32768, 32, 2048, BF16)

        emit_init(p)
        phase0_done = [False]
        for s in range(seqs):
            cur = dram["x"][s]
            cur_res = f"xinp{s}_"
            nslots = len(slots)
            for idx, sl in enumerate(slots):
                if sl == "l0f" and not phase0_done[0]:
                    emit_phase0_cast(p)
                    phase0_done[0] = True
                if idx == nslots - 1:
                    nxt = dram["y"][s]
                else:
                    nxt = dram["xa"][s] if idx % 2 == 0 else dram["xb"][s]
                nxt_res = f"xch{s}_{idx}_"
                emit_slot(p, sl, cur, nxt, cur_res, nxt_res)
                cur = nxt
                cur_res = nxt_res
        trk.emit(nc, ctx)
    return nc



def _t128(a):
    Cdim, X = a.shape
    return np.ascontiguousarray(
        a.reshape(Cdim // 128, 128, X).transpose(1, 0, 2).reshape(128, -1))


def pack_inputs(inp):
    f32 = np.float32
    g = {}
    g["consts"] = np.array([[1e-5, 1.0 / 1024, -0.5, 0.125, 1.0 / 64,
                             1.0, -1.0 / 1024, 1.0 / 64]], f32)
    g["identf"] = np.eye(128, dtype=f32)
    invf = (1.0 / 10000.0 ** (np.arange(32, dtype=np.float64) / 32))
    tpos = np.arange(T, dtype=np.float64)
    fr = np.outer(invf, tpos)
    g["ropec"] = np.ascontiguousarray(np.tile(np.cos(fr).astype(f32),
                                              (4, 1)))
    g["ropes"] = np.ascontiguousarray(np.tile(np.sin(fr).astype(f32),
                                              (4, 1)))
    kk = np.arange(128)[:, None]
    qq = np.arange(QCH)[None, :]
    g["masks"] = np.stack([(kk + 128 * j <= qq).astype(f32)
                           for j in range(4)])

    W = np.asarray(inp["L0_attn_Wqkv"], f32)
    wqk = np.empty((H, 128, C), f32)
    for h in range(H):
        q = W[:, h * 64:(h + 1) * 64]
        k = W[:, 1024 + h * 64:1024 + (h + 1) * 64]
        cols = np.concatenate([q[:, 0::2], k[:, 0::2], q[:, 1::2],
                               k[:, 1::2]], axis=1)
        wqk[h] = _t128(cols)
    g["a0_wqk"] = wqk
    wv = np.empty((4, 128, 2048), f32)
    for gi in range(4):
        wv[gi] = _t128(W[:, 2048 + gi * 256:2048 + (gi + 1) * 256])
    g["a0_wv"] = wv
    g["a0_wo"] = _t128(np.asarray(inp["L0_attn_Wo"], f32))
    g["a0_bo"] = np.asarray(inp["L0_attn_bo"], f32).reshape(1, C)

    W = np.asarray(inp["L0_ffn_Wqkv"], f32)
    bq = np.asarray(inp["L0_ffn_bqkv"], f32)
    wqk = np.empty((H, 128, C), f32)
    bqk = np.empty((1, H * 128), f32)
    for h in range(H):
        cols = np.concatenate([W[:, h * 64:(h + 1) * 64],
                               W[:, 1024 + h * 64:1024 + (h + 1) * 64]],
                              axis=1)
        wqk[h] = _t128(cols)
        bqk[0, h * 128:h * 128 + 64] = bq[h * 64:(h + 1) * 64]
        bqk[0, h * 128 + 64:(h + 1) * 128] = bq[1024 + h * 64:
                                                1024 + (h + 1) * 64]
    g["f0_wqk"] = wqk
    g["f0_bqk"] = bqk
    wv = np.empty((4, 128, 2048), f32)
    for gi in range(4):
        wv[gi] = _t128(W[:, 2048 + gi * 256:2048 + (gi + 1) * 256])
    g["f0_wv"] = wv
    g["f0_bv"] = np.ascontiguousarray(bq[2048:].reshape(1, 1024))
    g["f0_wo"] = _t128(np.asarray(inp["L0_ffn_Wo"], f32))
    g["f0_bo"] = np.asarray(inp["L0_ffn_bo"], f32).reshape(1, C)
    g["f0_ln1w"] = np.asarray(inp["L0_ffn_ln1_w"], f32).reshape(1, C)
    g["f0_ln1b"] = np.asarray(inp["L0_ffn_ln1_b"], f32).reshape(1, C)
    g["f0_ln2w"] = np.asarray(inp["L0_ffn_ln2_w"], f32).reshape(1, C)
    g["f0_ln2b"] = np.asarray(inp["L0_ffn_ln2_b"], f32).reshape(1, C)
    g["f0_gate"] = _t128(np.asarray(inp["L0_ffn_gate_W"], f32))
    fc1 = np.asarray(inp["L0_ffn_fc1_w"], f32)
    a = np.empty((NE, 8, 128, 4096), f32)
    for ei in range(NE):
        At = fc1[ei].T
        a[ei] = At.reshape(8, 128, 8, 512).transpose(2, 1, 0, 3).reshape(
            8, 128, 4096)
    g["f0_fc1"] = a
    g["f0_fc1b"] = np.ascontiguousarray(
        np.asarray(inp["L0_ffn_fc1_b"], f32).reshape(NE, 32, 128)
        .transpose(0, 2, 1))
    fc2 = np.asarray(inp["L0_ffn_fc2_w"], f32)
    a = np.empty((NE, 32, 128, C), f32)
    for ei in range(NE):
        a[ei] = fc2[ei].T.reshape(32, 128, C)
    g["f0_fc2"] = a
    g["f0_fc2b"] = np.asarray(inp["L0_ffn_fc2_b"], f32).reshape(NE, 1, C)

    Wq = np.asarray(inp["L1_attn_Wq"], f32)
    Wk = np.asarray(inp["L1_attn_Wk"], f32)
    Wv = np.asarray(inp["L1_attn_Wv"], f32)
    wqkv = np.empty((H, 128, CT * 192), f32)
    for h in range(H):
        cols = np.concatenate([Wq[h * 64:(h + 1) * 64, :].T,
                               Wk[h * 64:(h + 1) * 64, :].T,
                               Wv[h * 64:(h + 1) * 64, :].T], axis=1)
        wqkv[h] = _t128(cols)
    g["a1_wqkv"] = wqkv
    g["a1_wu"] = _t128(np.asarray(inp["L1_attn_Wu"], f32).T)
    lntab = np.zeros((1, 256), f32)
    lntab[0, 0:64] = np.asarray(inp["L1_attn_qln_w"], f32)
    lntab[0, 64:128] = np.asarray(inp["L1_attn_kln_w"], f32)
    lntab[0, 128:192] = np.asarray(inp["L1_attn_qln_b"], f32)
    lntab[0, 192:256] = np.asarray(inp["L1_attn_kln_b"], f32)
    g["a1_lntab"] = lntab

    qw = np.asarray(inp["L1_ffn_qweights"], np.int32)
    sc = np.asarray(inp["L1_ffn_scales"], f32)
    aq = np.empty((NMAT, 2, 128, 4096), np.int32)
    asc = np.empty((NMAT, 2, 128, 4096), f32)
    for i in range(NMAT):
        A = qw[i]
        aq[i] = A.reshape(2, 512, 8, 128).transpose(0, 3, 2, 1).reshape(
            2, 128, 4096)
        S = np.repeat(sc[i], 16, axis=1)
        asc[i] = S.reshape(2, 512, 8, 128).transpose(0, 3, 2, 1).reshape(
            2, 128, 4096)
    g["q1_qt"] = aq
    g["q1_sc"] = asc
    g["q1_b"] = np.asarray(inp["L1_ffn_biases"], f32).reshape(NMAT, 1, C)
    la = np.asarray(inp["L1_ffn_lora_a"], f32)
    a = np.empty((NMAT, 128, CT * LORA_R), f32)
    for i in range(NMAT):
        a[i] = _t128(la[i].T)
    g["q1_la"] = a
    g["q1_lb"] = np.ascontiguousarray(
        np.asarray(inp["L1_ffn_lora_b"], f32).transpose(0, 2, 1))
    g["q1_lng"] = np.asarray(inp["L1_ffn_ln_gamma"], f32).reshape(3, 1, C)
    g["q1_lnb"] = np.asarray(inp["L1_ffn_ln_beta"], f32).reshape(3, 1, C)
    return g


def kernel(**inp):
    x = np.asarray(inp["L0_attn_x"], np.float32)
    g = pack_inputs(inp)
    nc = build_program()
    in_maps = []
    for core in range(NCORES):
        m = dict(g)
        m["x"] = np.ascontiguousarray(x[core * SPC:(core + 1) * SPC])
        in_maps.append(m)
    res = run_bass_kernel_spmd(nc, in_maps, core_ids=list(range(NCORES)))
    out = np.concatenate([r["y"] for r in res.results], axis=0)
    return out.astype(np.float32)
